# version 02
# baseline (speedup 1.0000x reference)
import contextlib
import numpy as np
import concourse.bass as bass
import concourse.mybir as mybir
from concourse.bass_utils import run_bass_kernel_spmd

F32 = mybir.dt.float32
BF16 = mybir.dt.bfloat16
FP16 = mybir.dt.float16
I32 = mybir.dt.int32
AF = mybir.ActivationFunctionType
ALU = mybir.AluOpType
AX = mybir.AxisListType

MAGIC = 12582912.0
NCORE = 8
TOK = 2048
D = 2048
NKD = 16
H = 16
DK = DV = 128
CH = 16
TOKH = TOK + 3
HT = 2052
FFI = 8192
CLIP_LO = -128.49609375
CLIP_HI = 127.49609375
SLAB = 64 * 2 * 128 * 128


class SemC:
    def __init__(self, h):
        self.h = h
        self.v = 0

    def inc(self, instr, n=1):
        instr.then_inc(self.h, n)
        self.v += n
        return self.v


def _hadamard(n):
    h = np.array([[1.0]], dtype=np.float32)
    while h.shape[0] < n:
        h = np.block([[h, h], [h, -h]])
    return h


def _consts():
    i128 = np.eye(128, dtype=np.float32)
    h128 = _hadamard(128)
    h64 = _hadamard(64)
    hx = np.kron(h64, np.eye(2)).astype(np.float32)
    sel16 = np.zeros((16, 2048), dtype=np.float32)
    for hh in range(16):
        sel16[hh, hh * 128:(hh + 1) * 128] = 1.0
    tril_s = np.tril(np.ones((128, 128), dtype=np.float32), -1)
    triu_s = np.ascontiguousarray(tril_s.T)
    triu_i = np.triu(np.ones((128, 128), dtype=np.float32), 0)
    return {
        "c_I": i128, "c_H128": h128, "c_HX": hx,
        "c_mL": tril_s, "c_oL": (1.0 - tril_s) * 88.0,
        "c_mU": triu_s, "c_oU": (1.0 - triu_s) * (-88.0),
        "c_mUi": triu_i, "c_oUi": (1.0 - triu_i) * (-88.0),
        "c_sel16": sel16,
    }


def _host_inputs(inputs):
    x = np.ascontiguousarray(np.asarray(inputs["L0_attn_hidden_states"], dtype=np.float32))
    Wqkv = np.asarray(inputs["L0_attn_W_qkv"], dtype=np.float32)
    convw = np.asarray(inputs["L0_attn_conv_w"], dtype=np.float32)
    Wz = np.asarray(inputs["L0_attn_W_z"], dtype=np.float32)
    Wb = np.asarray(inputs["L0_attn_W_b"], dtype=np.float32)
    Wa = np.asarray(inputs["L0_attn_W_a"], dtype=np.float32)
    dtb = np.asarray(inputs["L0_attn_dt_bias"], dtype=np.float32)
    Alog = np.asarray(inputs["L0_attn_A_log"], dtype=np.float32)
    nw = np.asarray(inputs["L0_attn_norm_weight"], dtype=np.float32)
    Wout = np.asarray(inputs["L0_attn_W_out"], dtype=np.float32)
    wg = np.asarray(inputs["L0_ffn_w_gate"], dtype=np.float32)
    wu = np.asarray(inputs["L0_ffn_w_up"], dtype=np.float32)
    wd = np.asarray(inputs["L0_ffn_w_down"], dtype=np.float32)
    cc = np.asarray(inputs["L0_c"], dtype=np.float32)
    wmod = np.asarray(inputs["L0_wmod"], dtype=np.float32)

    def img(W, nm):
        nk = W.shape[0] // 128
        t = W.reshape(nk, 128, nm, 128)
        return np.ascontiguousarray(t.transpose(2, 1, 0, 3).reshape(nm, 128, nk * 128))

    wqkv_img = img(Wqkv, 48)
    wz_img = img(Wz, 16)
    Wba = np.concatenate([Wb, Wa], axis=1)
    wba_img = np.ascontiguousarray(
        Wba.reshape(16, 128, 32).transpose(1, 0, 2).reshape(128, 512))
    wout_img = np.ascontiguousarray(Wout.reshape(16, 128, 2048))
    convw_t = np.ascontiguousarray(convw.reshape(48, 128, 4))
    cT = np.ascontiguousarray(cc.T.reshape(16, 128, 4))
    wgT = np.ascontiguousarray(wg.T)
    wuT = np.ascontiguousarray(wu.T)
    wdT = np.ascontiguousarray(wd.T)

    def slab(WT, rows0, nkl, nm):
        s = WT[rows0:rows0 + nkl * 128]
        s = s.reshape(nkl, 128, nm, 128)
        return np.ascontiguousarray(s.transpose(2, 0, 1, 3))

    consts = _consts()
    maps = []
    for c in range(NCORE):
        b, s = c // 2, c % 2
        xs = np.zeros((TOKH, D), dtype=np.float32)
        t0 = s * TOK
        if t0 >= 3:
            xs[0:3] = x[b, t0 - 3:t0]
        xs[3:] = x[b, t0:t0 + TOK]
        halo = np.full((128, 1), 1.0 if s == 1 else 0.0, dtype=np.float32)
        sel = np.zeros((128, 4), dtype=np.float32)
        if s == 1:
            sel[:, (c - 1) // 2] = 1.0
        bsel = np.zeros((128, 4), dtype=np.float32)
        bsel[:, b] = 1.0
        m = {
            "x": xs, "halo_mask": halo, "sel4": sel, "bsel": bsel,
            "wqkv_img": wqkv_img, "wz_img": wz_img, "wba_img": wba_img,
            "wout_img": wout_img, "convw_t": convw_t, "cT": cT,
            "wmod_t": np.ascontiguousarray(
                wmod[:, c * 1536:(c + 1) * 1536].reshape(16, 128, 1536)),
            "dtb_col": dtb.reshape(16, 1).copy(),
            "alog_col": Alog.reshape(16, 1).copy(),
            "nw_row": nw.reshape(1, 128).copy(),
            "wg_slab": slab(wgT, c * 256, 2, 64),
            "wu_slab": slab(wuT, c * 256, 2, 64),
            "wd_slab": slab(wdT, c * 1024, 8, 16),
        }
        m.update(consts)
        maps.append(m)
    return maps


def build(phases=(0, 1, 2, 3, 4), taps=()):
    nc = bass.Bass("TRN2", target_bir_lowering=False)
    tap = set(taps)

    def dram_in(name, shape, dt=F32):
        return nc.dram_tensor(name, shape, dt, kind="ExternalInput").ap()

    def dram_scr(name, shape, dt):
        kind = "ExternalOutput" if name in tap else "Internal"
        return nc.dram_tensor(name, shape, dt, kind=kind).ap()

    x_d = dram_in("x", [TOKH, D])
    halo_d = dram_in("halo_mask", [128, 1])
    sel4_d = dram_in("sel4", [128, 4])
    bsel_d = dram_in("bsel", [128, 4])
    wqkv_d = dram_in("wqkv_img", [48, 128, 2048])
    wz_d = dram_in("wz_img", [16, 128, 2048])
    wba_d = dram_in("wba_img", [128, 512])
    wout_d = dram_in("wout_img", [16, 128, 2048])
    convw_d = dram_in("convw_t", [48, 128, 4])
    cT_d = dram_in("cT", [16, 128, 4])
    wmod_d = dram_in("wmod_t", [16, 128, 1536])
    dtb_d = dram_in("dtb_col", [16, 1])
    alog_d = dram_in("alog_col", [16, 1])
    nw_d = dram_in("nw_row", [1, 128])
    wgs_d = dram_in("wg_slab", [64, 2, 128, 128])
    wus_d = dram_in("wu_slab", [64, 2, 128, 128])
    wds_d = dram_in("wd_slab", [16, 8, 128, 128])
    cI_d = dram_in("c_I", [128, 128])
    cH_d = dram_in("c_H128", [128, 128])
    cHX_d = dram_in("c_HX", [128, 128])
    cmL_d = dram_in("c_mL", [128, 128])
    coL_d = dram_in("c_oL", [128, 128])
    cmU_d = dram_in("c_mU", [128, 128])
    coU_d = dram_in("c_oU", [128, 128])
    cmUi_d = dram_in("c_mUi", [128, 128])
    coUi_d = dram_in("c_oUi", [128, 128])
    csel_d = dram_in("c_sel16", [16, 2048])

    y_d = nc.dram_tensor("y", [TOK, D], F32, kind="ExternalOutput").ap()

    ag_w_in = dram_scr("ag_w_in", [3 * SLAB], BF16)
    ag_w_out = dram_scr("ag_w_out", [NCORE, 3 * SLAB], BF16)
    ar_s_in = dram_scr("ar_s_in", [1, 4], F32)
    ar_s_out = dram_scr("ar_s_out", [1, 4], F32)
    ag_m_in = dram_scr("ag_m_in", [4, 1536], F32)
    ag_m_out = dram_scr("ag_m_out", [NCORE, 4, 1536], F32)
    mod2_d = dram_scr("mod2", [4, 12288], F32)
    q_sp = dram_scr("q_sp", [CH, H, 128, 128], BF16)
    k_sp = dram_scr("k_sp", [CH, H, 128, 128], BF16)
    v_sp = dram_scr("v_sp", [CH, H, 128, 128], BF16)
    z_sp = dram_scr("z_sp", [CH, 128, 2048], BF16)
    gcum_d = dram_scr("gcum_sp", [16, 2048], F32)
    beta_d = dram_scr("beta_sp", [16, 2048], F32)
    op_sp = dram_scr("op_sp", [CH, 6, H, 128, 128], BF16)
    ag_s_in = dram_scr("ag_s_in", [128, H * 128], F32)
    ag_s_out = dram_scr("ag_s_out", [NCORE, 128, H * 128], F32)
    ocm_sp = dram_scr("ocm_sp", [H, 128, TOK], BF16)
    x2_sp = dram_scr("x2_sp", [16, 128, D], F32)
    inter_sp = dram_scr("inter_sp", [8, 64, 128, 256], BF16)
    xq2_sp = dram_scr("xq2_sp", [16, 64, 128, 128], BF16)
    rs1_d = dram_scr("rs1_sp", [16, 128], F32)
    s2tmp_d = dram_scr("s2tmp", [8, 2, 128], F32)
    rs2_d = dram_scr("rs2_sp", [16, 128], F32)

    es = contextlib.ExitStack()
    with es:
        sT = SemC(es.enter_context(nc.semaphore("sT")))
        sV = SemC(es.enter_context(nc.semaphore("sV")))
        sS = SemC(es.enter_context(nc.semaphore("sS")))
        sG = SemC(es.enter_context(nc.semaphore("sG")))
        sC = SemC(es.enter_context(nc.semaphore("sC")))

        class DmaPool:
            def __init__(self, name, eng, n):
                self.sems = [SemC(es.enter_context(nc.semaphore(f"{name}{i}")))
                             for i in range(n)]
                self.eng = eng
                self.i = 0
                self.used = [False] * n

            def start(self, args, _n=16, **kw):
                if not isinstance(args, tuple):
                    args = (args,)
                s = self.sems[self.i]
                if self.used[self.i]:
                    self.eng.wait_ge(s.h, s.v)
                self.used[self.i] = True
                self.i = (self.i + 1) % len(self.sems)
                s.inc(self.eng.dma_start(*args, **kw), 16)
                return (s, s.v)

            def wait(self, e, tok):
                e.wait_ge(tok[0].h, tok[1])

            def wait_all(self, e):
                for s in self.sems:
                    e.wait_ge(s.h, s.v)

        qX = DmaPool("qX", nc.sync, 8)
        qW = DmaPool("qW", nc.gpsimd, 6)
        qP = DmaPool("qP", nc.scalar, 6)

        pb = [es.enter_context(nc.psum_tensor(f"pb{i}", [128, 512], F32))
              for i in range(8)]
        bank_free = [(None, 0)] * 8
        bank_rr = [0, 4]

        def bank_wait(i):
            semc, val = bank_free[i]
            if semc is not None:
                nc.tensor.wait_ge(semc.h, val)

        def next_bank(grp=0):
            lo, hi = (0, 4) if grp == 0 else (4, 8)
            bank_rr[grp] = lo + (bank_rr[grp] + 1 - lo) % (hi - lo)
            return bank_rr[grp]

        def drain_dmas():
            qX.wait_all(nc.sync)
            qP.wait_all(nc.sync)
            qW.wait_all(nc.sync)
            nc.all_engine_barrier()

        def alloc(name, shape, dt):
            return es.enter_context(nc.sbuf_tensor(name, shape, dt))

        def VI(instr, n=1):
            sV.inc(instr, n)
            nc.vector.drain()
            return sV.v

        def SI(instr, n=1):
            sS.inc(instr, n)
            nc.scalar.drain()
            return sS.v

        def GI(instr, n=1):
            sG.inc(instr, n)
            nc.gpsimd.drain()
            return sG.v

        I_bf = alloc("I_bf", [128, 128], BF16)
        H128 = alloc("H128", [128, 128], BF16)
        HXs = alloc("HXs", [128, 128], BF16)
        mL = alloc("mL", [128, 128], F32)
        oL = alloc("oL", [128, 128], F32)
        mU = alloc("mU", [128, 128], F32)
        oU = alloc("oU", [128, 128], F32)
        mUi = alloc("mUi", [128, 128], F32)
        oUi = alloc("oUi", [128, 128], F32)
        ones_b = alloc("ones_b", [128, 32], BF16)
        ones16 = alloc("ones16", [16, 128], BF16)
        sel16b = alloc("sel16b", [16, 2048], BF16)
        halo_m = alloc("halo_s", [128, 1], F32)
        sel4 = alloc("sel4_s", [128, 4], F32)
        bsel = alloc("bsel_s", [128, 4], F32)
        dtb = alloc("dtb", [16, 1], F32)
        al_c = alloc("al_c", [16, 1], F32)
        negea = alloc("negea", [16, 1], F32)
        nwb = alloc("nwb", [128, 128], BF16)
        kc1 = alloc("kc1", [128, 1], F32)

        ca1b = alloc("ca1b", [128, 2048], BF16)
        sab = alloc("sab", [128, 2048], BF16)
        gab = alloc("gab", [128, 2048], BF16)
        cb1b = alloc("cb1b", [128, 2048], BF16)
        sbb = alloc("sbb", [128, 2048], BF16)
        gbb = alloc("gbb", [128, 2048], BF16)

        beta_r = alloc("beta_r", [16, 2048], F32)
        g_r = alloc("g_r", [16, 2048], F32)
        gc_r = alloc("gc_r", [16, 2048], F32)
        gcT_r = alloc("gcT_r", [128, 16 * CH], F32)
        btT_r = alloc("btT_r", [128, 16 * CH], F32)
        sbars = alloc("sbars", [128, 4], F32)
        sc1_r = alloc("sc1_r", [128, 16], F32)

        nc.gpsimd.memset(kc1[:], 1.0)
        nc.gpsimd.memset(ones_b[:], 1.0)
        GI(nc.gpsimd.memset(ones16[:], 1.0), 1)
        for dst, src in [(I_bf, cI_d), (H128, cH_d), (HXs, cHX_d),
                         (sel16b, csel_d),
                         (nwb, nw_d[0:1, :].broadcast_to((128, 128)))]:
            qW.start((dst[:], src), 16)
        for dst, src in [(mL, cmL_d), (oL, coL_d), (mU, cmU_d),
                         (oU, coU_d), (mUi, cmUi_d), (oUi, coUi_d),
                         (halo_m, halo_d), (sel4, sel4_d), (bsel, bsel_d),
                         (dtb, dtb_d), (al_c, alog_d)]:
            qX.start((dst[:], src), 16)
        for e in (nc.tensor, nc.vector, nc.scalar, nc.gpsimd, nc.sync):
            qW.wait_all(e)
            qX.wait_all(e)
            e.wait_ge(sG.h, sG.v)
        SI(nc.scalar.activation(negea[:], al_c[:], AF.Exp), 1)
        SI(nc.scalar.mul(negea[:], negea[:], -1.0), 1)
        nc.all_engine_barrier()

        def phase0():
          with contextlib.ExitStack() as ps:
            def pa(n, s, d):
                return ps.enter_context(nc.sbuf_tensor(n, s, d))
            csT = pa("csT", [128, 64], BF16)
            cTl = pa("cTl", [128, 64], F32)
            wmod_s = pa("wmod_s", [128, 8 * 1536], BF16)
            mod_sl = pa("mod_sl", [4, 1536], F32)
            qX.start((cTl[:], cT_d.transpose([1, 0, 2])), 16)
            qX.wait_all(nc.scalar)
            SI(nc.scalar.activation(csT[:], cTl[:], AF.Silu), 1)
            nc.tensor.wait_ge(sS.h, sS.v)
            mbks = [next_bank(), next_bank(), next_bank()]
            for half in range(2):
                qW.start((
                    wmod_s[:], wmod_d[half * 8:(half + 1) * 8]
                    .transpose([1, 0, 2])), 16)
                qW.wait_all(nc.tensor)
                for nb in range(3):
                    bk = mbks[nb]
                    if half == 0:
                        bank_wait(bk)
                    for k2 in range(8):
                        mm = nc.tensor.matmul(
                            pb[bk][0:4, :],
                            csT[:, (half * 8 + k2) * 4:(half * 8 + k2) * 4 + 4],
                            wmod_s[:, k2 * 1536 + nb * 512: k2 * 1536 + nb * 512 + 512],
                            start=(half == 0 and k2 == 0),
                            stop=(half == 1 and k2 == 7))
                    sT.inc(mm, 1)
                if half == 0:
                    sG.inc(nc.gpsimd.nop(), 1)
                    nc.gpsimd.wait_ge(sT.h, sT.v)
            for nb in range(3):
                bk = mbks[nb]
                nc.scalar.wait_ge(sT.h, sT.v)
                SI(nc.scalar.copy(
                    mod_sl[:, nb * 512:(nb + 1) * 512], pb[bk][0:4, :]), 1)
                bank_free[bk] = (sS, sS.v)
            nc.scalar.wait_ge(sS.h, sS.v)
            qP.start((ag_m_in, mod_sl[:]), 16)
            qP.wait_all(nc.gpsimd)
            sC.inc(nc.gpsimd.collective_compute(
                "AllGather", ALU.bypass, replica_groups=[list(range(NCORE))],
                ins=[ag_m_in.opt()], outs=[ag_m_out.opt()]), 1)

            asum = pa("asum", [128, 4], F32)
            red = pa("red", [128, 1], F32)
            wbuf = pa("wbuf", [128, 2 * 4096], F32)
            VI(nc.vector.memset(asum[:], 0.0), 1)
            flats = [wgs_d.rearrange("a b p f -> (a b) p f"),
                     wus_d.rearrange("a b p f -> (a b) p f"),
                     wds_d.rearrange("a b p f -> (a b) p f")]
            vred = []
            vg = 0
            for si in range(3):
                for gg in range(4):
                    buf = wbuf[:, (vg % 2) * 4096:(vg % 2) * 4096 + 4096]
                    if vg >= 2:
                        nc.sync.wait_ge(sV.h, vred[vg - 2])
                    qX.start((
                        buf, flats[si][gg * 32:(gg + 1) * 32].transpose([1, 0, 2])), 16)
                    qX.wait_all(nc.vector)
                    VI(nc.vector.tensor_reduce(
                        red[:], buf, AX.X, ALU.add, apply_absolute_value=True), 1)
                    VI(nc.vector.tensor_tensor(
                        asum[:, si:si + 1], asum[:, si:si + 1], red[:], ALU.add), 1)
                    vred.append(sV.v)
                    vg += 1
            p4h = pa("p4h", [128, 4], BF16)
            p4f = pa("p4f", [128, 4], F32)
            p4l = pa("p4l", [128, 4], BF16)
            p4r = pa("p4r", [128, 4], F32)
            p4l2 = pa("p4l2", [128, 4], BF16)
            arow = pa("arow", [1, 4], F32)
            VI(nc.vector.tensor_copy(p4h[:], asum[:]), 1)
            VI(nc.vector.tensor_copy(p4f[:], p4h[:]), 1)
            VI(nc.vector.tensor_tensor(p4r[:], asum[:], p4f[:], ALU.subtract), 1)
            VI(nc.vector.tensor_copy(p4l[:], p4r[:]), 1)
            VI(nc.vector.tensor_copy(p4f[:], p4l[:]), 1)
            VI(nc.vector.tensor_tensor(p4r[:], p4r[:], p4f[:], ALU.subtract), 1)
            VI(nc.vector.tensor_copy(p4l2[:], p4r[:]), 1)
            bk = next_bank()
            bank_wait(bk)
            nc.tensor.wait_ge(sV.h, sV.v)
            nc.tensor.matmul(pb[bk][0:32, 0:4], ones_b[:], p4h[:], start=True, stop=False)
            nc.tensor.matmul(pb[bk][0:32, 0:4], ones_b[:], p4l[:], start=False, stop=False)
            mm = nc.tensor.matmul(pb[bk][0:32, 0:4], ones_b[:], p4l2[:], start=False, stop=True)
            sT.inc(mm, 1)
            nc.scalar.wait_ge(sT.h, sT.v)
            SI(nc.scalar.copy(arow[:], pb[bk][0:1, 0:4]), 1)
            bank_free[bk] = (sS, sS.v)
            nc.scalar.wait_ge(sS.h, sS.v)
            qP.start((ar_s_in, arow[:]), 16)
            qP.wait_all(nc.gpsimd)
            sC.inc(nc.gpsimd.collective_compute(
                "AllReduce", ALU.add, replica_groups=[list(range(NCORE))],
                ins=[ar_s_in.opt()], outs=[ar_s_out.opt()]), 1)
            nc.sync.wait_ge(sC.h, sC.v)
            qX.start((
                sbars[:], ar_s_out[0:1, :].broadcast_to((128, 4))), 16)
            qX.wait_all(nc.vector)
            nv = 1.0 / float(2048 * 8192)
            VI(nc.vector.tensor_scalar_mul(sbars[:], sbars[:], nv), 1)
            thrb = pa("thrb", [128, 4], F32)
            nthrb = pa("nthrb", [128, 4], F32)
            VI(nc.vector.tensor_scalar_mul(thrb[:], sbars[:], 0.5), 1)
            VI(nc.vector.tensor_scalar_mul(nthrb[:], sbars[:], -0.5), 1)
            b1 = pa("b1", [128, 4096], F32)
            b2t = pa("b2t", [128, 2 * 4096], BF16)
            vq, pq = [], []
            vg = 0
            for si in range(3):
                for gg in range(4):
                    buf = wbuf[:, (vg % 2) * 4096:(vg % 2) * 4096 + 4096]
                    obuf = b2t[:, (vg % 2) * 4096:(vg % 2) * 4096 + 4096]
                    if vg >= 2:
                        nc.sync.wait_ge(sV.h, vq[vg - 2])
                    qX.start((
                        buf, flats[si][gg * 32:(gg + 1) * 32].transpose([1, 0, 2])), 16)
                    qX.wait_all(nc.vector)
                    if vg >= 2:
                        qP.wait(nc.vector, pq[vg - 2])
                    VI(nc.vector.tensor_scalar(
                        obuf, buf, nthrb[:, si:si + 1], None, ALU.is_lt), 1)
                    VI(nc.vector.tensor_scalar(
                        b1[:], buf, thrb[:, si:si + 1], None, ALU.is_gt), 1)
                    VI(nc.vector.tensor_tensor(obuf, b1[:], obuf, ALU.subtract), 1)
                    vq.append(sV.v)
                    nc.scalar.wait_ge(sV.h, sV.v)
                    dst = ag_w_in[si * SLAB + gg * 32 * 16384:
                                  si * SLAB + (gg + 1) * 32 * 16384]
                    pq.append(qP.start((
                        dst.rearrange("(t p f) -> t p f", t=32, p=128)
                        .transpose([1, 0, 2]), obuf)))

                    vg += 1
            qP.wait_all(nc.gpsimd)
            sC.inc(nc.gpsimd.collective_compute(
                "AllGather", ALU.bypass, replica_groups=[list(range(NCORE))],
                ins=[ag_w_in.opt()], outs=[ag_w_out.opt()]), 1)

            nc.sync.wait_ge(sC.h, sC.v)
            qX.start((
                mod2_d.rearrange("b (r j) -> b r j", r=8),
                ag_m_out.transpose([1, 0, 2])), 16)
            qX.wait_all(nc.sync)
            mtmp = pa("mtmp", [128, 2048], F32)
            macc = pa("macc", [128, 2048], F32)
            for ii, dst in enumerate((sab, ca1b, gab, sbb, cb1b, gbb)):
                for bb in range(4):
                    qX.start((
                        mtmp[:], mod2_d[bb, ii * 2048:(ii + 1) * 2048]
                        .unsqueeze(0).broadcast_to((128, 2048))), 16)
                    qX.wait_all(nc.vector)
                    if bb == 0:
                        VI(nc.vector.tensor_scalar(
                            macc[:], mtmp[:], bsel[:, 0:1], None, ALU.mult), 1)
                    else:
                        VI(nc.vector.scalar_tensor_tensor(
                            macc[:], mtmp[:], bsel[:, bb:bb + 1], macc[:],
                            ALU.mult, ALU.add), 1)
                    nc.sync.wait_ge(sV.h, sV.v)
                if ii in (1, 4):
                    VI(nc.vector.tensor_scalar(
                        macc[:], macc[:], 1.0, None, ALU.add), 1)
                if ii in (2, 5):
                    VI(nc.vector.tensor_scalar(
                        mtmp[:], macc[:], 0.0, 2.0, ALU.is_ge, ALU.mult), 1)
                    VI(nc.vector.tensor_scalar(
                        mtmp[:], mtmp[:], 1.0, None, ALU.subtract), 1)
                    VI(nc.vector.tensor_tensor(macc[:], macc[:], mtmp[:],
                                               ALU.mult), 1)
                    VI(nc.vector.tensor_scalar(
                        macc[:], macc[:], 0.25, None, ALU.max), 1)
                    VI(nc.vector.tensor_tensor(macc[:], macc[:], mtmp[:],
                                               ALU.mult), 1)
                VI(nc.vector.tensor_copy(dst[:], macc[:]), 1)
                nc.sync.wait_ge(sV.h, sV.v)
            drain_dmas()

        def phase1():
          with contextlib.ExitStack() as ps:
            def pa(n, s, d):
                return ps.enter_context(nc.sbuf_tensor(n, s, d))
            h1T = pa("h1T", [128, NKD * HT], BF16)
            kceps5 = pa("kceps5", [128, 1], F32)
            kceps6 = pa("kceps6", [128, 1], F32)
            nc.gpsimd.memset(kceps5[:], 1e-5)
            GI(nc.gpsimd.memset(kceps6[:], 1e-6), 1)
            nc.vector.wait_ge(sG.h, sG.v)
            nc.scalar.wait_ge(sG.h, sG.v)

            h1T3 = h1T[:].rearrange("p (k c) -> p k c", k=NKD)
            tr_done = {}
            ln_es = contextlib.ExitStack()
            xt = ln_es.enter_context(nc.sbuf_tensor("xt", [128, 2 * 2048], F32))
            sqt = ln_es.enter_context(nc.sbuf_tensor("sqt", [128, 2048], F32))
            stat = ln_es.enter_context(nc.sbuf_tensor("stat", [128, 8], F32))
            h1a = ln_es.enter_context(nc.sbuf_tensor("h1a", [128, 2048], F32))
            h1b = ln_es.enter_context(nc.sbuf_tensor("h1b", [128, 2 * 2048], BF16))
            xh = ln_es.enter_context(nc.sbuf_tensor("xh", [16, 2048], F32))

            def ln_tile(x_ap, np_, col0, slot, idx):
                hb = h1b[:, slot * 2048:(slot + 1) * 2048]
                VI(nc.vector.tensor_reduce(
                    stat[0:np_, 0:1], x_ap, AX.X, ALU.add), 1)
                SI(nc.scalar.activation(
                    sqt[0:np_, :], x_ap, AF.Square,
                    accum_out=stat[0:np_, 2:3]), 1)
                nc.vector.wait_ge(sS.h, sS.v)
                VI(nc.vector.tensor_scalar_mul(
                    stat[0:np_, 1:2], stat[0:np_, 0:1], 1.0 / 2048.0), 1)
                VI(nc.vector.tensor_scalar_mul(
                    stat[0:np_, 3:4], stat[0:np_, 2:3], 1.0 / 2048.0), 1)
                VI(nc.vector.tensor_tensor(
                    stat[0:np_, 4:5], stat[0:np_, 1:2], stat[0:np_, 1:2],
                    ALU.mult), 1)
                VI(nc.vector.tensor_tensor(
                    stat[0:np_, 3:4], stat[0:np_, 3:4], stat[0:np_, 4:5],
                    ALU.subtract), 1)
                nc.scalar.wait_ge(sV.h, sV.v)
                SI(nc.scalar.activation(
                    stat[0:np_, 5:6], stat[0:np_, 3:4], AF.Sqrt,
                    bias=kceps5[0:np_, :]), 1)
                nc.vector.wait_ge(sS.h, sS.v)
                VI(nc.vector.reciprocal(
                    stat[0:np_, 6:7], stat[0:np_, 5:6]), 1)
                VI(nc.vector.tensor_scalar(
                    h1a[0:np_, :], x_ap, stat[0:np_, 1:2], stat[0:np_, 6:7],
                    ALU.subtract, ALU.mult), 1)
                VI(nc.vector.tensor_tensor(
                    h1a[0:np_, :], h1a[0:np_, :], ca1b[0:np_, :], ALU.mult), 1)
                if idx - 2 in tr_done:
                    nc.vector.wait_ge(sT.h, tr_done[idx - 2])
                VI(nc.vector.tensor_tensor(
                    hb[0:np_, :], h1a[0:np_, :], sab[0:np_, :], ALU.add), 1)
                nc.tensor.wait_ge(sV.h, sV.v)
                kpart = max(np_, 16)
                for g4 in range(4):
                    bk = next_bank(0)
                    bank_wait(bk)
                    for j in range(4):
                        kk = g4 * 4 + j
                        mm = nc.tensor.matmul(
                            pb[bk][:, j * 128:j * 128 + np_],
                            hb[0:kpart, kk * 128:(kk + 1) * 128],
                            I_bf[0:kpart, 0:np_],
                            start=True, stop=True)
                    sT.inc(mm, 1)
                    nc.scalar.wait_ge(sT.h, sT.v)
                    SI(nc.scalar.copy(
                        h1T3[:, g4 * 4:(g4 + 1) * 4, col0:col0 + np_],
                        pb[bk][:, :].rearrange("p (j c) -> p j c", j=4)
                        [:, :, 0:np_]), 1)
                    bank_free[bk] = (sS, sS.v)
                tr_done[idx] = sT.v

            qX.start((xh[0:3, 0:2048], x_d[0:3, :]), 16)
            qX.wait_all(nc.vector)
            qX.wait_all(nc.scalar)
            VI(nc.vector.memset(h1b[0:16, 0:2048], 0.0), 1)
            ln_tile(xh[0:3, 0:2048], 3, 0, 0, 0)
            ln_v = {}
            for tt in range(16):
                xbuf = xt[:, (tt % 2) * 2048:(tt % 2) * 2048 + 2048]
                if tt >= 2:
                    nc.sync.wait_ge(sV.h, ln_v[tt - 2])
                qX.start((
                    xbuf, x_d[3 + tt * 128: 3 + (tt + 1) * 128, :]), 16)
                qX.wait_all(nc.vector)
                qX.wait_all(nc.scalar)
                ln_tile(xbuf, 128, 3 + tt * 128, (tt + 1) % 2, tt + 1)
                ln_v[tt] = sV.v
            drain_dmas()
            ln_es.close()

            qk_es = contextlib.ExitStack()
            def qa(n, s, d):
                return qk_es.enter_context(nc.sbuf_tensor(n, s, d))
            wimg = qa("wimg", [128, 2 * 2048], BF16)
            cwt = qa("cwt", [128, 2 * 4], F32)
            mp = qa("mp", [128, 2 * 2051], F32)
            acc = qa("acc", [128, 2 * 2048], BF16)
            silt = qa("silt", [128, 2 * 2048], BF16)
            sqb = qa("sqb", [128, 2048], BF16)
            rrow = qa("rrow", [16, 2048], F32)
            rrb = qa("rrb", [16, 2048], BF16)
            rb = qa("rb", [128, 2048], F32)
            stg = qa("stg", [128, 512], BF16)
            VI(nc.vector.memset(rrb[:], 0.0), 1)

            NBQ = [(0, 512), (512, 512), (1024, 512), (1536, 512), (2048, 3)]
            NBZ = [(3, 512), (515, 512), (1027, 512), (1539, 512)]
            wfree = {}
            mpfree = {}
            silu_s = {}
            spill_x = {}

            def stream_w(m, src_ap, conv_m):
                if m - 2 in wfree:
                    nc.gpsimd.wait_ge(sT.h, wfree[m - 2])
                if conv_m is not None and m - 2 in mpfree:
                    nc.gpsimd.wait_ge(sV.h, mpfree[m - 2])
                qW.start((
                    wimg[:, (m % 2) * 2048:(m % 2) * 2048 + 2048], src_ap), 16)
                if conv_m is not None:
                    qW.start((
                        cwt[:, (m % 2) * 4:(m % 2) * 4 + 4], convw_d[conv_m]), 16)
                qW.wait_all(nc.tensor)

            def mm_tile(m, nblocks, evac):
                for bi, (c0, cn) in enumerate(nblocks):
                    bk = next_bank(0)
                    bank_wait(bk)
                    for kk in range(NKD):
                        mm = nc.tensor.matmul(
                            pb[bk][:, 0:cn],
                            wimg[:, (m % 2) * 2048 + kk * 128:
                                 (m % 2) * 2048 + (kk + 1) * 128],
                            h1T3[:, kk, c0:c0 + cn],
                            start=(kk == 0), stop=(kk == NKD - 1))
                    sT.inc(mm, 1)
                    evac(bi, c0, cn, bk)
                wfree[m] = sT.v

            for m in range(48):
                stream_w(m, wqkv_d[m], m)
                if m - 2 in mpfree:
                    nc.tensor.wait_ge(sV.h, mpfree[m - 2])
                ms = mp[:, (m % 2) * 2051:(m % 2) * 2051 + 2051]

                def evac_q(bi, c0, cn, bk, ms=ms):
                    nc.scalar.wait_ge(sT.h, sT.v)
                    SI(nc.scalar.copy(ms[:, c0:c0 + cn], pb[bk][:, 0:cn]), 1)
                    bank_free[bk] = (sS, sS.v)
                mm_tile(m, NBQ, evac_q)
                nc.vector.wait_ge(sS.h, sS.v)
                VI(nc.vector.tensor_scalar(
                    ms[:, 0:3], ms[:, 0:3], halo_m[:, 0:1], None, ALU.mult), 1)
                a0 = acc[:, (m % 2) * 2048:(m % 2) * 2048 + 2048]
                a1 = acc[:, ((m + 1) % 2) * 2048:((m + 1) % 2) * 2048 + 2048]
                cw = cwt[:, (m % 2) * 4:(m % 2) * 4 + 4]
                if m - 1 in silu_s:
                    nc.vector.wait_ge(sS.h, silu_s[m - 1])
                VI(nc.vector.tensor_scalar(
                    a0, ms[:, 0:2048], cw[:, 0:1], None, ALU.mult), 1)
                VI(nc.vector.scalar_tensor_tensor(
                    a1, ms[:, 1:2049], cw[:, 1:2], a0, ALU.mult, ALU.add), 1)
                VI(nc.vector.scalar_tensor_tensor(
                    a0, ms[:, 2:2050], cw[:, 2:3], a1, ALU.mult, ALU.add), 1)
                VI(nc.vector.scalar_tensor_tensor(
                    a1, ms[:, 3:2051], cw[:, 3:4], a0, ALU.mult, ALU.add), 1)
                mpfree[m] = sV.v
                sl = silt[:, (m % 2) * 2048:(m % 2) * 2048 + 2048]
                nc.scalar.wait_ge(sV.h, sV.v)
                if m - 2 in spill_x:
                    qX.wait(nc.scalar, spill_x[m - 2])
                silu_s[m] = SI(nc.scalar.activation(sl, a1, AF.Silu), 1)
                if m < 32:
                    SI(nc.scalar.activation(sqb[:], sl, AF.Square), 1)
                    nc.tensor.wait_ge(sS.h, sS.v)
                    for nb in range(4):
                        bk = next_bank(1)
                        bank_wait(bk)
                        mm = nc.tensor.matmul(
                            pb[bk][0:32, 0:512], ones_b[:],
                            sqb[:, nb * 512:(nb + 1) * 512],
                            start=True, stop=True)
                        sT.inc(mm, 1)
                        nc.vector.wait_ge(sT.h, sT.v)
                        VI(nc.vector.tensor_copy(
                            rrow[0:1, nb * 512:(nb + 1) * 512],
                            pb[bk][0:1, 0:512]), 1)
                        bank_free[bk] = (sV, sV.v)
                    nc.scalar.wait_ge(sV.h, sV.v)
                    SI(nc.scalar.activation(
                        rrow[0:1, :], rrow[0:1, :], AF.Sqrt,
                        bias=kceps6[0:1, :]), 1)
                    nc.vector.wait_ge(sS.h, sS.v)
                    VI(nc.vector.reciprocal(rrow[0:1, :], rrow[0:1, :]), 1)
                    if m < 16:
                        VI(nc.vector.tensor_scalar_mul(
                            rrow[0:1, :], rrow[0:1, :], float(128 ** -0.5)), 1)
                    VI(nc.vector.tensor_copy(rrb[0:1, :], rrow[0:1, :]), 1)
                    nc.tensor.wait_ge(sV.h, sV.v)
                    for nb in range(4):
                        bk = next_bank(1)
                        bank_wait(bk)
                        mm = nc.tensor.matmul(
                            pb[bk][:, 0:512], ones16[:],
                            rrb[:, nb * 512:(nb + 1) * 512], start=True, stop=True)
                        sT.inc(mm, 1)
                        nc.scalar.wait_ge(sT.h, sT.v)
                        SI(nc.scalar.copy(
                            rb[:, nb * 512:(nb + 1) * 512], pb[bk][:, 0:512]), 1)
                        bank_free[bk] = (sS, sS.v)
                    nc.vector.wait_ge(sS.h, sS.v)
                    VI(nc.vector.tensor_tensor(sl, sl, rb[:], ALU.mult), 1)
                    last = (sV, sV.v)
                else:
                    last = (sS, silu_s[m])
                dst = (q_sp if m < 16 else k_sp if m < 32 else v_sp)
                hh = m % 16
                nc.sync.wait_ge(last[0].h, last[1])
                spill_x[m] = qX.start((
                    dst[:, hh].transpose([1, 0, 2]), sl))
            drain_dmas()

            zspill = {}
            for m in range(16):
                stream_w(m, wz_d[m], None)

                def evac_z(bi, c0, cn, bk):
                    nc.scalar.wait_ge(sT.h, sT.v)
                    SI(nc.scalar.activation(
                        silt[:, bi * 512:(bi + 1) * 512], pb[bk][:, 0:cn],
                        AF.Silu), 1)
                    bank_free[bk] = (sS, sS.v)
                mm_tile(m, NBZ, evac_z)
                nc.tensor.wait_ge(sS.h, sS.v)
                for c4 in range(4):
                    bk = next_bank(1)
                    bank_wait(bk)
                    for j in range(4):
                        cci = c4 * 4 + j
                        mm = nc.tensor.matmul(
                            pb[bk][:, j * 128:(j + 1) * 128],
                            silt[:, cci * 128:(cci + 1) * 128],
                            I_bf[:], start=True, stop=True)
                    sT.inc(mm, 1)
                    nc.vector.wait_ge(sT.h, sT.v)
                    if (m, c4) != (0, 0):
                        qX.wait_all(nc.vector)
                    VI(nc.vector.tensor_copy(stg[:], pb[bk][:, :]), 1)
                    bank_free[bk] = (sV, sV.v)
                    nc.sync.wait_ge(sV.h, sV.v)
                    qX.start((
                        z_sp[c4 * 4:(c4 + 1) * 4, :, m * 128:(m + 1) * 128]
                        .transpose([1, 0, 2]), stg[:]), 16)
            drain_dmas()
            qk_es.close()

            wbal = pa("wbal", [128, 512], BF16)
            bag = pa("bag", [16, 2048], F32)
            aag = pa("aag", [16, 2048], F32)
            qW.start((wbal[:], wba_d), 16)
            qW.wait_all(nc.tensor)
            for part, out in ((0, bag), (16, aag)):
                for bi, (c0, cn) in enumerate(NBZ):
                    bk = next_bank(0)
                    bank_wait(bk)
                    for kk in range(NKD):
                        mm = nc.tensor.matmul(
                            pb[bk][0:16, 0:cn],
                            wbal[:, kk * 32 + part:kk * 32 + part + 16],
                            h1T3[:, kk, c0:c0 + cn],
                            start=(kk == 0), stop=(kk == NKD - 1))
                    sT.inc(mm, 1)
                    nc.scalar.wait_ge(sT.h, sT.v)
                    SI(nc.scalar.copy(out[:, bi * 512:(bi + 1) * 512],
                                          pb[bk][0:16, 0:cn]), 1)
                    bank_free[bk] = (sS, sS.v)
            SI(nc.scalar.activation(beta_r[:], bag[:], AF.Sigmoid), 1)
            SI(nc.scalar.activation(g_r[:], aag[:], AF.Exp, bias=dtb[:]), 1)
            SI(nc.scalar.activation(g_r[:], g_r[:], AF.Ln,
                                        bias=kc1[0:16, :]), 1)
            nc.vector.wait_ge(sS.h, sS.v)
            VI(nc.vector.tensor_scalar(
                g_r[:], g_r[:], negea[:], None, ALU.mult), 1)
            ga_, gb_ = g_r, gc_r
            for st_ in range(7):
                hh = 1 << st_
                va = ga_[:].rearrange("p (c f) -> p c f", c=CH)
                vb = gb_[:].rearrange("p (c f) -> p c f", c=CH)
                VI(nc.vector.tensor_copy(vb[:, :, 0:hh], va[:, :, 0:hh]), 1)
                VI(nc.vector.tensor_tensor(
                    vb[:, :, hh:128], va[:, :, hh:128], va[:, :, 0:128 - hh],
                    ALU.add), 1)
                ga_, gb_ = gb_, ga_
            nc.sync.wait_ge(sV.h, sV.v)
            qX.start((gcum_d, gc_r[:]), 16)
            qX.start((beta_d, beta_r[:]), 16)
            p_hi = pa("p_hi", [16, 2048], BF16)
            p_f = pa("p_f", [16, 2048], F32)
            p_lo = pa("p_lo", [16, 2048], BF16)
            p_r = pa("p_r", [16, 2048], F32)
            p_l2 = pa("p_l2", [16, 2048], BF16)
            bt_b = pa("bt_b", [16, 2048], BF16)
            VI(nc.vector.tensor_copy(p_hi[:], gc_r[:]), 1)
            VI(nc.vector.tensor_copy(p_f[:], p_hi[:]), 1)
            VI(nc.vector.tensor_tensor(p_r[:], gc_r[:], p_f[:], ALU.subtract), 1)
            VI(nc.vector.tensor_copy(p_lo[:], p_r[:]), 1)
            VI(nc.vector.tensor_copy(p_f[:], p_lo[:]), 1)
            VI(nc.vector.tensor_tensor(p_r[:], p_r[:], p_f[:], ALU.subtract), 1)
            VI(nc.vector.tensor_copy(p_l2[:], p_r[:]), 1)
            VI(nc.vector.tensor_copy(bt_b[:], beta_r[:]), 1)
            nc.tensor.wait_ge(sV.h, sV.v)
            for c in range(CH):
                bk = next_bank(0)
                bank_wait(bk)
                cs = slice(c * 128, (c + 1) * 128)
                nc.tensor.matmul(pb[bk][:, 0:16], p_hi[:, cs], I_bf[0:16, 0:16],
                                 start=True, stop=False)
                nc.tensor.matmul(pb[bk][:, 0:16], p_lo[:, cs], I_bf[0:16, 0:16],
                                 start=False, stop=False)
                nc.tensor.matmul(pb[bk][:, 0:16], p_l2[:, cs], I_bf[0:16, 0:16],
                                 start=False, stop=True)
                mm = nc.tensor.matmul(pb[bk][:, 16:32], bt_b[:, cs],
                                      I_bf[0:16, 0:16], start=True, stop=True)
                sT.inc(mm, 1)
                nc.scalar.wait_ge(sT.h, sT.v)
                SI(nc.scalar.copy(gcT_r[:, c * 16:(c + 1) * 16],
                                      pb[bk][:, 0:16]), 1)
                SI(nc.scalar.copy(btT_r[:, c * 16:(c + 1) * 16],
                                      pb[bk][:, 16:32]), 1)
                bank_free[bk] = (sS, sS.v)
            drain_dmas()


        def phase2():
          with contextlib.ExitStack() as ps:
            def pa(n, s, d):
                return ps.enter_context(nc.sbuf_tensor(n, s, d))
            qc = pa("qc", [128, 2048], BF16)
            kc = pa("kc", [128, 2048], BF16)
            vc = pa("vc", [128, 2048], BF16)
            ktil = pa("ktil", [128, 2048], BF16)
            qbar = pa("qbar", [128, 2048], BF16)
            aprow = pa("aprow", [128, 2048], BF16)
            e1 = pa("e1", [128, 2048], BF16)
            e2 = pa("e2", [128, 2048], BF16)
            e3 = pa("e3", [128, 2048], BF16)
            pre_all = pa("pre_all", [128, 2048], BF16)
            Amat = pa("Amat", [128, 2048], BF16)
            Atmat = pa("Atmat", [128, 2048], BF16)
            Ptmat = pa("Ptmat", [128, 2048], BF16)
            vT = pa("vT", [128, 2048], BF16)
            kT = pa("kT", [128, 2048], BF16)
            khat = pa("khat", [128, 2048], BF16)
            w2tn = pa("w2tn", [128, 2048], BF16)
            U_all = pa("U_all", [128, 2048], BF16)
            S_f = pa("S_f", [128, 2048], F32)
            S_b = pa("S_b", [128, 2048], BF16)
            Yb = pa("Yb", [128, 4096], BF16)
            rowb = pa("rowb", [128, 2048], F32)
            browb = pa("browb", [128, 2048], F32)
            art = pa("art", [128, 2048], F32)
            art2 = pa("art2", [128, 2048], F32)
            sq16 = pa("sq16", [128, 16], F32)
            akm = pa("akm", [128, 16], F32)
            Gbe = pa("Gbe", [128, 16], F32)
            apT = pa("apT", [128, 16], F32)
            bapT = pa("bapT", [128, 16], F32)
            obuf = pa("obuf", [128, 2048], BF16)
            onb = pa("onb", [128, 2048], BF16)
            zc = pa("zc", [128, 2048], BF16)
            stg2 = pa("stg2", [128, 512], BF16)
            sqs = pa("sqs", [128, 512], BF16)
            ep6 = pa("ep6", [128, 1], F32)
            GI(nc.gpsimd.memset(ep6[:], 1e-6), 1)
            VI(nc.vector.memset(S_f[:], 0.0), 1)
            VI(nc.vector.memset(S_b[:], 0.0), 1)
            nc.gpsimd.wait_ge(sG.h, sG.v)

            Y3 = Yb[:].rearrange("p (h u) -> p h u", h=H)
            qc3 = qc[:].rearrange("p (h t) -> p h t", h=H)
            kc3 = kc[:].rearrange("p (h t) -> p h t", h=H)
            art3 = art[:].rearrange("p (h t) -> p h t", h=H)
            rowb3 = rowb[:].rearrange("p (h t) -> p h t", h=H)

            def bc16(x):
                return x.unsqueeze(2).broadcast_to((128, 16, 128))

            def m128(x):
                return x.unsqueeze(1).broadcast_to((128, 16, 128))

            def soft_sync():
                for e in (nc.tensor, nc.vector, nc.scalar, nc.sync, nc.gpsimd):
                    e.wait_ge(sT.h, sT.v)
                    e.wait_ge(sV.h, sV.v)
                    e.wait_ge(sS.h, sS.v)
                qX.wait_all(nc.sync)
                qX.wait_all(nc.tensor)
                qX.wait_all(nc.vector)
                qX.wait_all(nc.scalar)
                qP.wait_all(nc.sync)
                qP.wait_all(nc.tensor)

            def chunk_pre(c):
                qX.start((qc[:], q_sp[c].transpose([1, 0, 2])))
                qX.start((kc[:], k_sp[c].transpose([1, 0, 2])))
                qX.start((vc[:], v_sp[c].transpose([1, 0, 2])))
                qX.start((rowb[:],
                          gcum_d[0:16, c * 128:(c + 1) * 128]
                          .unsqueeze(0).broadcast_to((128, 16, 128))))
                qX.start((browb[:],
                          beta_d[0:16, c * 128:(c + 1) * 128]
                          .unsqueeze(0).broadcast_to((128, 16, 128))))
                qX.wait_all(nc.vector)
                qX.wait_all(nc.scalar)
                gcs = gcT_r[:, c * 16:(c + 1) * 16]
                bts = btT_r[:, c * 16:(c + 1) * 16]
                VI(nc.vector.tensor_tensor(art3, rowb3, bc16(gcs), ALU.subtract), 1)
                SI(nc.scalar.activation(aprow[:], rowb[:], AF.Exp), 1)
                nc.scalar.wait_ge(sV.h, sV.v)
                SI(nc.scalar.activation(
                    akm[:].unsqueeze(2), art3[:, :, 127:128], AF.Exp), 1)
                SI(nc.scalar.activation(
                    Gbe[:].unsqueeze(2), rowb3[:, :, 127:128], AF.Exp), 1)
                SI(nc.scalar.activation(apT[:], gcs, AF.Exp), 1)
                a2v = art2[:].rearrange("p (h t) -> p h t", h=H)
                for msk, off, scl, dst in ((mL, oL, -1.0, e1),
                                           (mU, oU, 1.0, e2),
                                           (mUi, oUi, 1.0, e3)):
                    nc.vector.wait_ge(sS.h, sS.v)
                    VI(nc.vector.tensor_tensor(a2v, art3, m128(msk[:]), ALU.mult), 1)
                    VI(nc.vector.tensor_tensor(a2v, a2v, m128(off[:]), ALU.add), 1)
                    nc.scalar.wait_ge(sV.h, sV.v)
                    SI(nc.scalar.activation(dst[:], art2[:], AF.Exp, scale=scl), 1)
                nc.vector.wait_ge(sS.h, sS.v)
                VI(nc.vector.tensor_tensor(ktil[:], kc[:], browb[:], ALU.mult), 1)
                VI(nc.vector.tensor_tensor(qbar[:], qc[:], aprow[:], ALU.mult), 1)
                VI(nc.vector.tensor_tensor(bapT[:], bts, apT[:], ALU.mult), 1)
                nc.tensor.wait_ge(sV.h, sV.v)
                for lhs, rhs, emat, dst in ((ktil, kc, e1, Amat),
                                            (kc, ktil, e2, Atmat),
                                            (kc, qc, e3, Ptmat)):
                    nc.scalar.wait_ge(sV.h, sV.v)
                    for hg in range(4):
                        bk = next_bank(0)
                        bank_wait(bk)
                        for j in range(4):
                            hh = hg * 4 + j
                            mm = nc.tensor.matmul(
                                pb[bk][:, j * 128:(j + 1) * 128],
                                lhs[:, hh * 128:(hh + 1) * 128],
                                rhs[:, hh * 128:(hh + 1) * 128],
                                start=True, stop=True)
                        sT.inc(mm, 1)
                        nc.scalar.wait_ge(sT.h, sT.v)
                        SI(nc.scalar.copy(
                            pre_all[:, hg * 512:(hg + 1) * 512], pb[bk][:, :]), 1)
                        bank_free[bk] = (sS, sS.v)
                    nc.vector.wait_ge(sS.h, sS.v)
                    VI(nc.vector.tensor_tensor(dst[:], pre_all[:], emat[:],
                                               ALU.mult), 1)
                for srcb, dstb in ((vc, vT), (kc, kT)):
                    nc.tensor.wait_ge(sV.h, sV.v)
                    for hg in range(4):
                        bk = next_bank(0)
                        bank_wait(bk)
                        for j in range(4):
                            hh = hg * 4 + j
                            mm = nc.tensor.matmul(
                                pb[bk][:, j * 128:(j + 1) * 128],
                                srcb[:, hh * 128:(hh + 1) * 128],
                                I_bf[:], start=True, stop=True)
                        sT.inc(mm, 1)
                        nc.scalar.wait_ge(sT.h, sT.v)
                        SI(nc.scalar.copy(
                            dstb[:, hg * 512:(hg + 1) * 512], pb[bk][:, :]), 1)
                        bank_free[bk] = (sS, sS.v)
                vT3 = vT[:].rearrange("p (h t) -> p h t", h=H)
                kT3 = kT[:].rearrange("p (h t) -> p h t", h=H)
                nc.vector.wait_ge(sS.h, sS.v)
                VI(nc.vector.tensor_tensor(Y3[:, :, 0:128], vT3, bc16(bts),
                                           ALU.mult), 1)
                VI(nc.vector.tensor_tensor(Y3[:, :, 128:256], kT3, bc16(bapT[:]),
                                           ALU.mult), 1)
                VI(nc.vector.tensor_tensor(
                    khat[:].rearrange("p (h t) -> p h t", h=H), kT3,
                    bc16(akm[:]), ALU.mult), 1)
                nc.tensor.wait_ge(sV.h, sV.v)
                yev = [None] * 8
                aev = [0] * 8
                for pr in range(8):
                    bk = next_bank(pr % 2)
                    bank_wait(bk)
                    for u in range(2):
                        hh = pr * 2 + u
                        hs = slice(hh * 128, (hh + 1) * 128)
                        mm = nc.tensor.matmul(
                            pb[bk][:, u * 256:(u + 1) * 256], Atmat[:, hs],
                            Yb[:, hh * 256:(hh + 1) * 256],
                            start=True, stop=True)
                    sT.inc(mm, 1)
                    nc.vector.wait_ge(sT.h, sT.v)
                    VI(nc.vector.tensor_tensor(
                        Yb[:, pr * 512:(pr + 1) * 512],
                        Yb[:, pr * 512:(pr + 1) * 512],
                        pb[bk][:, 0:512], ALU.subtract), 1)
                    bank_free[bk] = (sV, sV.v)
                    yev[pr] = (sV.h, sV.v)
                for j in range(1, 7):
                    for pr in range(8):
                        bk = next_bank(pr % 2)
                        bank_wait(bk)
                        if j > 1:
                            nc.tensor.wait_ge(sS.h, aev[pr])
                        for u in range(2):
                            hh = pr * 2 + u
                            hs = slice(hh * 128, (hh + 1) * 128)
                            if j < 6:
                                nc.tensor.matmul(
                                    pb[bk][:, u * 256:u * 256 + 128],
                                    Atmat[:, hs], Amat[:, hs],
                                    start=True, stop=True)
                            mm = nc.tensor.matmul(
                                pb[bk][:, u * 256 + 128:u * 256 + 256],
                                Amat[:, hs], Atmat[:, hs],
                                start=True, stop=True)
                        sT.inc(mm, 1)
                        nc.scalar.wait_ge(sT.h, sT.v)
                        for u in range(2):
                            hh = pr * 2 + u
                            hs = slice(hh * 128, (hh + 1) * 128)
                            if j < 6:
                                SI(nc.scalar.copy(
                                    Amat[:, hs],
                                    pb[bk][:, u * 256:u * 256 + 128]), 1)
                            SI(nc.scalar.copy(
                                Atmat[:, hs],
                                pb[bk][:, u * 256 + 128:u * 256 + 256]), 1)
                        bank_free[bk] = (sS, sS.v)
                        aev[pr] = sS.v
                    for pr in range(8):
                        bk = next_bank(pr % 2)
                        bank_wait(bk)
                        nc.tensor.wait_ge(sS.h, aev[pr])
                        nc.tensor.wait_ge(*yev[pr])
                        for u in range(2):
                            hh = pr * 2 + u
                            hs = slice(hh * 128, (hh + 1) * 128)
                            nc.tensor.matmul(
                                pb[bk][:, u * 256:(u + 1) * 256], I_bf[:],
                                Yb[:, hh * 256:(hh + 1) * 256],
                                start=True, stop=False)
                            mm = nc.tensor.matmul(
                                pb[bk][:, u * 256:(u + 1) * 256], Atmat[:, hs],
                                Yb[:, hh * 256:(hh + 1) * 256],
                                start=False, stop=True)
                        sT.inc(mm, 1)
                        nc.scalar.wait_ge(sT.h, sT.v)
                        SI(nc.scalar.copy(Yb[:, pr * 512:(pr + 1) * 512],
                                          pb[bk][:, 0:512]), 1)
                        bank_free[bk] = (sS, sS.v)
                        yev[pr] = (sS.h, sS.v)
                nc.tensor.wait_ge(sS.h, sS.v)
                for hg in range(4):
                    bk = next_bank(0)
                    bank_wait(bk)
                    for j in range(4):
                        hh = hg * 4 + j
                        mm = nc.tensor.matmul(
                            pb[bk][:, j * 128:(j + 1) * 128],
                            Yb[:, hh * 256 + 128:(hh + 1) * 256],
                            I_bf[:], start=True, stop=True)
                    sT.inc(mm, 1)
                    nc.scalar.wait_ge(sT.h, sT.v)
                    SI(nc.scalar.activation(
                        w2tn[:, hg * 512:(hg + 1) * 512], pb[bk][:, :],
                        AF.Copy, scale=-1.0), 1)
                    bank_free[bk] = (sS, sS.v)

            def seq_step(c, with_o):
                nc.tensor.wait_ge(sS.h, sS.v)
                nc.tensor.wait_ge(sV.h, sV.v)
                uev = [0] * 4
                for qq in range(4):
                    bk = next_bank(1)
                    bank_wait(bk)
                    for u in range(4):
                        hh = qq * 4 + u
                        hs = slice(hh * 128, (hh + 1) * 128)
                        mm = nc.tensor.matmul(
                            pb[bk][:, u * 128:(u + 1) * 128], w2tn[:, hs],
                            S_b[:, hs], start=True, stop=True)
                    sT.inc(mm, 1)
                    nc.vector.wait_ge(sT.h, sT.v)
                    VI(nc.vector.tensor_tensor(
                        U_all[:, qq * 512:(qq + 1) * 512]
                        .rearrange("p (q t) -> p q t", q=4),
                        Y3[:, qq * 4:(qq + 1) * 4, 0:128],
                        pb[bk][:, 0:512].rearrange("p (q t) -> p q t", q=4),
                        ALU.add), 1)
                    bank_free[bk] = (sV, sV.v)
                    uev[qq] = sV.v
                if with_o:
                    for qq in range(4):
                        bk = next_bank(0)
                        bank_wait(bk)
                        nc.tensor.wait_ge(sV.h, uev[qq])
                        for u in range(4):
                            hh = qq * 4 + u
                            hs = slice(hh * 128, (hh + 1) * 128)
                            nc.tensor.matmul(
                                pb[bk][:, u * 128:(u + 1) * 128], qbar[:, hs],
                                S_b[:, hs], start=True, stop=False)
                            mm = nc.tensor.matmul(
                                pb[bk][:, u * 128:(u + 1) * 128], Ptmat[:, hs],
                                U_all[:, hs], start=False, stop=True)
                        sT.inc(mm, 1)
                        nc.scalar.wait_ge(sT.h, sT.v)
                        SI(nc.scalar.copy(obuf[:, qq * 512:(qq + 1) * 512],
                                          pb[bk][:, 0:512]), 1)
                        bank_free[bk] = (sS, sS.v)
                for qq in range(4):
                    bk = next_bank(1)
                    bank_wait(bk)
                    nc.tensor.wait_ge(sV.h, uev[qq])
                    for u in range(4):
                        hh = qq * 4 + u
                        hs = slice(hh * 128, (hh + 1) * 128)
                        mm = nc.tensor.matmul(
                            pb[bk][:, u * 128:(u + 1) * 128], khat[:, hs],
                            U_all[:, hs], start=True, stop=True)
                    sT.inc(mm, 1)
                    nc.vector.wait_ge(sT.h, sT.v)
                    for u in range(4):
                        hh = qq * 4 + u
                        hs = slice(hh * 128, (hh + 1) * 128)
                        VI(nc.vector.scalar_tensor_tensor(
                            S_f[:, hs], S_f[:, hs], Gbe[:, hh:hh + 1],
                            pb[bk][:, u * 128:(u + 1) * 128],
                            ALU.mult, ALU.add), 1)
                    VI(nc.vector.tensor_copy(
                        S_b[:, qq * 512:(qq + 1) * 512],
                        S_f[:, qq * 512:(qq + 1) * 512]), 1)
                    bank_free[bk] = (sV, sV.v)

            _sid2a, _ = nc.enter_named_scope("p2a", False)
            for c in range(CH):
                soft_sync()
                chunk_pre(c)
                nc.scalar.wait_ge(sV.h, sV.v)
                qP.start((op_sp[c, 0].transpose([1, 0, 2]), Y3[:, :, 0:128]))
                qP.start((op_sp[c, 1].transpose([1, 0, 2]), Y3[:, :, 128:256]))
                qP.start((op_sp[c, 2].transpose([1, 0, 2]),
                          w2tn[:].rearrange("p (h t) -> p h t", h=H)))
                qP.start((op_sp[c, 3].transpose([1, 0, 2]),
                          khat[:].rearrange("p (h t) -> p h t", h=H)))
                qP.start((op_sp[c, 4].transpose([1, 0, 2]),
                          Ptmat[:].rearrange("p (h t) -> p h t", h=H)))
                qP.start((op_sp[c, 5].transpose([1, 0, 2]),
                          qbar[:].rearrange("p (h t) -> p h t", h=H)))
                seq_step(c, with_o=False)
            nc.leave_named_scope("p2a", _sid2a, False)
            _sid2g, _ = nc.enter_named_scope("p2ag", False)
            soft_sync()
            qX.start((ag_s_in, S_f[:]))
            qX.wait_all(nc.gpsimd)
            sC.inc(nc.gpsimd.collective_compute(
                "AllGather", ALU.bypass, replica_groups=[list(range(NCORE))],
                ins=[ag_s_in.opt()], outs=[ag_s_out.opt()]), 1)
            nc.sync.wait_ge(sC.h, sC.v)
            nc.vector.wait_ge(sC.h, sC.v)
            qX.wait_all(nc.vector)
            VI(nc.vector.memset(S_f[:], 0.0), 1)
            for j in range(4):
                qX.start((rowb[:], ag_s_out[2 * j]))
                qX.wait_all(nc.vector)
                VI(nc.vector.scalar_tensor_tensor(
                    S_f[:], rowb[:], sel4[:, j:j + 1], S_f[:],
                    ALU.mult, ALU.add), 1)
                nc.sync.wait_ge(sV.h, sV.v)
            VI(nc.vector.tensor_copy(S_b[:], S_f[:]), 1)

            nc.leave_named_scope("p2ag", _sid2g, False)
            _sid2b, _ = nc.enter_named_scope("p2b", False)
            for c in range(CH):
                soft_sync()
                qX.start((Y3[:, :, 0:128], op_sp[c, 0].transpose([1, 0, 2])))
                qX.start((Y3[:, :, 128:256], op_sp[c, 1].transpose([1, 0, 2])))
                qX.start((w2tn[:], op_sp[c, 2].transpose([1, 0, 2])))
                qX.start((khat[:], op_sp[c, 3].transpose([1, 0, 2])))
                qX.start((Ptmat[:], op_sp[c, 4].transpose([1, 0, 2])))
                qX.start((qbar[:], op_sp[c, 5].transpose([1, 0, 2])))
                qX.start((zc[:], z_sp[c]))
                qX.start((rowb[:],
                          gcum_d[0:16, c * 128:(c + 1) * 128]
                          .unsqueeze(0).broadcast_to((128, 16, 128))))
                qX.wait_all(nc.scalar)
                SI(nc.scalar.activation(
                    Gbe[:].unsqueeze(2), rowb3[:, :, 127:128], AF.Exp), 1)
                qX.wait_all(nc.tensor)
                qX.wait_all(nc.vector)
                seq_step(c, with_o=True)
                ob3 = obuf[:].rearrange("p (h t) -> p h t", h=H)
                on3 = onb[:].rearrange("p (h t) -> p h t", h=H)
                nc.vector.wait_ge(sS.h, sS.v)
                VI(nc.vector.tensor_tensor(obuf[:], obuf[:], zc[:], ALU.mult), 1)
                VI(nc.vector.tensor_tensor(art[:], obuf[:], obuf[:], ALU.mult), 1)
                VI(nc.vector.tensor_reduce(
                    sq16[:], art[:].rearrange("p (h t) -> p h t", h=H),
                    AX.X, ALU.add), 1)
                VI(nc.vector.tensor_scalar_mul(sq16[:], sq16[:], 1.0 / 128.0), 1)
                nc.scalar.wait_ge(sV.h, sV.v)
                SI(nc.scalar.activation(sq16[:], sq16[:], AF.Sqrt,
                                        bias=ep6[:]), 1)
                nc.vector.wait_ge(sS.h, sS.v)
                VI(nc.vector.reciprocal(sq16[:], sq16[:]), 1)
                VI(nc.vector.tensor_tensor(on3, ob3, bc16(sq16[:]), ALU.mult), 1)
                VI(nc.vector.tensor_tensor(
                    on3, on3, nwb[:].unsqueeze(1).broadcast_to((128, 16, 128)),
                    ALU.mult), 1)
                nc.tensor.wait_ge(sV.h, sV.v)
                for hg in range(4):
                    bk = next_bank(0)
                    bank_wait(bk)
                    for j in range(4):
                        hh = hg * 4 + j
                        mm = nc.tensor.matmul(
                            pb[bk][:, j * 128:(j + 1) * 128],
                            onb[:, hh * 128:(hh + 1) * 128],
                            I_bf[:], start=True, stop=True)
                    sT.inc(mm, 1)
                    nc.vector.wait_ge(sT.h, sT.v)
                    qX.wait_all(nc.vector)
                    VI(nc.vector.tensor_copy(stg2[:], pb[bk][:, :]), 1)
                    bank_free[bk] = (sV, sV.v)
                    nc.sync.wait_ge(sV.h, sV.v)
                    qX.start((ocm_sp[hg * 4:(hg + 1) * 4, :, c * 128:(c + 1) * 128]
                              .transpose([1, 0, 2]), stg2[:]))
            nc.leave_named_scope("p2b", _sid2b, False)
            drain_dmas()


        def phase3():
          with contextlib.ExitStack() as ps:
            def pa(n, s, d):
                return ps.enter_context(nc.sbuf_tensor(n, s, d))
            wo = pa("wo", [128, 16 * 2048], BF16)
            oc_l = pa("oc_l", [128, 2048], BF16)
            xres = pa("xres", [128, 2048], F32)
            x2t = pa("x2t", [128, 2048], F32)
            for kk in range(NKD):
                qW.start((wo[:, kk * 2048:(kk + 1) * 2048], wout_d[kk]))
            qW.wait_all(nc.tensor)
            for tt in range(16):
                qX.wait_all(nc.sync)
                tko = qX.start((oc_l[:], ocm_sp[:, :, tt * 128:(tt + 1) * 128]
                                .transpose([1, 0, 2])))
                txr = qX.start((xres[:], x_d[3 + tt * 128:3 + (tt + 1) * 128, :]))
                qX.wait(nc.tensor, tko)
                qX.wait(nc.vector, txr)
                for nb in range(4):
                    bk = next_bank(0)
                    bank_wait(bk)
                    for kk in range(NKD):
                        mm = nc.tensor.matmul(
                            pb[bk][:, 0:512],
                            oc_l[:, kk * 128:(kk + 1) * 128],
                            wo[:, kk * 2048 + nb * 512:kk * 2048 + nb * 512 + 512],
                            start=(kk == 0), stop=(kk == NKD - 1))
                    sT.inc(mm, 1)
                    nc.vector.wait_ge(sT.h, sT.v)
                    cs = slice(nb * 512, (nb + 1) * 512)
                    VI(nc.vector.tensor_tensor(
                        x2t[:, cs], pb[bk][:, 0:512], gab[:, cs], ALU.mult), 1)
                    bank_free[bk] = (sV, sV.v)
                    VI(nc.vector.tensor_tensor(
                        x2t[:, cs], x2t[:, cs], xres[:, cs], ALU.add), 1)
                nc.sync.wait_ge(sV.h, sV.v)
                qX.start((x2_sp[tt], x2t[:]))
            drain_dmas()

        def phase4():
          with contextlib.ExitStack() as ps:
            def pa(n, s, d):
                return ps.enter_context(nc.sbuf_tensor(n, s, d))
            ep5 = pa("ep5", [128, 1], F32)
            GI(nc.gpsimd.memset(ep5[:], 1e-5), 1)
            nc.vector.wait_ge(sG.h, sG.v)
            nc.scalar.wait_ge(sG.h, sG.v)
            N1 = float(1.0 / np.sqrt(8192.0))
            rs2_all = pa("rs2_all", [128, 16], F32)
            x1_es = contextlib.ExitStack()
            xq1T = x1_es.enter_context(
                nc.sbuf_tensor("xq1T", [128, NKD * 2048], BF16))
            xq1T3 = xq1T[:].rearrange("p (k c) -> p k c", k=NKD)

            _sc2, _ = nc.enter_named_scope("c2", False)
            c2_es = contextlib.ExitStack()
            def ca(n, s, d):
                return c2_es.enter_context(nc.sbuf_tensor(n, s, d))
            x2t = ca("x2l", [128, 2 * 2048], F32)
            sqt = ca("sqt4", [128, 2048], F32)
            stat = ca("stat4", [128, 8], F32)
            h2a = ca("h2a", [128, 2048], F32)
            h2b = ca("h2b", [128, 2 * 2048], BF16)
            i32t = ca("i32t", [128, 2048], I32)
            tr_done4 = {}
            ln_v4 = {}
            rs1tok = {}
            for tt in range(16):
                xbuf = x2t[:, (tt % 2) * 2048:(tt % 2) * 2048 + 2048]
                if tt >= 2:
                    nc.sync.wait_ge(sV.h, ln_v4[tt - 2])
                tx = qX.start((xbuf, x2_sp[tt]))
                qX.wait(nc.vector, tx)
                qX.wait(nc.scalar, tx)
                VI(nc.vector.tensor_reduce(stat[:, 0:1], xbuf, AX.X, ALU.add), 1)
                SI(nc.scalar.activation(sqt[:], xbuf, AF.Square,
                                        accum_out=stat[:, 2:3]), 1)
                nc.vector.wait_ge(sS.h, sS.v)
                VI(nc.vector.tensor_scalar_mul(stat[:, 1:2], stat[:, 0:1],
                                               1.0 / 2048.0), 1)
                VI(nc.vector.tensor_scalar_mul(stat[:, 3:4], stat[:, 2:3],
                                               1.0 / 2048.0), 1)
                VI(nc.vector.tensor_tensor(stat[:, 4:5], stat[:, 1:2],
                                           stat[:, 1:2], ALU.mult), 1)
                VI(nc.vector.tensor_tensor(stat[:, 3:4], stat[:, 3:4],
                                           stat[:, 4:5], ALU.subtract), 1)
                nc.scalar.wait_ge(sV.h, sV.v)
                SI(nc.scalar.activation(stat[:, 5:6], stat[:, 3:4], AF.Sqrt,
                                        bias=ep5[:]), 1)
                nc.vector.wait_ge(sS.h, sS.v)
                VI(nc.vector.reciprocal(stat[:, 6:7], stat[:, 5:6]), 1)
                VI(nc.vector.tensor_scalar(h2a[:], xbuf, stat[:, 1:2],
                                           stat[:, 6:7], ALU.subtract, ALU.mult), 1)
                VI(nc.vector.tensor_tensor(h2a[:], h2a[:], cb1b[:], ALU.mult), 1)
                VI(nc.vector.tensor_tensor(h2a[:], h2a[:], sbb[:], ALU.add), 1)
                VI(nc.vector.tensor_reduce(stat[:, 0:1], h2a[:], AX.X, ALU.max,
                                           apply_absolute_value=True), 1)
                VI(nc.vector.tensor_scalar(stat[:, 0:1], stat[:, 0:1], 1e-5,
                                           None, ALU.max), 1)
                VI(nc.vector.reciprocal(stat[:, 1:2], stat[:, 0:1]), 1)
                VI(nc.vector.tensor_scalar_mul(stat[:, 2:3], stat[:, 1:2],
                                               128.0), 1)
                VI(nc.vector.tensor_scalar_mul(stat[:, 3:4], stat[:, 0:1],
                                               1.0 / 128.0), 1)
                VI(nc.vector.tensor_copy(sc1_r[:, tt:tt + 1], stat[:, 3:4]), 1)
                hb = h2b[:, (tt % 2) * 2048:(tt % 2) * 2048 + 2048]
                if tt - 2 in tr_done4:
                    nc.vector.wait_ge(sT.h, tr_done4[tt - 2])
                VI(nc.vector.tensor_scalar(h2a[:], h2a[:], stat[:, 2:3],
                                           CLIP_HI, ALU.mult, ALU.min), 1)
                VI(nc.vector.tensor_scalar(h2a[:], h2a[:], CLIP_LO, None,
                                           ALU.max), 1)
                VI(nc.vector.tensor_scalar(i32t[:], h2a[:], MAGIC, MAGIC,
                                           ALU.add, ALU.subtract), 1)
                VI(nc.vector.tensor_copy(hb, i32t[:]), 1)
                nc.tensor.wait_ge(sV.h, sV.v)
                for g4 in range(4):
                    bk = next_bank(0)
                    bank_wait(bk)
                    for j in range(4):
                        kk = g4 * 4 + j
                        mm = nc.tensor.matmul(
                            pb[bk][:, j * 128:(j + 1) * 128],
                            hb[:, kk * 128:(kk + 1) * 128],
                            I_bf[:], start=True, stop=True)
                    sT.inc(mm, 1)
                    nc.scalar.wait_ge(sT.h, sT.v)
                    SI(nc.scalar.copy(
                        xq1T3[:, g4 * 4:(g4 + 1) * 4, tt * 128:(tt + 1) * 128],
                        pb[bk][:, :].rearrange("p (j c) -> p j c", j=4)), 1)
                    bank_free[bk] = (sS, sS.v)
                tr_done4[tt] = sT.v
                ln_v4[tt] = sV.v
            drain_dmas()
            c2_es.close()

            nc.leave_named_scope("c2", _sc2, False)
            _sc3, _ = nc.enter_named_scope("c3", False)
            c3_es = contextlib.ExitStack()
            def c3a(n, s, d):
                return c3_es.enter_context(nc.sbuf_tensor(n, s, d))
            wgt = c3a("wgt", [128, 2 * 2048], BF16)
            wut = c3a("wut", [128, 2 * 2048], BF16)
            rsbg = c3a("rsbg", [128, 2048], F32)
            rsbu = c3a("rsbu", [128, 2048], F32)
            gsil = c3a("gsil", [128, 2048], BF16)
            upt = c3a("upt", [128, 2048], BF16)
            intb = c3a("intb", [128, 2048], BF16)
            rs1b = c3a("rs1b", [128, 16], BF16)
            rrow16 = c3a("rrow16", [16, 128], BF16)
            VI(nc.vector.tensor_copy(rs1b[:], sc1_r[:]), 1)
            nc.tensor.wait_ge(sV.h, sV.v)
            bk = next_bank(0)
            bank_wait(bk)
            mm = nc.tensor.matmul(pb[bk][0:16, 0:128], rs1b[:], I_bf[:],
                                  start=True, stop=True)
            sT.inc(mm, 1)
            nc.vector.wait_ge(sT.h, sT.v)
            VI(nc.vector.tensor_copy(rrow16[:], pb[bk][0:16, 0:128]), 1)
            bank_free[bk] = (sV, sV.v)
            nc.tensor.wait_ge(sV.h, sV.v)
            for tt in range(16):
                bk = next_bank(0)
                bank_wait(bk)
                mm = nc.tensor.matmul(
                    pb[bk][:, 0:128], sel16b[:, tt * 128:(tt + 1) * 128],
                    rrow16[:], start=True, stop=True)
                sT.inc(mm, 1)
                nc.vector.wait_ge(sT.h, sT.v)
                VI(nc.vector.tensor_copy(rsbg[:, tt * 128:(tt + 1) * 128],
                                         pb[bk][:, 0:128]), 1)
                bank_free[bk] = (sV, sV.v)
            VI(nc.vector.tensor_scalar(rsbu[:], rsbg[:], sbars[:, 1:2],
                                       None, ALU.mult), 1)
            VI(nc.vector.tensor_scalar(rsbg[:], rsbg[:], sbars[:, 0:1],
                                       None, ALU.mult), 1)
            c3wt = [None, None]
            c3mm = {}
            sp_tok = [None]

            def c3_loads(m):
                toks = []
                for wi, soff in ((0, 0), (1, SLAB)):
                    buf = wgt if wi == 0 else wut
                    dst = buf[:, (m % 2) * 2048:(m % 2) * 2048 + 2048]
                    dst4 = dst.rearrange("p (r kl c) -> p r kl c", r=8, kl=2)
                    for kl in range(2):
                        src = ag_w_out[:, soff + m * 32768 + kl * 16384:
                                       soff + m * 32768 + (kl + 1) * 16384]
                        toks.append(qX.start(
                            (dst4[:, :, kl, :],
                             src.rearrange("r (pp c) -> pp r c", pp=128))))
                c3wt[m % 2] = toks

            c3_loads(0)
            for m in range(64):
                if m + 1 < 64:
                    if m >= 1:
                        nc.sync.wait_ge(sT.h, c3mm[m - 1])
                    c3_loads(m + 1)
                for tok in c3wt[m % 2]:
                    qX.wait(nc.tensor, tok)
                bkv = []
                for wi in range(2):
                    buf = wgt if wi == 0 else wut
                    for nb in range(4):
                        bk = wi * 4 + nb
                        bank_wait(bk)
                        for kk in range(NKD):
                            mm = nc.tensor.matmul(
                                pb[bk][:, 0:512],
                                buf[:, (m % 2) * 2048 + kk * 128:
                                    (m % 2) * 2048 + (kk + 1) * 128],
                                xq1T3[:, kk, nb * 512:(nb + 1) * 512],
                                start=(kk == 0), stop=(kk == NKD - 1))
                        sT.inc(mm, 1)
                        bkv.append(sT.v)
                c3mm[m] = sT.v
                for nb in range(4):
                    cs = slice(nb * 512, (nb + 1) * 512)
                    nc.vector.wait_ge(sT.h, bkv[nb])
                    VI(nc.vector.tensor_tensor(
                        gsil[:, cs], pb[nb][:, 0:512], rsbg[:, cs], ALU.mult), 1)
                    bank_free[nb] = (sV, sV.v)
                    nc.vector.wait_ge(sT.h, bkv[4 + nb])
                    VI(nc.vector.tensor_tensor(
                        upt[:, cs], pb[4 + nb][:, 0:512], rsbu[:, cs],
                        ALU.mult), 1)
                    bank_free[4 + nb] = (sV, sV.v)
                nc.scalar.wait_ge(sV.h, sV.v)
                SI(nc.scalar.activation(gsil[:], gsil[:], AF.Silu), 1)
                nc.vector.wait_ge(sS.h, sS.v)
                if sp_tok[0] is not None:
                    qX.wait(nc.vector, sp_tok[0])
                VI(nc.vector.tensor_tensor(intb[:], gsil[:], upt[:], ALU.mult), 1)
                nc.sync.wait_ge(sV.h, sV.v)
                sp_tok[0] = qX.start((inter_sp[:, m].transpose([1, 0, 2]),
                                      intb[:]))
            drain_dmas()
            c3_es.close()
            x1_es.close()

            nc.leave_named_scope("c3", _sc3, False)
            _sc4a, _ = nc.enter_named_scope("c4a", False)
            c4_es = contextlib.ExitStack()
            def c4a(n, s, d):
                return c4_es.enter_context(nc.sbuf_tensor(n, s, d))
            iz = c4a("iz", [128, 64 * 256], BF16)
            z1 = c4a("z1", [128, 64 * 256], BF16)
            fw = iz
            s2a = c4a("s2a", [128, 512], BF16)
            s2b = c4a("s2b", [128, 512], BF16)
            mxj = c4a("mxj", [128, 256], F32)
            mxjb = c4a("mxjb", [128, 256], BF16)
            mcol = c4a("mcol", [128, 2], F32)
            mcolb = c4a("mcolb", [128, 2], BF16)
            mrow = c4a("mrow", [16, 128], BF16)
            s2r = c4a("s2r", [128, 256], F32)
            xq2b = z1
            i32f = c4a("i32f", [128, 16 * 256], I32)
            fw32 = c4a("fw32", [128, 16 * 256], F32)
            VI(nc.vector.memset(mrow[:], 0.0), 1)
            for blk in range(8):
                qX.wait_all(nc.sync)
                ti = qX.start((iz[:], inter_sp[blk].transpose([1, 0, 2])))
                qX.wait(nc.tensor, ti)
                iz3 = iz[:].rearrange("p (i t) -> p i t", i=64)
                z13 = z1[:].rearrange("p (i t) -> p i t", i=64)
                fw3 = fw[:].rearrange("p (i t) -> p i t", i=64)
                z1g = z1[:].rearrange("p (g i t) -> p g i t", g=128, i=64)
                for ig in range(32):
                    bk = next_bank(0)
                    bank_wait(bk)
                    for j in range(2):
                        mm = nc.tensor.matmul(
                            pb[bk][:, j * 256:(j + 1) * 256],
                            H128[:], iz3[:, ig * 2 + j, :],
                            start=True, stop=True)
                    sT.inc(mm, 1)
                    nc.scalar.wait_ge(sT.h, sT.v)
                    SI(nc.scalar.copy(
                        z1g[:, :, ig * 2:ig * 2 + 2, :].transpose([0, 2, 1, 3]),
                        pb[bk][:, :].rearrange("p (i g t) -> p i g t",
                                               i=2, g=128)), 1)
                    bank_free[bk] = (sS, sS.v)
                nc.tensor.wait_ge(sS.h, sS.v)
                for gq in range(32):
                    bk = next_bank(0)
                    bank_wait(bk)
                    for j in range(4):
                        g = gq * 4 + j
                        mm = nc.tensor.matmul(
                            pb[bk][:, j * 128:(j + 1) * 128],
                            z1[:, g * 128:(g + 1) * 128], I_bf[:],
                            start=True, stop=True)
                    sT.inc(mm, 1)
                    nc.scalar.wait_ge(sT.h, sT.v)
                    SI(nc.scalar.copy(s2a[:], pb[bk][:, :]), 1)
                    bank_free[bk] = (sS, sS.v)
                    bk2 = next_bank(0)
                    bank_wait(bk2)
                    nc.tensor.wait_ge(sS.h, sS.v)
                    for j in range(4):
                        mm = nc.tensor.matmul(
                            pb[bk2][:, j * 128:(j + 1) * 128],
                            HXs[:], s2a[:, j * 128:(j + 1) * 128],
                            start=True, stop=True)
                    sT.inc(mm, 1)
                    nc.scalar.wait_ge(sT.h, sT.v)
                    SI(nc.scalar.copy(s2b[:], pb[bk2][:, :]), 1)
                    bank_free[bk2] = (sS, sS.v)
                    bk3 = next_bank(0)
                    bank_wait(bk3)
                    nc.tensor.wait_ge(sS.h, sS.v)
                    for j in range(4):
                        mm = nc.tensor.matmul(
                            pb[bk3][:, j * 128:(j + 1) * 128],
                            s2b[:, j * 128:(j + 1) * 128], I_bf[:],
                            start=True, stop=True)
                    sT.inc(mm, 1)
                    nc.vector.wait_ge(sT.h, sT.v)
                    for j in range(4):
                        g = gq * 4 + j
                        VI(nc.vector.tensor_copy(
                            fw3[:, :, 2 * g:2 * g + 2],
                            pb[bk3][:, j * 128:(j + 1) * 128]
                            .rearrange("p (i t) -> p i t", i=64)), 1)
                    bank_free[bk3] = (sV, sV.v)
                    nc.tensor.wait_ge(sV.h, sV.v)
                VI(nc.vector.tensor_reduce(
                    mxj[:], fw3.transpose([0, 2, 1]), AX.X, ALU.max,
                    apply_absolute_value=True), 1)
                VI(nc.vector.tensor_copy(mxjb[:], mxj[:]), 1)
                nc.tensor.wait_ge(sV.h, sV.v)
                for j in range(2):
                    bk = next_bank(0)
                    bank_wait(bk)
                    mm = nc.tensor.matmul(
                        pb[bk][:, 0:128], mxjb[:, j * 128:(j + 1) * 128],
                        I_bf[:], start=True, stop=True)
                    sT.inc(mm, 1)
                    nc.vector.wait_ge(sT.h, sT.v)
                    VI(nc.vector.tensor_reduce(
                        mcol[:, j:j + 1], pb[bk][:, 0:128], AX.X, ALU.max), 1)
                    bank_free[bk] = (sV, sV.v)
                VI(nc.vector.tensor_scalar(mcol[:], mcol[:], N1, 1e-5,
                                           ALU.mult, ALU.max), 1)
                VI(nc.vector.reciprocal(mcol[:], mcol[:]), 1)
                VI(nc.vector.tensor_scalar_mul(mcol[:], mcol[:], 128.0 * N1), 1)
                VI(nc.vector.tensor_copy(mcolb[:], mcol[:]), 1)
                VI(nc.vector.tensor_copy(mcol[:], mcolb[:]), 1)
                VI(nc.vector.reciprocal(s2r[:, 0:2], mcol[:]), 1)
                VI(nc.vector.tensor_scalar(s2r[:, 0:2], s2r[:, 0:2],
                                           sbars[:, 2:3], N1,
                                           ALU.mult, ALU.mult), 1)
                VI(nc.vector.tensor_copy(rs2_all[:, blk * 2:blk * 2 + 2],
                                         s2r[:, 0:2]), 1)
                nc.tensor.wait_ge(sV.h, sV.v)
                bk = next_bank(0)
                bank_wait(bk)
                mm = nc.tensor.matmul(pb[bk][0:2, 0:128], mcolb[:], I_bf[:],
                                      start=True, stop=True)
                sT.inc(mm, 1)
                nc.vector.wait_ge(sT.h, sT.v)
                VI(nc.vector.tensor_copy(mrow[0:2, :], pb[bk][0:2, 0:128]), 1)
                bank_free[bk] = (sV, sV.v)
                nc.tensor.wait_ge(sV.h, sV.v)
                bk = next_bank(0)
                bank_wait(bk)
                nc.tensor.matmul(pb[bk][:, 0:128],
                                 sel16b[:, 0:128], mrow[:, 0:128],
                                 start=True, stop=True)
                mm = nc.tensor.matmul(pb[bk][:, 128:256],
                                      sel16b[:, 128:256], mrow[:, 0:128],
                                      start=True, stop=True)
                sT.inc(mm, 1)
                nc.vector.wait_ge(sT.h, sT.v)
                VI(nc.vector.tensor_copy(s2r[:], pb[bk][:, 0:256]), 1)
                bank_free[bk] = (sV, sV.v)
                for qc_ in range(4):
                    i0 = qc_ * 16
                    VI(nc.vector.tensor_tensor(
                        fw32[:].rearrange("p (i t) -> p i t", i=16),
                        fw3[:, i0:i0 + 16, :],
                        s2r[:].unsqueeze(1).broadcast_to((128, 16, 256)),
                        ALU.mult), 1)
                    VI(nc.vector.tensor_scalar(fw32[:], fw32[:], CLIP_LO,
                                               CLIP_HI, ALU.max, ALU.min), 1)
                    VI(nc.vector.tensor_scalar(i32f[:], fw32[:], MAGIC, MAGIC,
                                               ALU.add, ALU.subtract), 1)
                    VI(nc.vector.tensor_copy(
                        xq2b[:, i0 * 256:(i0 + 16) * 256], i32f[:]), 1)
                nc.sync.wait_ge(sV.h, sV.v)
                for sb_ in range(2):
                    for iq in range(4):
                        qX.start((xq2_sp[2 * blk + sb_, iq * 16:(iq + 1) * 16]
                                  .transpose([1, 0, 2]),
                                  xq2b[:].rearrange("p (i t) -> p i t", i=64)
                                  [:, iq * 16:(iq + 1) * 16,
                                   sb_ * 128:(sb_ + 1) * 128]))
            drain_dmas()
            c4_es.close()

            nc.leave_named_scope("c4a", _sc4a, False)
            _sc4b, _ = nc.enter_named_scope("c4b", False)
            c5_es = contextlib.ExitStack()
            def c5a(n, s, d):
                return c5_es.enter_context(nc.sbuf_tensor(n, s, d))
            wdn = c5a("wdn", [128, 64 * 512], BF16)
            xqm = c5a("xqm", [128, 2 * 64 * 128], BF16)
            x2s = c5a("x2s", [128, 2 * 512], F32)
            yt = c5a("yt", [128, 2 * 512], F32)
            xq_tok = [None, None]
            x2_tok = [None, None]
            st_tok = [None, None]
            mt_mm = {}
            mt_vv = {}

            def c4b_loads(mt, nb):
                sl = mt % 2
                if mt >= 2:
                    nc.sync.wait_ge(sT.h, mt_mm[mt - 2])
                    nc.sync.wait_ge(sV.h, mt_vv[mt - 2])
                xqv3 = xqm[:, sl * 8192:sl * 8192 + 8192] \
                    .rearrange("p (i t) -> p i t", i=64)
                xq_tok[sl] = [qX.start(
                    (xqv3[:, iq * 16:(iq + 1) * 16, :],
                     xq2_sp[mt, iq * 16:(iq + 1) * 16].transpose([1, 0, 2])))
                    for iq in range(4)]
                x2_tok[sl] = qX.start(
                    (x2s[:, sl * 512:(sl + 1) * 512],
                     x2_sp[mt][:, nb * 512:(nb + 1) * 512]))

            for nb in range(4):
                nc.sync.wait_ge(sT.h, sT.v)
                wdt = []
                wdn4 = wdn[:].rearrange("p (k m c) -> p k m c", k=64, m=4)
                for r in range(8):
                    for mloc in range(4):
                        off = 2 * SLAB + (4 * nb + mloc) * 8 * 16384
                        srcw = ag_w_out[r, off:off + 8 * 16384]
                        wdt.append(qX.start(
                            (wdn4[:, r * 8:(r + 1) * 8, mloc, :],
                             srcw.rearrange("(kl pp c) -> pp kl c",
                                            kl=8, pp=128))))
                mt_mm.clear()
                mt_vv.clear()
                c4b_loads(0, nb)
                for t_ in wdt:
                    qX.wait(nc.tensor, t_)
                for mt in range(16):
                    sl = mt % 2
                    if mt + 1 < 16:
                        c4b_loads(mt + 1, nb)
                    for t_ in xq_tok[sl]:
                        qX.wait(nc.tensor, t_)
                    bk = next_bank(0)
                    bank_wait(bk)
                    for kk in range(64):
                        mm = nc.tensor.matmul(
                            pb[bk][:, 0:512],
                            xqm[:, sl * 8192 + kk * 128:
                                sl * 8192 + (kk + 1) * 128],
                            wdn[:, kk * 512:(kk + 1) * 512],
                            start=(kk == 0), stop=(kk == 63))
                    sT.inc(mm, 1)
                    mt_mm[mt] = sT.v
                    qX.wait(nc.vector, x2_tok[sl])
                    if st_tok[sl] is not None:
                        qX.wait(nc.vector, st_tok[sl])
                    nc.vector.wait_ge(sT.h, sT.v)
                    ys = yt[:, sl * 512:(sl + 1) * 512]
                    VI(nc.vector.tensor_scalar(
                        ys, pb[bk][:, 0:512], rs2_all[:, mt:mt + 1],
                        None, ALU.mult), 1)
                    bank_free[bk] = (sV, sV.v)
                    VI(nc.vector.tensor_tensor(
                        ys, ys, gbb[:, nb * 512:(nb + 1) * 512],
                        ALU.mult), 1)
                    VI(nc.vector.tensor_tensor(
                        ys, ys, x2s[:, sl * 512:(sl + 1) * 512], ALU.add), 1)
                    mt_vv[mt] = sV.v
                    nc.sync.wait_ge(sV.h, sV.v)
                    st_tok[sl] = qX.start(
                        (y_d[mt * 128:(mt + 1) * 128,
                             nb * 512:(nb + 1) * 512], ys))
            nc.leave_named_scope("c4b", _sc4b, False)
            drain_dmas()
            c5_es.close()

        def _scoped(name, fn):
            sid, _ = nc.enter_named_scope(name, False)
            fn()
            nc.leave_named_scope(name, sid, False)
        if 0 in phases:
            _scoped("p0", phase0)
        if 1 in phases:
            _scoped("p1", phase1)
        if 2 in phases:
            _scoped("p2", phase2)
        if 3 in phases:
            _scoped("p3", phase3)
        if 4 in phases:
            _scoped("p4", phase4)
        nc.all_engine_barrier()
    return nc


def kernel(**inputs):
    nc = build()
    maps = _host_inputs(inputs)
    res = run_bass_kernel_spmd(nc, maps, core_ids=list(range(NCORE)))
    out = np.empty((4, 4096, 2048), dtype=np.float32)
    for c in range(NCORE):
        b, s = c // 2, c % 2
        out[b, s * TOK:(s + 1) * TOK] = res.results[c]["y"]
    return out


# version 03
# speedup vs baseline: 1.0842x; 1.0842x over previous
import contextlib
import numpy as np
import concourse.bass as bass
import concourse.mybir as mybir
from concourse.bass_utils import run_bass_kernel_spmd

F32 = mybir.dt.float32
BF16 = mybir.dt.bfloat16
FP16 = mybir.dt.float16
I32 = mybir.dt.int32
AF = mybir.ActivationFunctionType
ALU = mybir.AluOpType
AX = mybir.AxisListType

MAGIC = 12582912.0
NCORE = 8
TOK = 2048
D = 2048
NKD = 16
H = 16
DK = DV = 128
CH = 16
TOKH = TOK + 3
HT = 2052
FFI = 8192
CLIP_LO = -128.49609375
CLIP_HI = 127.49609375
SLAB = 64 * 2 * 128 * 128


class SemC:
    def __init__(self, h):
        self.h = h
        self.v = 0

    def inc(self, instr, n=1):
        instr.then_inc(self.h, n)
        self.v += n
        return self.v


def _hadamard(n):
    h = np.array([[1.0]], dtype=np.float32)
    while h.shape[0] < n:
        h = np.block([[h, h], [h, -h]])
    return h


def _consts():
    i128 = np.eye(128, dtype=np.float32)
    h128 = _hadamard(128)
    h64 = _hadamard(64)
    hx = np.kron(h64, np.eye(2)).astype(np.float32)
    sel16 = np.zeros((16, 2048), dtype=np.float32)
    for hh in range(16):
        sel16[hh, hh * 128:(hh + 1) * 128] = 1.0
    tril_s = np.tril(np.ones((128, 128), dtype=np.float32), -1)
    triu_s = np.ascontiguousarray(tril_s.T)
    triu_i = np.triu(np.ones((128, 128), dtype=np.float32), 0)
    return {
        "c_I": i128, "c_H128": h128, "c_HX": hx,
        "c_mL": tril_s, "c_oL": (1.0 - tril_s) * 88.0,
        "c_mU": triu_s, "c_oU": (1.0 - triu_s) * (-88.0),
        "c_mUi": triu_i, "c_oUi": (1.0 - triu_i) * (-88.0),
        "c_sel16": sel16,
    }


def _host_inputs(inputs):
    x = np.ascontiguousarray(np.asarray(inputs["L0_attn_hidden_states"], dtype=np.float32))
    Wqkv = np.asarray(inputs["L0_attn_W_qkv"], dtype=np.float32)
    convw = np.asarray(inputs["L0_attn_conv_w"], dtype=np.float32)
    Wz = np.asarray(inputs["L0_attn_W_z"], dtype=np.float32)
    Wb = np.asarray(inputs["L0_attn_W_b"], dtype=np.float32)
    Wa = np.asarray(inputs["L0_attn_W_a"], dtype=np.float32)
    dtb = np.asarray(inputs["L0_attn_dt_bias"], dtype=np.float32)
    Alog = np.asarray(inputs["L0_attn_A_log"], dtype=np.float32)
    nw = np.asarray(inputs["L0_attn_norm_weight"], dtype=np.float32)
    Wout = np.asarray(inputs["L0_attn_W_out"], dtype=np.float32)
    wg = np.asarray(inputs["L0_ffn_w_gate"], dtype=np.float32)
    wu = np.asarray(inputs["L0_ffn_w_up"], dtype=np.float32)
    wd = np.asarray(inputs["L0_ffn_w_down"], dtype=np.float32)
    cc = np.asarray(inputs["L0_c"], dtype=np.float32)
    wmod = np.asarray(inputs["L0_wmod"], dtype=np.float32)

    def img(W, nm):
        nk = W.shape[0] // 128
        t = W.reshape(nk, 128, nm, 128)
        return np.ascontiguousarray(t.transpose(2, 1, 0, 3).reshape(nm, 128, nk * 128))

    wqkv_img = img(Wqkv, 48)
    wz_img = img(Wz, 16)
    Wba = np.concatenate([Wb, Wa], axis=1)
    wba_img = np.ascontiguousarray(
        Wba.reshape(16, 128, 32).transpose(1, 0, 2).reshape(128, 512))
    wout_img = np.ascontiguousarray(Wout.reshape(16, 128, 2048))
    convw_t = np.ascontiguousarray(convw.reshape(48, 128, 4))
    cT = np.ascontiguousarray(cc.T.reshape(16, 128, 4))
    wgT = np.ascontiguousarray(wg.T)
    wuT = np.ascontiguousarray(wu.T)
    wdT = np.ascontiguousarray(wd.T)

    def slab(WT, rows0, nkl, nm):
        s = WT[rows0:rows0 + nkl * 128]
        s = s.reshape(nkl, 128, nm, 128)
        return np.ascontiguousarray(s.transpose(2, 0, 1, 3))

    consts = _consts()
    maps = []
    for c in range(NCORE):
        b, s = c // 2, c % 2
        xs = np.zeros((TOKH, D), dtype=np.float32)
        t0 = s * TOK
        if t0 >= 3:
            xs[0:3] = x[b, t0 - 3:t0]
        xs[3:] = x[b, t0:t0 + TOK]
        halo = np.full((128, 1), 1.0 if s == 1 else 0.0, dtype=np.float32)
        sel = np.zeros((128, 4), dtype=np.float32)
        if s == 1:
            sel[:, (c - 1) // 2] = 1.0
        bsel = np.zeros((128, 4), dtype=np.float32)
        bsel[:, b] = 1.0
        m = {
            "x": xs, "halo_mask": halo, "sel4": sel, "bsel": bsel,
            "wqkv_img": wqkv_img, "wz_img": wz_img, "wba_img": wba_img,
            "wout_img": wout_img, "convw_t": convw_t, "cT": cT,
            "wmod_t": np.ascontiguousarray(
                wmod[:, c * 1536:(c + 1) * 1536].reshape(16, 128, 1536)),
            "dtb_col": dtb.reshape(16, 1).copy(),
            "alog_col": Alog.reshape(16, 1).copy(),
            "nw_row": nw.reshape(1, 128).copy(),
            "wg_slab": slab(wgT, c * 256, 2, 64),
            "wu_slab": slab(wuT, c * 256, 2, 64),
            "wd_slab": slab(wdT, c * 1024, 8, 16),
        }
        m.update(consts)
        maps.append(m)
    return maps


def build(phases=(0, 1, 2, 3, 4), taps=()):
    nc = bass.Bass("TRN2", target_bir_lowering=False)
    tap = set(taps)

    def dram_in(name, shape, dt=F32):
        return nc.dram_tensor(name, shape, dt, kind="ExternalInput").ap()

    def dram_scr(name, shape, dt):
        kind = "ExternalOutput" if name in tap else "Internal"
        return nc.dram_tensor(name, shape, dt, kind=kind).ap()

    x_d = dram_in("x", [TOKH, D])
    halo_d = dram_in("halo_mask", [128, 1])
    sel4_d = dram_in("sel4", [128, 4])
    bsel_d = dram_in("bsel", [128, 4])
    wqkv_d = dram_in("wqkv_img", [48, 128, 2048])
    wz_d = dram_in("wz_img", [16, 128, 2048])
    wba_d = dram_in("wba_img", [128, 512])
    wout_d = dram_in("wout_img", [16, 128, 2048])
    convw_d = dram_in("convw_t", [48, 128, 4])
    cT_d = dram_in("cT", [16, 128, 4])
    wmod_d = dram_in("wmod_t", [16, 128, 1536])
    dtb_d = dram_in("dtb_col", [16, 1])
    alog_d = dram_in("alog_col", [16, 1])
    nw_d = dram_in("nw_row", [1, 128])
    wgs_d = dram_in("wg_slab", [64, 2, 128, 128])
    wus_d = dram_in("wu_slab", [64, 2, 128, 128])
    wds_d = dram_in("wd_slab", [16, 8, 128, 128])
    cI_d = dram_in("c_I", [128, 128])
    cH_d = dram_in("c_H128", [128, 128])
    cHX_d = dram_in("c_HX", [128, 128])
    cmL_d = dram_in("c_mL", [128, 128])
    coL_d = dram_in("c_oL", [128, 128])
    cmU_d = dram_in("c_mU", [128, 128])
    coU_d = dram_in("c_oU", [128, 128])
    cmUi_d = dram_in("c_mUi", [128, 128])
    coUi_d = dram_in("c_oUi", [128, 128])
    csel_d = dram_in("c_sel16", [16, 2048])

    y_d = nc.dram_tensor("y", [TOK, D], F32, kind="ExternalOutput").ap()

    ag_w_in = dram_scr("ag_w_in", [3 * SLAB], BF16)
    ag_w_out = dram_scr("ag_w_out", [NCORE, 3 * SLAB], BF16)
    ar_s_in = dram_scr("ar_s_in", [1, 4], F32)
    ar_s_out = dram_scr("ar_s_out", [1, 4], F32)
    ag_m_in = dram_scr("ag_m_in", [4, 1536], F32)
    ag_m_out = dram_scr("ag_m_out", [NCORE, 4, 1536], F32)
    mod2_d = dram_scr("mod2", [4, 12288], F32)
    q_sp = dram_scr("q_sp", [CH, H, 128, 128], BF16)
    k_sp = dram_scr("k_sp", [CH, H, 128, 128], BF16)
    v_sp = dram_scr("v_sp", [CH, H, 128, 128], BF16)
    z_sp = dram_scr("z_sp", [CH, 128, 2048], BF16)
    gcum_d = dram_scr("gcum_sp", [16, 2048], F32)
    beta_d = dram_scr("beta_sp", [16, 2048], F32)
    op_sp = dram_scr("op_sp", [CH, 6, H, 128, 128], BF16)
    ag_s_in = dram_scr("ag_s_in", [128, H * 128], F32)
    ag_s_out = dram_scr("ag_s_out", [NCORE, 128, H * 128], F32)
    ocm_sp = dram_scr("ocm_sp", [H, 128, TOK], BF16)
    x2_sp = dram_scr("x2_sp", [16, 128, D], F32)
    inter_sp = dram_scr("inter_sp", [8, 64, 128, 256], BF16)
    xq2_sp = dram_scr("xq2_sp", [16, 64, 128, 128], BF16)
    rs1_d = dram_scr("rs1_sp", [16, 128], F32)
    s2tmp_d = dram_scr("s2tmp", [8, 2, 128], F32)
    rs2_d = dram_scr("rs2_sp", [16, 128], F32)

    es = contextlib.ExitStack()
    with es:
        sT = SemC(es.enter_context(nc.semaphore("sT")))
        sV = SemC(es.enter_context(nc.semaphore("sV")))
        sS = SemC(es.enter_context(nc.semaphore("sS")))
        sG = SemC(es.enter_context(nc.semaphore("sG")))
        sC = SemC(es.enter_context(nc.semaphore("sC")))

        class DmaPool:
            def __init__(self, name, eng, n):
                self.sems = [SemC(es.enter_context(nc.semaphore(f"{name}{i}")))
                             for i in range(n)]
                self.eng = eng
                self.i = 0
                self.used = [False] * n

            def start(self, args, _n=16, **kw):
                if not isinstance(args, tuple):
                    args = (args,)
                s = self.sems[self.i]
                if self.used[self.i]:
                    self.eng.wait_ge(s.h, s.v)
                self.used[self.i] = True
                self.i = (self.i + 1) % len(self.sems)
                s.inc(self.eng.dma_start(*args, **kw), 16)
                return (s, s.v)

            def wait(self, e, tok):
                e.wait_ge(tok[0].h, tok[1])

            def wait_all(self, e):
                for s in self.sems:
                    e.wait_ge(s.h, s.v)

        qX = DmaPool("qX", nc.sync, 8)
        qW = DmaPool("qW", nc.gpsimd, 6)
        qP = DmaPool("qP", nc.scalar, 6)

        pb = [es.enter_context(nc.psum_tensor(f"pb{i}", [128, 512], F32))
              for i in range(8)]
        bank_free = [(None, 0)] * 8
        bank_rr = [0, 4]

        def bank_wait(i):
            semc, val = bank_free[i]
            if semc is not None:
                nc.tensor.wait_ge(semc.h, val)

        def next_bank(grp=0):
            lo, hi = (0, 4) if grp == 0 else (4, 8)
            bank_rr[grp] = lo + (bank_rr[grp] + 1 - lo) % (hi - lo)
            return bank_rr[grp]

        def drain_dmas():
            qX.wait_all(nc.sync)
            qP.wait_all(nc.sync)
            qW.wait_all(nc.sync)
            nc.all_engine_barrier()

        def alloc(name, shape, dt):
            return es.enter_context(nc.sbuf_tensor(name, shape, dt))

        def VI(instr, n=1):
            sV.inc(instr, n)
            nc.vector.drain()
            return sV.v

        def SI(instr, n=1):
            sS.inc(instr, n)
            nc.scalar.drain()
            return sS.v

        def GI(instr, n=1):
            sG.inc(instr, n)
            nc.gpsimd.drain()
            return sG.v

        I_bf = alloc("I_bf", [128, 128], BF16)
        H128 = alloc("H128", [128, 128], BF16)
        HXs = alloc("HXs", [128, 128], BF16)
        mL = alloc("mL", [128, 128], F32)
        oL = alloc("oL", [128, 128], F32)
        mU = alloc("mU", [128, 128], F32)
        oU = alloc("oU", [128, 128], F32)
        mUi = alloc("mUi", [128, 128], F32)
        oUi = alloc("oUi", [128, 128], F32)
        ones_b = alloc("ones_b", [128, 32], BF16)
        ones16 = alloc("ones16", [16, 128], BF16)
        sel16b = alloc("sel16b", [16, 2048], BF16)
        halo_m = alloc("halo_s", [128, 1], F32)
        sel4 = alloc("sel4_s", [128, 4], F32)
        bsel = alloc("bsel_s", [128, 4], F32)
        dtb = alloc("dtb", [16, 1], F32)
        al_c = alloc("al_c", [16, 1], F32)
        negea = alloc("negea", [16, 1], F32)
        nwb = alloc("nwb", [128, 128], BF16)
        kc1 = alloc("kc1", [128, 1], F32)

        ca1b = alloc("ca1b", [128, 2048], BF16)
        sab = alloc("sab", [128, 2048], BF16)
        gab = alloc("gab", [128, 2048], BF16)
        cb1b = alloc("cb1b", [128, 2048], BF16)
        sbb = alloc("sbb", [128, 2048], BF16)
        gbb = alloc("gbb", [128, 2048], BF16)

        beta_r = alloc("beta_r", [16, 2048], F32)
        g_r = alloc("g_r", [16, 2048], F32)
        gc_r = alloc("gc_r", [16, 2048], F32)
        gcT_r = alloc("gcT_r", [128, 16 * CH], F32)
        btT_r = alloc("btT_r", [128, 16 * CH], F32)
        sbars = alloc("sbars", [128, 4], F32)
        sc1_r = alloc("sc1_r", [128, 16], F32)

        nc.gpsimd.memset(kc1[:], 1.0)
        nc.gpsimd.memset(ones_b[:], 1.0)
        GI(nc.gpsimd.memset(ones16[:], 1.0), 1)
        for dst, src in [(I_bf, cI_d), (H128, cH_d), (HXs, cHX_d),
                         (sel16b, csel_d),
                         (nwb, nw_d[0:1, :].broadcast_to((128, 128)))]:
            qW.start((dst[:], src), 16)
        for dst, src in [(mL, cmL_d), (oL, coL_d), (mU, cmU_d),
                         (oU, coU_d), (mUi, cmUi_d), (oUi, coUi_d),
                         (halo_m, halo_d), (sel4, sel4_d), (bsel, bsel_d),
                         (dtb, dtb_d), (al_c, alog_d)]:
            qX.start((dst[:], src), 16)
        for e in (nc.tensor, nc.vector, nc.scalar, nc.gpsimd, nc.sync):
            qW.wait_all(e)
            qX.wait_all(e)
            e.wait_ge(sG.h, sG.v)
        SI(nc.scalar.activation(negea[:], al_c[:], AF.Exp), 1)
        SI(nc.scalar.mul(negea[:], negea[:], -1.0), 1)
        nc.all_engine_barrier()

        def phase0():
          with contextlib.ExitStack() as ps:
            def pa(n, s, d):
                return ps.enter_context(nc.sbuf_tensor(n, s, d))
            csT = pa("csT", [128, 64], BF16)
            cTl = pa("cTl", [128, 64], F32)
            wmod_s = pa("wmod_s", [128, 8 * 1536], BF16)
            mod_sl = pa("mod_sl", [4, 1536], F32)
            qX.start((cTl[:], cT_d.transpose([1, 0, 2])), 16)
            qX.wait_all(nc.scalar)
            SI(nc.scalar.activation(csT[:], cTl[:], AF.Silu), 1)
            nc.tensor.wait_ge(sS.h, sS.v)
            mbks = [next_bank(), next_bank(), next_bank()]
            for half in range(2):
                qW.start((
                    wmod_s[:], wmod_d[half * 8:(half + 1) * 8]
                    .transpose([1, 0, 2])), 16)
                qW.wait_all(nc.tensor)
                for nb in range(3):
                    bk = mbks[nb]
                    if half == 0:
                        bank_wait(bk)
                    for k2 in range(8):
                        mm = nc.tensor.matmul(
                            pb[bk][0:4, :],
                            csT[:, (half * 8 + k2) * 4:(half * 8 + k2) * 4 + 4],
                            wmod_s[:, k2 * 1536 + nb * 512: k2 * 1536 + nb * 512 + 512],
                            start=(half == 0 and k2 == 0),
                            stop=(half == 1 and k2 == 7))
                    sT.inc(mm, 1)
                if half == 0:
                    sG.inc(nc.gpsimd.nop(), 1)
                    nc.gpsimd.wait_ge(sT.h, sT.v)
            for nb in range(3):
                bk = mbks[nb]
                nc.scalar.wait_ge(sT.h, sT.v)
                SI(nc.scalar.copy(
                    mod_sl[:, nb * 512:(nb + 1) * 512], pb[bk][0:4, :]), 1)
                bank_free[bk] = (sS, sS.v)
            nc.scalar.wait_ge(sS.h, sS.v)
            qP.start((ag_m_in, mod_sl[:]), 16)
            qP.wait_all(nc.gpsimd)
            sC.inc(nc.gpsimd.collective_compute(
                "AllGather", ALU.bypass, replica_groups=[list(range(NCORE))],
                ins=[ag_m_in.opt()], outs=[ag_m_out.opt()]), 1)

            asum = pa("asum", [128, 4], F32)
            red = pa("red", [128, 1], F32)
            wbuf = pa("wbuf", [128, 2 * 4096], F32)
            VI(nc.vector.memset(asum[:], 0.0), 1)
            flats = [wgs_d.rearrange("a b p f -> (a b) p f"),
                     wus_d.rearrange("a b p f -> (a b) p f"),
                     wds_d.rearrange("a b p f -> (a b) p f")]
            vred = []
            vg = 0
            for si in range(3):
                for gg in range(4):
                    buf = wbuf[:, (vg % 2) * 4096:(vg % 2) * 4096 + 4096]
                    if vg >= 2:
                        nc.sync.wait_ge(sV.h, vred[vg - 2])
                    qX.start((
                        buf, flats[si][gg * 32:(gg + 1) * 32].transpose([1, 0, 2])), 16)
                    qX.wait_all(nc.vector)
                    VI(nc.vector.tensor_reduce(
                        red[:], buf, AX.X, ALU.add, apply_absolute_value=True), 1)
                    VI(nc.vector.tensor_tensor(
                        asum[:, si:si + 1], asum[:, si:si + 1], red[:], ALU.add), 1)
                    vred.append(sV.v)
                    vg += 1
            p4h = pa("p4h", [128, 4], BF16)
            p4f = pa("p4f", [128, 4], F32)
            p4l = pa("p4l", [128, 4], BF16)
            p4r = pa("p4r", [128, 4], F32)
            p4l2 = pa("p4l2", [128, 4], BF16)
            arow = pa("arow", [1, 4], F32)
            VI(nc.vector.tensor_copy(p4h[:], asum[:]), 1)
            VI(nc.vector.tensor_copy(p4f[:], p4h[:]), 1)
            VI(nc.vector.tensor_tensor(p4r[:], asum[:], p4f[:], ALU.subtract), 1)
            VI(nc.vector.tensor_copy(p4l[:], p4r[:]), 1)
            VI(nc.vector.tensor_copy(p4f[:], p4l[:]), 1)
            VI(nc.vector.tensor_tensor(p4r[:], p4r[:], p4f[:], ALU.subtract), 1)
            VI(nc.vector.tensor_copy(p4l2[:], p4r[:]), 1)
            bk = next_bank()
            bank_wait(bk)
            nc.tensor.wait_ge(sV.h, sV.v)
            nc.tensor.matmul(pb[bk][0:32, 0:4], ones_b[:], p4h[:], start=True, stop=False)
            nc.tensor.matmul(pb[bk][0:32, 0:4], ones_b[:], p4l[:], start=False, stop=False)
            mm = nc.tensor.matmul(pb[bk][0:32, 0:4], ones_b[:], p4l2[:], start=False, stop=True)
            sT.inc(mm, 1)
            nc.scalar.wait_ge(sT.h, sT.v)
            SI(nc.scalar.copy(arow[:], pb[bk][0:1, 0:4]), 1)
            bank_free[bk] = (sS, sS.v)
            nc.scalar.wait_ge(sS.h, sS.v)
            qP.start((ar_s_in, arow[:]), 16)
            qP.wait_all(nc.gpsimd)
            sC.inc(nc.gpsimd.collective_compute(
                "AllReduce", ALU.add, replica_groups=[list(range(NCORE))],
                ins=[ar_s_in.opt()], outs=[ar_s_out.opt()]), 1)
            nc.sync.wait_ge(sC.h, sC.v)
            qX.start((
                sbars[:], ar_s_out[0:1, :].broadcast_to((128, 4))), 16)
            qX.wait_all(nc.vector)
            nv = 1.0 / float(2048 * 8192)
            VI(nc.vector.tensor_scalar_mul(sbars[:], sbars[:], nv), 1)
            thrb = pa("thrb", [128, 4], F32)
            nthrb = pa("nthrb", [128, 4], F32)
            VI(nc.vector.tensor_scalar_mul(thrb[:], sbars[:], 0.5), 1)
            VI(nc.vector.tensor_scalar_mul(nthrb[:], sbars[:], -0.5), 1)
            b1 = pa("b1", [128, 4096], F32)
            b2t = pa("b2t", [128, 2 * 4096], BF16)
            vq, pq = [], []
            vg = 0
            for si in range(3):
                for gg in range(4):
                    buf = wbuf[:, (vg % 2) * 4096:(vg % 2) * 4096 + 4096]
                    obuf = b2t[:, (vg % 2) * 4096:(vg % 2) * 4096 + 4096]
                    if vg >= 2:
                        nc.sync.wait_ge(sV.h, vq[vg - 2])
                    qX.start((
                        buf, flats[si][gg * 32:(gg + 1) * 32].transpose([1, 0, 2])), 16)
                    qX.wait_all(nc.vector)
                    if vg >= 2:
                        qP.wait(nc.vector, pq[vg - 2])
                    VI(nc.vector.tensor_scalar(
                        obuf, buf, nthrb[:, si:si + 1], None, ALU.is_lt), 1)
                    VI(nc.vector.tensor_scalar(
                        b1[:], buf, thrb[:, si:si + 1], None, ALU.is_gt), 1)
                    VI(nc.vector.tensor_tensor(obuf, b1[:], obuf, ALU.subtract), 1)
                    vq.append(sV.v)
                    nc.scalar.wait_ge(sV.h, sV.v)
                    dst = ag_w_in[si * SLAB + gg * 32 * 16384:
                                  si * SLAB + (gg + 1) * 32 * 16384]
                    pq.append(qP.start((
                        dst.rearrange("(t p f) -> t p f", t=32, p=128)
                        .transpose([1, 0, 2]), obuf)))

                    vg += 1
            qP.wait_all(nc.gpsimd)
            sC.inc(nc.gpsimd.collective_compute(
                "AllGather", ALU.bypass, replica_groups=[list(range(NCORE))],
                ins=[ag_w_in.opt()], outs=[ag_w_out.opt()]), 1)

            nc.sync.wait_ge(sC.h, sC.v)
            qX.start((
                mod2_d.rearrange("b (r j) -> b r j", r=8),
                ag_m_out.transpose([1, 0, 2])), 16)
            qX.wait_all(nc.sync)
            mtmp = pa("mtmp", [128, 2048], F32)
            macc = pa("macc", [128, 2048], F32)
            for ii, dst in enumerate((sab, ca1b, gab, sbb, cb1b, gbb)):
                for bb in range(4):
                    qX.start((
                        mtmp[:], mod2_d[bb, ii * 2048:(ii + 1) * 2048]
                        .unsqueeze(0).broadcast_to((128, 2048))), 16)
                    qX.wait_all(nc.vector)
                    if bb == 0:
                        VI(nc.vector.tensor_scalar(
                            macc[:], mtmp[:], bsel[:, 0:1], None, ALU.mult), 1)
                    else:
                        VI(nc.vector.scalar_tensor_tensor(
                            macc[:], mtmp[:], bsel[:, bb:bb + 1], macc[:],
                            ALU.mult, ALU.add), 1)
                    nc.sync.wait_ge(sV.h, sV.v)
                if ii in (1, 4):
                    VI(nc.vector.tensor_scalar(
                        macc[:], macc[:], 1.0, None, ALU.add), 1)
                if ii in (2, 5):
                    VI(nc.vector.tensor_scalar(
                        mtmp[:], macc[:], 0.0, 2.0, ALU.is_ge, ALU.mult), 1)
                    VI(nc.vector.tensor_scalar(
                        mtmp[:], mtmp[:], 1.0, None, ALU.subtract), 1)
                    VI(nc.vector.tensor_tensor(macc[:], macc[:], mtmp[:],
                                               ALU.mult), 1)
                    VI(nc.vector.tensor_scalar(
                        macc[:], macc[:], 0.25, None, ALU.max), 1)
                    VI(nc.vector.tensor_tensor(macc[:], macc[:], mtmp[:],
                                               ALU.mult), 1)
                VI(nc.vector.tensor_copy(dst[:], macc[:]), 1)
                nc.sync.wait_ge(sV.h, sV.v)
            drain_dmas()

        def phase1():
          with contextlib.ExitStack() as ps:
            def pa(n, s, d):
                return ps.enter_context(nc.sbuf_tensor(n, s, d))
            h1T = pa("h1T", [128, NKD * HT], BF16)
            kceps5 = pa("kceps5", [128, 1], F32)
            kceps6 = pa("kceps6", [128, 1], F32)
            nc.gpsimd.memset(kceps5[:], 1e-5)
            GI(nc.gpsimd.memset(kceps6[:], 1e-6), 1)
            nc.vector.wait_ge(sG.h, sG.v)
            nc.scalar.wait_ge(sG.h, sG.v)

            h1T3 = h1T[:].rearrange("p (k c) -> p k c", k=NKD)
            tr_done = {}
            ln_es = contextlib.ExitStack()
            xt = ln_es.enter_context(nc.sbuf_tensor("xt", [128, 2 * 2048], F32))
            sqt = ln_es.enter_context(nc.sbuf_tensor("sqt", [128, 2048], F32))
            stat = ln_es.enter_context(nc.sbuf_tensor("stat", [128, 8], F32))
            h1a = ln_es.enter_context(nc.sbuf_tensor("h1a", [128, 2048], F32))
            h1b = ln_es.enter_context(nc.sbuf_tensor("h1b", [128, 2 * 2048], BF16))
            xh = ln_es.enter_context(nc.sbuf_tensor("xh", [16, 2048], F32))

            def ln_tile(x_ap, np_, col0, slot, idx):
                hb = h1b[:, slot * 2048:(slot + 1) * 2048]
                VI(nc.vector.tensor_reduce(
                    stat[0:np_, 0:1], x_ap, AX.X, ALU.add), 1)
                SI(nc.scalar.activation(
                    sqt[0:np_, :], x_ap, AF.Square,
                    accum_out=stat[0:np_, 2:3]), 1)
                nc.vector.wait_ge(sS.h, sS.v)
                VI(nc.vector.tensor_scalar_mul(
                    stat[0:np_, 1:2], stat[0:np_, 0:1], 1.0 / 2048.0), 1)
                VI(nc.vector.tensor_scalar_mul(
                    stat[0:np_, 3:4], stat[0:np_, 2:3], 1.0 / 2048.0), 1)
                VI(nc.vector.tensor_tensor(
                    stat[0:np_, 4:5], stat[0:np_, 1:2], stat[0:np_, 1:2],
                    ALU.mult), 1)
                VI(nc.vector.tensor_tensor(
                    stat[0:np_, 3:4], stat[0:np_, 3:4], stat[0:np_, 4:5],
                    ALU.subtract), 1)
                nc.scalar.wait_ge(sV.h, sV.v)
                SI(nc.scalar.activation(
                    stat[0:np_, 5:6], stat[0:np_, 3:4], AF.Sqrt,
                    bias=kceps5[0:np_, :]), 1)
                nc.vector.wait_ge(sS.h, sS.v)
                VI(nc.vector.reciprocal(
                    stat[0:np_, 6:7], stat[0:np_, 5:6]), 1)
                VI(nc.vector.tensor_scalar(
                    h1a[0:np_, :], x_ap, stat[0:np_, 1:2], stat[0:np_, 6:7],
                    ALU.subtract, ALU.mult), 1)
                VI(nc.vector.tensor_tensor(
                    h1a[0:np_, :], h1a[0:np_, :], ca1b[0:np_, :], ALU.mult), 1)
                if idx - 2 in tr_done:
                    nc.vector.wait_ge(sT.h, tr_done[idx - 2])
                VI(nc.vector.tensor_tensor(
                    hb[0:np_, :], h1a[0:np_, :], sab[0:np_, :], ALU.add), 1)
                nc.tensor.wait_ge(sV.h, sV.v)
                kpart = max(np_, 16)
                for g4 in range(4):
                    bk = next_bank(0)
                    bank_wait(bk)
                    for j in range(4):
                        kk = g4 * 4 + j
                        mm = nc.tensor.matmul(
                            pb[bk][:, j * 128:j * 128 + np_],
                            hb[0:kpart, kk * 128:(kk + 1) * 128],
                            I_bf[0:kpart, 0:np_],
                            start=True, stop=True)
                    sT.inc(mm, 1)
                    nc.scalar.wait_ge(sT.h, sT.v)
                    SI(nc.scalar.copy(
                        h1T3[:, g4 * 4:(g4 + 1) * 4, col0:col0 + np_],
                        pb[bk][:, :].rearrange("p (j c) -> p j c", j=4)
                        [:, :, 0:np_]), 1)
                    bank_free[bk] = (sS, sS.v)
                tr_done[idx] = sT.v

            qX.start((xh[0:3, 0:2048], x_d[0:3, :]), 16)
            qX.wait_all(nc.vector)
            qX.wait_all(nc.scalar)
            VI(nc.vector.memset(h1b[0:16, 0:2048], 0.0), 1)
            ln_tile(xh[0:3, 0:2048], 3, 0, 0, 0)
            ln_v = {}
            for tt in range(16):
                xbuf = xt[:, (tt % 2) * 2048:(tt % 2) * 2048 + 2048]
                if tt >= 2:
                    nc.sync.wait_ge(sV.h, ln_v[tt - 2])
                qX.start((
                    xbuf, x_d[3 + tt * 128: 3 + (tt + 1) * 128, :]), 16)
                qX.wait_all(nc.vector)
                qX.wait_all(nc.scalar)
                ln_tile(xbuf, 128, 3 + tt * 128, (tt + 1) % 2, tt + 1)
                ln_v[tt] = sV.v
            drain_dmas()
            ln_es.close()

            qk_es = contextlib.ExitStack()
            def qa(n, s, d):
                return qk_es.enter_context(nc.sbuf_tensor(n, s, d))
            wimg = qa("wimg", [128, 2 * 2048], BF16)
            cwt = qa("cwt", [128, 2 * 4], F32)
            mp = qa("mp", [128, 2 * 2051], F32)
            acc = qa("acc", [128, 2 * 2048], BF16)
            silt = qa("silt", [128, 2 * 2048], BF16)
            sqb = qa("sqb", [128, 2048], BF16)
            rrow = qa("rrow", [16, 2048], F32)
            rrb = qa("rrb", [16, 2048], BF16)
            rb = qa("rb", [128, 2048], F32)
            stg = qa("stg", [128, 512], BF16)
            VI(nc.vector.memset(rrb[:], 0.0), 1)

            NBQ = [(0, 512), (512, 512), (1024, 512), (1536, 512), (2048, 3)]
            NBZ = [(3, 512), (515, 512), (1027, 512), (1539, 512)]
            wfree = {}
            mpfree = {}
            silu_s = {}
            spill_x = {}

            def stream_w(m, src_ap, conv_m):
                if m - 2 in wfree:
                    nc.gpsimd.wait_ge(sT.h, wfree[m - 2])
                if conv_m is not None and m - 2 in mpfree:
                    nc.gpsimd.wait_ge(sV.h, mpfree[m - 2])
                qW.start((
                    wimg[:, (m % 2) * 2048:(m % 2) * 2048 + 2048], src_ap), 16)
                if conv_m is not None:
                    qW.start((
                        cwt[:, (m % 2) * 4:(m % 2) * 4 + 4], convw_d[conv_m]), 16)
                qW.wait_all(nc.tensor)

            def mm_tile(m, nblocks, evac):
                for bi, (c0, cn) in enumerate(nblocks):
                    bk = next_bank(0)
                    bank_wait(bk)
                    for kk in range(NKD):
                        mm = nc.tensor.matmul(
                            pb[bk][:, 0:cn],
                            wimg[:, (m % 2) * 2048 + kk * 128:
                                 (m % 2) * 2048 + (kk + 1) * 128],
                            h1T3[:, kk, c0:c0 + cn],
                            start=(kk == 0), stop=(kk == NKD - 1))
                    sT.inc(mm, 1)
                    evac(bi, c0, cn, bk)
                wfree[m] = sT.v

            for m in range(48):
                stream_w(m, wqkv_d[m], m)
                if m - 2 in mpfree:
                    nc.tensor.wait_ge(sV.h, mpfree[m - 2])
                ms = mp[:, (m % 2) * 2051:(m % 2) * 2051 + 2051]

                def evac_q(bi, c0, cn, bk, ms=ms):
                    nc.scalar.wait_ge(sT.h, sT.v)
                    SI(nc.scalar.copy(ms[:, c0:c0 + cn], pb[bk][:, 0:cn]), 1)
                    bank_free[bk] = (sS, sS.v)
                mm_tile(m, NBQ, evac_q)
                nc.vector.wait_ge(sS.h, sS.v)
                VI(nc.vector.tensor_scalar(
                    ms[:, 0:3], ms[:, 0:3], halo_m[:, 0:1], None, ALU.mult), 1)
                a0 = acc[:, (m % 2) * 2048:(m % 2) * 2048 + 2048]
                a1 = acc[:, ((m + 1) % 2) * 2048:((m + 1) % 2) * 2048 + 2048]
                cw = cwt[:, (m % 2) * 4:(m % 2) * 4 + 4]
                if m - 1 in silu_s:
                    nc.vector.wait_ge(sS.h, silu_s[m - 1])
                VI(nc.vector.tensor_scalar(
                    a0, ms[:, 0:2048], cw[:, 0:1], None, ALU.mult), 1)
                VI(nc.vector.scalar_tensor_tensor(
                    a1, ms[:, 1:2049], cw[:, 1:2], a0, ALU.mult, ALU.add), 1)
                VI(nc.vector.scalar_tensor_tensor(
                    a0, ms[:, 2:2050], cw[:, 2:3], a1, ALU.mult, ALU.add), 1)
                VI(nc.vector.scalar_tensor_tensor(
                    a1, ms[:, 3:2051], cw[:, 3:4], a0, ALU.mult, ALU.add), 1)
                mpfree[m] = sV.v
                sl = silt[:, (m % 2) * 2048:(m % 2) * 2048 + 2048]
                nc.scalar.wait_ge(sV.h, sV.v)
                if m - 2 in spill_x:
                    qX.wait(nc.scalar, spill_x[m - 2])
                silu_s[m] = SI(nc.scalar.activation(sl, a1, AF.Silu), 1)
                if m < 32:
                    SI(nc.scalar.activation(sqb[:], sl, AF.Square), 1)
                    nc.tensor.wait_ge(sS.h, sS.v)
                    for nb in range(4):
                        bk = next_bank(1)
                        bank_wait(bk)
                        mm = nc.tensor.matmul(
                            pb[bk][0:32, 0:512], ones_b[:],
                            sqb[:, nb * 512:(nb + 1) * 512],
                            start=True, stop=True)
                        sT.inc(mm, 1)
                        nc.vector.wait_ge(sT.h, sT.v)
                        VI(nc.vector.tensor_copy(
                            rrow[0:1, nb * 512:(nb + 1) * 512],
                            pb[bk][0:1, 0:512]), 1)
                        bank_free[bk] = (sV, sV.v)
                    nc.scalar.wait_ge(sV.h, sV.v)
                    SI(nc.scalar.activation(
                        rrow[0:1, :], rrow[0:1, :], AF.Sqrt,
                        bias=kceps6[0:1, :]), 1)
                    nc.vector.wait_ge(sS.h, sS.v)
                    VI(nc.vector.reciprocal(rrow[0:1, :], rrow[0:1, :]), 1)
                    if m < 16:
                        VI(nc.vector.tensor_scalar_mul(
                            rrow[0:1, :], rrow[0:1, :], float(128 ** -0.5)), 1)
                    VI(nc.vector.tensor_copy(rrb[0:1, :], rrow[0:1, :]), 1)
                    nc.tensor.wait_ge(sV.h, sV.v)
                    for nb in range(4):
                        bk = next_bank(1)
                        bank_wait(bk)
                        mm = nc.tensor.matmul(
                            pb[bk][:, 0:512], ones16[:],
                            rrb[:, nb * 512:(nb + 1) * 512], start=True, stop=True)
                        sT.inc(mm, 1)
                        nc.scalar.wait_ge(sT.h, sT.v)
                        SI(nc.scalar.copy(
                            rb[:, nb * 512:(nb + 1) * 512], pb[bk][:, 0:512]), 1)
                        bank_free[bk] = (sS, sS.v)
                    nc.vector.wait_ge(sS.h, sS.v)
                    VI(nc.vector.tensor_tensor(sl, sl, rb[:], ALU.mult), 1)
                    last = (sV, sV.v)
                else:
                    last = (sS, silu_s[m])
                dst = (q_sp if m < 16 else k_sp if m < 32 else v_sp)
                hh = m % 16
                nc.sync.wait_ge(last[0].h, last[1])
                spill_x[m] = qX.start((
                    dst[:, hh].transpose([1, 0, 2]), sl))
            drain_dmas()

            zspill = {}
            for m in range(16):
                stream_w(m, wz_d[m], None)

                def evac_z(bi, c0, cn, bk):
                    nc.scalar.wait_ge(sT.h, sT.v)
                    SI(nc.scalar.activation(
                        silt[:, bi * 512:(bi + 1) * 512], pb[bk][:, 0:cn],
                        AF.Silu), 1)
                    bank_free[bk] = (sS, sS.v)
                mm_tile(m, NBZ, evac_z)
                nc.tensor.wait_ge(sS.h, sS.v)
                for c4 in range(4):
                    bk = next_bank(1)
                    bank_wait(bk)
                    for j in range(4):
                        cci = c4 * 4 + j
                        mm = nc.tensor.matmul(
                            pb[bk][:, j * 128:(j + 1) * 128],
                            silt[:, cci * 128:(cci + 1) * 128],
                            I_bf[:], start=True, stop=True)
                    sT.inc(mm, 1)
                    nc.vector.wait_ge(sT.h, sT.v)
                    if (m, c4) != (0, 0):
                        qX.wait_all(nc.vector)
                    VI(nc.vector.tensor_copy(stg[:], pb[bk][:, :]), 1)
                    bank_free[bk] = (sV, sV.v)
                    nc.sync.wait_ge(sV.h, sV.v)
                    qX.start((
                        z_sp[c4 * 4:(c4 + 1) * 4, :, m * 128:(m + 1) * 128]
                        .transpose([1, 0, 2]), stg[:]), 16)
            drain_dmas()
            qk_es.close()

            wbal = pa("wbal", [128, 512], BF16)
            bag = pa("bag", [16, 2048], F32)
            aag = pa("aag", [16, 2048], F32)
            qW.start((wbal[:], wba_d), 16)
            qW.wait_all(nc.tensor)
            for part, out in ((0, bag), (16, aag)):
                for bi, (c0, cn) in enumerate(NBZ):
                    bk = next_bank(0)
                    bank_wait(bk)
                    for kk in range(NKD):
                        mm = nc.tensor.matmul(
                            pb[bk][0:16, 0:cn],
                            wbal[:, kk * 32 + part:kk * 32 + part + 16],
                            h1T3[:, kk, c0:c0 + cn],
                            start=(kk == 0), stop=(kk == NKD - 1))
                    sT.inc(mm, 1)
                    nc.scalar.wait_ge(sT.h, sT.v)
                    SI(nc.scalar.copy(out[:, bi * 512:(bi + 1) * 512],
                                          pb[bk][0:16, 0:cn]), 1)
                    bank_free[bk] = (sS, sS.v)
            SI(nc.scalar.activation(beta_r[:], bag[:], AF.Sigmoid), 1)
            SI(nc.scalar.activation(g_r[:], aag[:], AF.Exp, bias=dtb[:]), 1)
            SI(nc.scalar.activation(g_r[:], g_r[:], AF.Ln,
                                        bias=kc1[0:16, :]), 1)
            nc.vector.wait_ge(sS.h, sS.v)
            VI(nc.vector.tensor_scalar(
                g_r[:], g_r[:], negea[:], None, ALU.mult), 1)
            ga_, gb_ = g_r, gc_r
            for st_ in range(7):
                hh = 1 << st_
                va = ga_[:].rearrange("p (c f) -> p c f", c=CH)
                vb = gb_[:].rearrange("p (c f) -> p c f", c=CH)
                VI(nc.vector.tensor_copy(vb[:, :, 0:hh], va[:, :, 0:hh]), 1)
                VI(nc.vector.tensor_tensor(
                    vb[:, :, hh:128], va[:, :, hh:128], va[:, :, 0:128 - hh],
                    ALU.add), 1)
                ga_, gb_ = gb_, ga_
            nc.sync.wait_ge(sV.h, sV.v)
            qX.start((gcum_d, gc_r[:]), 16)
            qX.start((beta_d, beta_r[:]), 16)
            p_hi = pa("p_hi", [16, 2048], BF16)
            p_f = pa("p_f", [16, 2048], F32)
            p_lo = pa("p_lo", [16, 2048], BF16)
            p_r = pa("p_r", [16, 2048], F32)
            p_l2 = pa("p_l2", [16, 2048], BF16)
            bt_b = pa("bt_b", [16, 2048], BF16)
            VI(nc.vector.tensor_copy(p_hi[:], gc_r[:]), 1)
            VI(nc.vector.tensor_copy(p_f[:], p_hi[:]), 1)
            VI(nc.vector.tensor_tensor(p_r[:], gc_r[:], p_f[:], ALU.subtract), 1)
            VI(nc.vector.tensor_copy(p_lo[:], p_r[:]), 1)
            VI(nc.vector.tensor_copy(p_f[:], p_lo[:]), 1)
            VI(nc.vector.tensor_tensor(p_r[:], p_r[:], p_f[:], ALU.subtract), 1)
            VI(nc.vector.tensor_copy(p_l2[:], p_r[:]), 1)
            VI(nc.vector.tensor_copy(bt_b[:], beta_r[:]), 1)
            nc.tensor.wait_ge(sV.h, sV.v)
            for c in range(CH):
                bk = next_bank(0)
                bank_wait(bk)
                cs = slice(c * 128, (c + 1) * 128)
                nc.tensor.matmul(pb[bk][:, 0:16], p_hi[:, cs], I_bf[0:16, 0:16],
                                 start=True, stop=False)
                nc.tensor.matmul(pb[bk][:, 0:16], p_lo[:, cs], I_bf[0:16, 0:16],
                                 start=False, stop=False)
                nc.tensor.matmul(pb[bk][:, 0:16], p_l2[:, cs], I_bf[0:16, 0:16],
                                 start=False, stop=True)
                mm = nc.tensor.matmul(pb[bk][:, 16:32], bt_b[:, cs],
                                      I_bf[0:16, 0:16], start=True, stop=True)
                sT.inc(mm, 1)
                nc.scalar.wait_ge(sT.h, sT.v)
                SI(nc.scalar.copy(gcT_r[:, c * 16:(c + 1) * 16],
                                      pb[bk][:, 0:16]), 1)
                SI(nc.scalar.copy(btT_r[:, c * 16:(c + 1) * 16],
                                      pb[bk][:, 16:32]), 1)
                bank_free[bk] = (sS, sS.v)
            drain_dmas()


        def phase2():
          with contextlib.ExitStack() as ps:
            def pa(n, s, d):
                return ps.enter_context(nc.sbuf_tensor(n, s, d))
            qc = pa("qc", [128, 2048], BF16)
            kc = pa("kc", [128, 2048], BF16)
            vc = pa("vc", [128, 2048], BF16)
            ktil = pa("ktil", [128, 2048], BF16)
            qbar = pa("qbar", [128, 2048], BF16)
            aprow = pa("aprow", [128, 2048], BF16)
            e1 = pa("e1", [128, 2048], BF16)
            e2 = pa("e2", [128, 2048], BF16)
            e3 = pa("e3", [128, 2048], BF16)
            pre_all = pa("pre_all", [128, 2048], BF16)
            Amat = pa("Amat", [128, 2048], BF16)
            Atmat = pa("Atmat", [128, 2048], BF16)
            Ptmat = pa("Ptmat", [128, 2048], BF16)
            vT = pa("vT", [128, 2048], BF16)
            kT = pa("kT", [128, 2048], BF16)
            khat = pa("khat", [128, 2048], BF16)
            w2tn = pa("w2tn", [128, 2048], BF16)
            U_all = pa("U_all", [128, 2048], BF16)
            S_f = pa("S_f", [128, 2048], F32)
            S_b = pa("S_b", [128, 2048], BF16)
            Yb = pa("Yb", [128, 4096], BF16)
            rowb = pa("rowb", [128, 2048], F32)
            browb = pa("browb", [128, 2048], F32)
            art = pa("art", [128, 2048], F32)
            art2 = pa("art2", [128, 2048], F32)
            sq16 = pa("sq16", [128, 16], F32)
            akm = pa("akm", [128, 16], F32)
            Gbe = pa("Gbe", [128, 16], F32)
            apT = pa("apT", [128, 16], F32)
            bapT = pa("bapT", [128, 16], F32)
            obuf = pa("obuf", [128, 2048], BF16)
            onb = pa("onb", [128, 2048], BF16)
            zc = pa("zc", [128, 2048], BF16)
            stg2 = pa("stg2", [128, 512], BF16)
            sqs = pa("sqs", [128, 512], BF16)
            ep6 = pa("ep6", [128, 1], F32)
            GI(nc.gpsimd.memset(ep6[:], 1e-6), 1)
            VI(nc.vector.memset(S_f[:], 0.0), 1)
            VI(nc.vector.memset(S_b[:], 0.0), 1)
            nc.gpsimd.wait_ge(sG.h, sG.v)

            Y3 = Yb[:].rearrange("p (h u) -> p h u", h=H)
            qc3 = qc[:].rearrange("p (h t) -> p h t", h=H)
            kc3 = kc[:].rearrange("p (h t) -> p h t", h=H)
            art3 = art[:].rearrange("p (h t) -> p h t", h=H)
            rowb3 = rowb[:].rearrange("p (h t) -> p h t", h=H)

            def bc16(x):
                return x.unsqueeze(2).broadcast_to((128, 16, 128))

            def m128(x):
                return x.unsqueeze(1).broadcast_to((128, 16, 128))

            def soft_sync():
                nc.sync.wait_ge(sT.h, sT.v)
                nc.sync.wait_ge(sV.h, sV.v)
                nc.sync.wait_ge(sS.h, sS.v)
                nc.vector.wait_ge(sT.h, sT.v)
                nc.scalar.wait_ge(sT.h, sT.v)
                qP.wait_all(nc.vector)
                qP.wait_all(nc.scalar)
                qX.wait_all(nc.vector)
                qX.wait_all(nc.scalar)

            def chunk_pre(c):
                qX.start((qc[:], q_sp[c].transpose([1, 0, 2])))
                qX.start((kc[:], k_sp[c].transpose([1, 0, 2])))
                qX.start((vc[:], v_sp[c].transpose([1, 0, 2])))
                qX.start((rowb[:],
                          gcum_d[0:16, c * 128:(c + 1) * 128]
                          .unsqueeze(0).broadcast_to((128, 16, 128))))
                qX.start((browb[:],
                          beta_d[0:16, c * 128:(c + 1) * 128]
                          .unsqueeze(0).broadcast_to((128, 16, 128))))
                qX.wait_all(nc.vector)
                qX.wait_all(nc.scalar)
                gcs = gcT_r[:, c * 16:(c + 1) * 16]
                bts = btT_r[:, c * 16:(c + 1) * 16]
                VI(nc.vector.tensor_tensor(art3, rowb3, bc16(gcs), ALU.subtract), 1)
                SI(nc.scalar.activation(aprow[:], rowb[:], AF.Exp), 1)
                nc.scalar.wait_ge(sV.h, sV.v)
                SI(nc.scalar.activation(
                    akm[:].unsqueeze(2), art3[:, :, 127:128], AF.Exp), 1)
                SI(nc.scalar.activation(
                    Gbe[:].unsqueeze(2), rowb3[:, :, 127:128], AF.Exp), 1)
                SI(nc.scalar.activation(apT[:], gcs, AF.Exp), 1)
                a2v = art2[:].rearrange("p (h t) -> p h t", h=H)
                for msk, off, scl, dst in ((mL, oL, -1.0, e1),
                                           (mU, oU, 1.0, e2),
                                           (mUi, oUi, 1.0, e3)):
                    nc.vector.wait_ge(sS.h, sS.v)
                    VI(nc.vector.tensor_tensor(a2v, art3, m128(msk[:]), ALU.mult), 1)
                    VI(nc.vector.tensor_tensor(a2v, a2v, m128(off[:]), ALU.add), 1)
                    nc.scalar.wait_ge(sV.h, sV.v)
                    SI(nc.scalar.activation(dst[:], art2[:], AF.Exp, scale=scl), 1)
                nc.vector.wait_ge(sS.h, sS.v)
                VI(nc.vector.tensor_tensor(ktil[:], kc[:], browb[:], ALU.mult), 1)
                VI(nc.vector.tensor_tensor(qbar[:], qc[:], aprow[:], ALU.mult), 1)
                VI(nc.vector.tensor_tensor(bapT[:], bts, apT[:], ALU.mult), 1)
                nc.tensor.wait_ge(sV.h, sV.v)
                for lhs, rhs, emat, dst in ((ktil, kc, e1, Amat),
                                            (kc, ktil, e2, Atmat),
                                            (kc, qc, e3, Ptmat)):
                    nc.scalar.wait_ge(sV.h, sV.v)
                    for hg in range(4):
                        bk = next_bank(0)
                        bank_wait(bk)
                        for j in range(4):
                            hh = hg * 4 + j
                            mm = nc.tensor.matmul(
                                pb[bk][:, j * 128:(j + 1) * 128],
                                lhs[:, hh * 128:(hh + 1) * 128],
                                rhs[:, hh * 128:(hh + 1) * 128],
                                start=True, stop=True)
                        sT.inc(mm, 1)
                        nc.scalar.wait_ge(sT.h, sT.v)
                        SI(nc.scalar.copy(
                            pre_all[:, hg * 512:(hg + 1) * 512], pb[bk][:, :]), 1)
                        bank_free[bk] = (sS, sS.v)
                    nc.vector.wait_ge(sS.h, sS.v)
                    VI(nc.vector.tensor_tensor(dst[:], pre_all[:], emat[:],
                                               ALU.mult), 1)
                for srcb, dstb in ((vc, vT), (kc, kT)):
                    nc.tensor.wait_ge(sV.h, sV.v)
                    for hg in range(4):
                        bk = next_bank(0)
                        bank_wait(bk)
                        for j in range(4):
                            hh = hg * 4 + j
                            mm = nc.tensor.matmul(
                                pb[bk][:, j * 128:(j + 1) * 128],
                                srcb[:, hh * 128:(hh + 1) * 128],
                                I_bf[:], start=True, stop=True)
                        sT.inc(mm, 1)
                        nc.scalar.wait_ge(sT.h, sT.v)
                        SI(nc.scalar.copy(
                            dstb[:, hg * 512:(hg + 1) * 512], pb[bk][:, :]), 1)
                        bank_free[bk] = (sS, sS.v)
                vT3 = vT[:].rearrange("p (h t) -> p h t", h=H)
                kT3 = kT[:].rearrange("p (h t) -> p h t", h=H)
                nc.vector.wait_ge(sS.h, sS.v)
                VI(nc.vector.tensor_tensor(Y3[:, :, 0:128], vT3, bc16(bts),
                                           ALU.mult), 1)
                VI(nc.vector.tensor_tensor(Y3[:, :, 128:256], kT3, bc16(bapT[:]),
                                           ALU.mult), 1)
                VI(nc.vector.tensor_tensor(
                    khat[:].rearrange("p (h t) -> p h t", h=H), kT3,
                    bc16(akm[:]), ALU.mult), 1)
                nc.tensor.wait_ge(sV.h, sV.v)
                yev = [None] * 8
                aev = [0] * 8
                for pr in range(8):
                    bk = next_bank(pr % 2)
                    bank_wait(bk)
                    for u in range(2):
                        hh = pr * 2 + u
                        hs = slice(hh * 128, (hh + 1) * 128)
                        mm = nc.tensor.matmul(
                            pb[bk][:, u * 256:(u + 1) * 256], Atmat[:, hs],
                            Yb[:, hh * 256:(hh + 1) * 256],
                            start=True, stop=True)
                    sT.inc(mm, 1)
                    nc.vector.wait_ge(sT.h, sT.v)
                    VI(nc.vector.tensor_tensor(
                        Yb[:, pr * 512:(pr + 1) * 512],
                        Yb[:, pr * 512:(pr + 1) * 512],
                        pb[bk][:, 0:512], ALU.subtract), 1)
                    bank_free[bk] = (sV, sV.v)
                    yev[pr] = (sV.h, sV.v)
                for j in range(1, 7):
                    for pr in range(8):
                        bk = next_bank(pr % 2)
                        bank_wait(bk)
                        if j > 1:
                            nc.tensor.wait_ge(sS.h, aev[pr])
                        for u in range(2):
                            hh = pr * 2 + u
                            hs = slice(hh * 128, (hh + 1) * 128)
                            if j < 6:
                                nc.tensor.matmul(
                                    pb[bk][:, u * 256:u * 256 + 128],
                                    Atmat[:, hs], Amat[:, hs],
                                    start=True, stop=True)
                            mm = nc.tensor.matmul(
                                pb[bk][:, u * 256 + 128:u * 256 + 256],
                                Amat[:, hs], Atmat[:, hs],
                                start=True, stop=True)
                        sT.inc(mm, 1)
                        nc.scalar.wait_ge(sT.h, sT.v)
                        for u in range(2):
                            hh = pr * 2 + u
                            hs = slice(hh * 128, (hh + 1) * 128)
                            if j < 6:
                                SI(nc.scalar.copy(
                                    Amat[:, hs],
                                    pb[bk][:, u * 256:u * 256 + 128]), 1)
                            SI(nc.scalar.copy(
                                Atmat[:, hs],
                                pb[bk][:, u * 256 + 128:u * 256 + 256]), 1)
                        bank_free[bk] = (sS, sS.v)
                        aev[pr] = sS.v
                    for pr in range(8):
                        bk = next_bank(pr % 2)
                        bank_wait(bk)
                        nc.tensor.wait_ge(sS.h, aev[pr])
                        nc.tensor.wait_ge(*yev[pr])
                        for u in range(2):
                            hh = pr * 2 + u
                            hs = slice(hh * 128, (hh + 1) * 128)
                            mm = nc.tensor.matmul(
                                pb[bk][:, u * 256:(u + 1) * 256], Atmat[:, hs],
                                Yb[:, hh * 256:(hh + 1) * 256],
                                start=True, stop=True)
                        sT.inc(mm, 1)
                        nc.vector.wait_ge(sT.h, sT.v)
                        VI(nc.vector.tensor_tensor(
                            Yb[:, pr * 512:(pr + 1) * 512],
                            Yb[:, pr * 512:(pr + 1) * 512],
                            pb[bk][:, 0:512], ALU.add), 1)
                        bank_free[bk] = (sV, sV.v)
                        yev[pr] = (sV.h, sV.v)
                nc.tensor.wait_ge(sS.h, sS.v)
                nc.tensor.wait_ge(sV.h, sV.v)
                for hg in range(4):
                    bk = next_bank(0)
                    bank_wait(bk)
                    for j in range(4):
                        hh = hg * 4 + j
                        mm = nc.tensor.matmul(
                            pb[bk][:, j * 128:(j + 1) * 128],
                            Yb[:, hh * 256 + 128:(hh + 1) * 256],
                            I_bf[:], start=True, stop=True)
                    sT.inc(mm, 1)
                    nc.scalar.wait_ge(sT.h, sT.v)
                    SI(nc.scalar.activation(
                        w2tn[:, hg * 512:(hg + 1) * 512], pb[bk][:, :],
                        AF.Copy, scale=-1.0), 1)
                    bank_free[bk] = (sS, sS.v)

            def seq_step(c, with_o):
                nc.tensor.wait_ge(sS.h, sS.v)
                nc.tensor.wait_ge(sV.h, sV.v)
                uev = [0] * 4
                for qq in range(4):
                    bk = next_bank(1)
                    bank_wait(bk)
                    for u in range(4):
                        hh = qq * 4 + u
                        hs = slice(hh * 128, (hh + 1) * 128)
                        mm = nc.tensor.matmul(
                            pb[bk][:, u * 128:(u + 1) * 128], w2tn[:, hs],
                            S_b[:, hs], start=True, stop=True)
                    sT.inc(mm, 1)
                    nc.vector.wait_ge(sT.h, sT.v)
                    VI(nc.vector.tensor_tensor(
                        U_all[:, qq * 512:(qq + 1) * 512]
                        .rearrange("p (q t) -> p q t", q=4),
                        Y3[:, qq * 4:(qq + 1) * 4, 0:128],
                        pb[bk][:, 0:512].rearrange("p (q t) -> p q t", q=4),
                        ALU.add), 1)
                    bank_free[bk] = (sV, sV.v)
                    uev[qq] = sV.v
                if with_o:
                    for qq in range(4):
                        bk = next_bank(0)
                        bank_wait(bk)
                        nc.tensor.wait_ge(sV.h, uev[qq])
                        for u in range(4):
                            hh = qq * 4 + u
                            hs = slice(hh * 128, (hh + 1) * 128)
                            nc.tensor.matmul(
                                pb[bk][:, u * 128:(u + 1) * 128], qbar[:, hs],
                                S_b[:, hs], start=True, stop=False)
                            mm = nc.tensor.matmul(
                                pb[bk][:, u * 128:(u + 1) * 128], Ptmat[:, hs],
                                U_all[:, hs], start=False, stop=True)
                        sT.inc(mm, 1)
                        nc.scalar.wait_ge(sT.h, sT.v)
                        SI(nc.scalar.copy(obuf[:, qq * 512:(qq + 1) * 512],
                                          pb[bk][:, 0:512]), 1)
                        bank_free[bk] = (sS, sS.v)
                for qq in range(4):
                    bk = next_bank(1)
                    bank_wait(bk)
                    nc.tensor.wait_ge(sV.h, uev[qq])
                    for u in range(4):
                        hh = qq * 4 + u
                        hs = slice(hh * 128, (hh + 1) * 128)
                        mm = nc.tensor.matmul(
                            pb[bk][:, u * 128:(u + 1) * 128], khat[:, hs],
                            U_all[:, hs], start=True, stop=True)
                    sT.inc(mm, 1)
                    nc.vector.wait_ge(sT.h, sT.v)
                    for u in range(4):
                        hh = qq * 4 + u
                        hs = slice(hh * 128, (hh + 1) * 128)
                        VI(nc.vector.scalar_tensor_tensor(
                            S_f[:, hs], S_f[:, hs], Gbe[:, hh:hh + 1],
                            pb[bk][:, u * 128:(u + 1) * 128],
                            ALU.mult, ALU.add), 1)
                    VI(nc.vector.tensor_copy(
                        S_b[:, qq * 512:(qq + 1) * 512],
                        S_f[:, qq * 512:(qq + 1) * 512]), 1)
                    bank_free[bk] = (sV, sV.v)

            _sid2a, _ = nc.enter_named_scope("p2a", False)
            for c in range(CH):
                soft_sync()
                chunk_pre(c)
                nc.scalar.wait_ge(sV.h, sV.v)
                qP.start((op_sp[c, 0].transpose([1, 0, 2]), Y3[:, :, 0:128]))
                qP.start((op_sp[c, 1].transpose([1, 0, 2]), Y3[:, :, 128:256]))
                qP.start((op_sp[c, 2].transpose([1, 0, 2]),
                          w2tn[:].rearrange("p (h t) -> p h t", h=H)))
                qP.start((op_sp[c, 3].transpose([1, 0, 2]),
                          khat[:].rearrange("p (h t) -> p h t", h=H)))
                qP.start((op_sp[c, 4].transpose([1, 0, 2]),
                          Ptmat[:].rearrange("p (h t) -> p h t", h=H)))
                qP.start((op_sp[c, 5].transpose([1, 0, 2]),
                          qbar[:].rearrange("p (h t) -> p h t", h=H)))
                seq_step(c, with_o=False)
            nc.leave_named_scope("p2a", _sid2a, False)
            _sid2g, _ = nc.enter_named_scope("p2ag", False)
            soft_sync()
            qX.start((ag_s_in, S_f[:]))
            qX.wait_all(nc.gpsimd)
            sC.inc(nc.gpsimd.collective_compute(
                "AllGather", ALU.bypass, replica_groups=[list(range(NCORE))],
                ins=[ag_s_in.opt()], outs=[ag_s_out.opt()]), 1)
            nc.sync.wait_ge(sC.h, sC.v)
            nc.vector.wait_ge(sC.h, sC.v)
            qX.wait_all(nc.vector)
            VI(nc.vector.memset(S_f[:], 0.0), 1)
            for j in range(4):
                qX.start((rowb[:], ag_s_out[2 * j]))
                qX.wait_all(nc.vector)
                VI(nc.vector.scalar_tensor_tensor(
                    S_f[:], rowb[:], sel4[:, j:j + 1], S_f[:],
                    ALU.mult, ALU.add), 1)
                nc.sync.wait_ge(sV.h, sV.v)
            VI(nc.vector.tensor_copy(S_b[:], S_f[:]), 1)

            nc.leave_named_scope("p2ag", _sid2g, False)
            _sid2b, _ = nc.enter_named_scope("p2b", False)
            for c in range(CH):
                soft_sync()
                qX.start((Y3[:, :, 0:128], op_sp[c, 0].transpose([1, 0, 2])))
                qX.start((Y3[:, :, 128:256], op_sp[c, 1].transpose([1, 0, 2])))
                qX.start((w2tn[:], op_sp[c, 2].transpose([1, 0, 2])))
                qX.start((khat[:], op_sp[c, 3].transpose([1, 0, 2])))
                qX.start((Ptmat[:], op_sp[c, 4].transpose([1, 0, 2])))
                qX.start((qbar[:], op_sp[c, 5].transpose([1, 0, 2])))
                qX.start((zc[:], z_sp[c]))
                qX.start((rowb[:],
                          gcum_d[0:16, c * 128:(c + 1) * 128]
                          .unsqueeze(0).broadcast_to((128, 16, 128))))
                qX.wait_all(nc.scalar)
                SI(nc.scalar.activation(
                    Gbe[:].unsqueeze(2), rowb3[:, :, 127:128], AF.Exp), 1)
                qX.wait_all(nc.tensor)
                qX.wait_all(nc.vector)
                seq_step(c, with_o=True)
                ob3 = obuf[:].rearrange("p (h t) -> p h t", h=H)
                on3 = onb[:].rearrange("p (h t) -> p h t", h=H)
                nc.vector.wait_ge(sS.h, sS.v)
                VI(nc.vector.tensor_tensor(obuf[:], obuf[:], zc[:], ALU.mult), 1)
                VI(nc.vector.tensor_tensor(art[:], obuf[:], obuf[:], ALU.mult), 1)
                VI(nc.vector.tensor_reduce(
                    sq16[:], art[:].rearrange("p (h t) -> p h t", h=H),
                    AX.X, ALU.add), 1)
                VI(nc.vector.tensor_scalar_mul(sq16[:], sq16[:], 1.0 / 128.0), 1)
                nc.scalar.wait_ge(sV.h, sV.v)
                SI(nc.scalar.activation(sq16[:], sq16[:], AF.Sqrt,
                                        bias=ep6[:]), 1)
                nc.vector.wait_ge(sS.h, sS.v)
                VI(nc.vector.reciprocal(sq16[:], sq16[:]), 1)
                VI(nc.vector.tensor_tensor(on3, ob3, bc16(sq16[:]), ALU.mult), 1)
                VI(nc.vector.tensor_tensor(
                    on3, on3, nwb[:].unsqueeze(1).broadcast_to((128, 16, 128)),
                    ALU.mult), 1)
                nc.tensor.wait_ge(sV.h, sV.v)
                for hg in range(4):
                    bk = next_bank(0)
                    bank_wait(bk)
                    for j in range(4):
                        hh = hg * 4 + j
                        mm = nc.tensor.matmul(
                            pb[bk][:, j * 128:(j + 1) * 128],
                            onb[:, hh * 128:(hh + 1) * 128],
                            I_bf[:], start=True, stop=True)
                    sT.inc(mm, 1)
                    nc.vector.wait_ge(sT.h, sT.v)
                    qX.wait_all(nc.vector)
                    VI(nc.vector.tensor_copy(stg2[:], pb[bk][:, :]), 1)
                    bank_free[bk] = (sV, sV.v)
                    nc.sync.wait_ge(sV.h, sV.v)
                    qX.start((ocm_sp[hg * 4:(hg + 1) * 4, :, c * 128:(c + 1) * 128]
                              .transpose([1, 0, 2]), stg2[:]))
            nc.leave_named_scope("p2b", _sid2b, False)
            drain_dmas()


        def phase3():
          with contextlib.ExitStack() as ps:
            def pa(n, s, d):
                return ps.enter_context(nc.sbuf_tensor(n, s, d))
            wo = pa("wo", [128, 16 * 2048], BF16)
            oc_l = pa("oc_l", [128, 2048], BF16)
            xres = pa("xres", [128, 2048], F32)
            x2t = pa("x2t", [128, 2048], F32)
            for kk in range(NKD):
                qW.start((wo[:, kk * 2048:(kk + 1) * 2048], wout_d[kk]))
            qW.wait_all(nc.tensor)
            for tt in range(16):
                qX.wait_all(nc.sync)
                tko = qX.start((oc_l[:], ocm_sp[:, :, tt * 128:(tt + 1) * 128]
                                .transpose([1, 0, 2])))
                txr = qX.start((xres[:], x_d[3 + tt * 128:3 + (tt + 1) * 128, :]))
                qX.wait(nc.tensor, tko)
                qX.wait(nc.vector, txr)
                for nb in range(4):
                    bk = next_bank(0)
                    bank_wait(bk)
                    for kk in range(NKD):
                        mm = nc.tensor.matmul(
                            pb[bk][:, 0:512],
                            oc_l[:, kk * 128:(kk + 1) * 128],
                            wo[:, kk * 2048 + nb * 512:kk * 2048 + nb * 512 + 512],
                            start=(kk == 0), stop=(kk == NKD - 1))
                    sT.inc(mm, 1)
                    nc.vector.wait_ge(sT.h, sT.v)
                    cs = slice(nb * 512, (nb + 1) * 512)
                    VI(nc.vector.tensor_tensor(
                        x2t[:, cs], pb[bk][:, 0:512], gab[:, cs], ALU.mult), 1)
                    bank_free[bk] = (sV, sV.v)
                    VI(nc.vector.tensor_tensor(
                        x2t[:, cs], x2t[:, cs], xres[:, cs], ALU.add), 1)
                nc.sync.wait_ge(sV.h, sV.v)
                qX.start((x2_sp[tt], x2t[:]))
            drain_dmas()

        def phase4():
          with contextlib.ExitStack() as ps:
            def pa(n, s, d):
                return ps.enter_context(nc.sbuf_tensor(n, s, d))
            ep5 = pa("ep5", [128, 1], F32)
            GI(nc.gpsimd.memset(ep5[:], 1e-5), 1)
            nc.vector.wait_ge(sG.h, sG.v)
            nc.scalar.wait_ge(sG.h, sG.v)
            N1 = float(1.0 / np.sqrt(8192.0))
            rs2_all = pa("rs2_all", [128, 16], F32)
            x1_es = contextlib.ExitStack()
            xq1T = x1_es.enter_context(
                nc.sbuf_tensor("xq1T", [128, NKD * 2048], BF16))
            xq1T3 = xq1T[:].rearrange("p (k c) -> p k c", k=NKD)

            _sc2, _ = nc.enter_named_scope("c2", False)
            c2_es = contextlib.ExitStack()
            def ca(n, s, d):
                return c2_es.enter_context(nc.sbuf_tensor(n, s, d))
            x2t = ca("x2l", [128, 2 * 2048], F32)
            sqt = ca("sqt4", [128, 2048], F32)
            stat = ca("stat4", [128, 8], F32)
            h2a = ca("h2a", [128, 2048], F32)
            h2b = ca("h2b", [128, 2 * 2048], BF16)
            i32t = ca("i32t", [128, 2048], I32)
            tr_done4 = {}
            ln_v4 = {}
            rs1tok = {}
            for tt in range(16):
                xbuf = x2t[:, (tt % 2) * 2048:(tt % 2) * 2048 + 2048]
                if tt >= 2:
                    nc.sync.wait_ge(sV.h, ln_v4[tt - 2])
                tx = qX.start((xbuf, x2_sp[tt]))
                qX.wait(nc.vector, tx)
                qX.wait(nc.scalar, tx)
                VI(nc.vector.tensor_reduce(stat[:, 0:1], xbuf, AX.X, ALU.add), 1)
                SI(nc.scalar.activation(sqt[:], xbuf, AF.Square,
                                        accum_out=stat[:, 2:3]), 1)
                nc.vector.wait_ge(sS.h, sS.v)
                VI(nc.vector.tensor_scalar_mul(stat[:, 1:2], stat[:, 0:1],
                                               1.0 / 2048.0), 1)
                VI(nc.vector.tensor_scalar_mul(stat[:, 3:4], stat[:, 2:3],
                                               1.0 / 2048.0), 1)
                VI(nc.vector.tensor_tensor(stat[:, 4:5], stat[:, 1:2],
                                           stat[:, 1:2], ALU.mult), 1)
                VI(nc.vector.tensor_tensor(stat[:, 3:4], stat[:, 3:4],
                                           stat[:, 4:5], ALU.subtract), 1)
                nc.scalar.wait_ge(sV.h, sV.v)
                SI(nc.scalar.activation(stat[:, 5:6], stat[:, 3:4], AF.Sqrt,
                                        bias=ep5[:]), 1)
                nc.vector.wait_ge(sS.h, sS.v)
                VI(nc.vector.reciprocal(stat[:, 6:7], stat[:, 5:6]), 1)
                VI(nc.vector.tensor_scalar(h2a[:], xbuf, stat[:, 1:2],
                                           stat[:, 6:7], ALU.subtract, ALU.mult), 1)
                VI(nc.vector.tensor_tensor(h2a[:], h2a[:], cb1b[:], ALU.mult), 1)
                VI(nc.vector.tensor_tensor(h2a[:], h2a[:], sbb[:], ALU.add), 1)
                VI(nc.vector.tensor_reduce(stat[:, 0:1], h2a[:], AX.X, ALU.max,
                                           apply_absolute_value=True), 1)
                VI(nc.vector.tensor_scalar(stat[:, 0:1], stat[:, 0:1], 1e-5,
                                           None, ALU.max), 1)
                VI(nc.vector.reciprocal(stat[:, 1:2], stat[:, 0:1]), 1)
                VI(nc.vector.tensor_scalar_mul(stat[:, 2:3], stat[:, 1:2],
                                               128.0), 1)
                VI(nc.vector.tensor_scalar_mul(stat[:, 3:4], stat[:, 0:1],
                                               1.0 / 128.0), 1)
                VI(nc.vector.tensor_copy(sc1_r[:, tt:tt + 1], stat[:, 3:4]), 1)
                hb = h2b[:, (tt % 2) * 2048:(tt % 2) * 2048 + 2048]
                if tt - 2 in tr_done4:
                    nc.vector.wait_ge(sT.h, tr_done4[tt - 2])
                VI(nc.vector.tensor_scalar(h2a[:], h2a[:], stat[:, 2:3],
                                           CLIP_HI, ALU.mult, ALU.min), 1)
                VI(nc.vector.tensor_scalar(h2a[:], h2a[:], CLIP_LO, None,
                                           ALU.max), 1)
                VI(nc.vector.tensor_scalar(i32t[:], h2a[:], MAGIC, MAGIC,
                                           ALU.add, ALU.subtract), 1)
                VI(nc.vector.tensor_copy(hb, i32t[:]), 1)
                nc.tensor.wait_ge(sV.h, sV.v)
                for g4 in range(4):
                    bk = next_bank(0)
                    bank_wait(bk)
                    for j in range(4):
                        kk = g4 * 4 + j
                        mm = nc.tensor.matmul(
                            pb[bk][:, j * 128:(j + 1) * 128],
                            hb[:, kk * 128:(kk + 1) * 128],
                            I_bf[:], start=True, stop=True)
                    sT.inc(mm, 1)
                    nc.scalar.wait_ge(sT.h, sT.v)
                    SI(nc.scalar.copy(
                        xq1T3[:, g4 * 4:(g4 + 1) * 4, tt * 128:(tt + 1) * 128],
                        pb[bk][:, :].rearrange("p (j c) -> p j c", j=4)), 1)
                    bank_free[bk] = (sS, sS.v)
                tr_done4[tt] = sT.v
                ln_v4[tt] = sV.v
            drain_dmas()
            c2_es.close()

            nc.leave_named_scope("c2", _sc2, False)
            _sc3, _ = nc.enter_named_scope("c3", False)
            c3_es = contextlib.ExitStack()
            def c3a(n, s, d):
                return c3_es.enter_context(nc.sbuf_tensor(n, s, d))
            wgt = c3a("wgt", [128, 2 * 2048], BF16)
            wut = c3a("wut", [128, 2 * 2048], BF16)
            rsbg = c3a("rsbg", [128, 2048], F32)
            rsbu = c3a("rsbu", [128, 2048], F32)
            gsil = c3a("gsil", [128, 2048], BF16)
            upt = c3a("upt", [128, 2048], BF16)
            intb = c3a("intb", [128, 2048], BF16)
            rs1b = c3a("rs1b", [128, 16], BF16)
            rrow16 = c3a("rrow16", [16, 128], BF16)
            VI(nc.vector.tensor_copy(rs1b[:], sc1_r[:]), 1)
            nc.tensor.wait_ge(sV.h, sV.v)
            bk = next_bank(0)
            bank_wait(bk)
            mm = nc.tensor.matmul(pb[bk][0:16, 0:128], rs1b[:], I_bf[:],
                                  start=True, stop=True)
            sT.inc(mm, 1)
            nc.vector.wait_ge(sT.h, sT.v)
            VI(nc.vector.tensor_copy(rrow16[:], pb[bk][0:16, 0:128]), 1)
            bank_free[bk] = (sV, sV.v)
            nc.tensor.wait_ge(sV.h, sV.v)
            for tt in range(16):
                bk = next_bank(0)
                bank_wait(bk)
                mm = nc.tensor.matmul(
                    pb[bk][:, 0:128], sel16b[:, tt * 128:(tt + 1) * 128],
                    rrow16[:], start=True, stop=True)
                sT.inc(mm, 1)
                nc.vector.wait_ge(sT.h, sT.v)
                VI(nc.vector.tensor_copy(rsbg[:, tt * 128:(tt + 1) * 128],
                                         pb[bk][:, 0:128]), 1)
                bank_free[bk] = (sV, sV.v)
            VI(nc.vector.tensor_scalar(rsbu[:], rsbg[:], sbars[:, 1:2],
                                       None, ALU.mult), 1)
            VI(nc.vector.tensor_scalar(rsbg[:], rsbg[:], sbars[:, 0:1],
                                       None, ALU.mult), 1)
            c3wt = [None, None]
            c3mm = {}
            sp_tok = [None]

            def c3_loads(m):
                toks = []
                for wi, soff in ((0, 0), (1, SLAB)):
                    buf = wgt if wi == 0 else wut
                    dst = buf[:, (m % 2) * 2048:(m % 2) * 2048 + 2048]
                    dst4 = dst.rearrange("p (r kl c) -> p r kl c", r=8, kl=2)
                    for kl in range(2):
                        src = ag_w_out[:, soff + m * 32768 + kl * 16384:
                                       soff + m * 32768 + (kl + 1) * 16384]
                        toks.append(qX.start(
                            (dst4[:, :, kl, :],
                             src.rearrange("r (pp c) -> pp r c", pp=128))))
                c3wt[m % 2] = toks

            c3_loads(0)
            for m in range(64):
                if m + 1 < 64:
                    if m >= 1:
                        nc.sync.wait_ge(sT.h, c3mm[m - 1])
                    c3_loads(m + 1)
                for tok in c3wt[m % 2]:
                    qX.wait(nc.tensor, tok)
                bkv = []
                for wi in range(2):
                    buf = wgt if wi == 0 else wut
                    for nb in range(4):
                        bk = wi * 4 + nb
                        bank_wait(bk)
                        for kk in range(NKD):
                            mm = nc.tensor.matmul(
                                pb[bk][:, 0:512],
                                buf[:, (m % 2) * 2048 + kk * 128:
                                    (m % 2) * 2048 + (kk + 1) * 128],
                                xq1T3[:, kk, nb * 512:(nb + 1) * 512],
                                start=(kk == 0), stop=(kk == NKD - 1))
                        sT.inc(mm, 1)
                        bkv.append(sT.v)
                c3mm[m] = sT.v
                for nb in range(4):
                    cs = slice(nb * 512, (nb + 1) * 512)
                    nc.vector.wait_ge(sT.h, bkv[nb])
                    VI(nc.vector.tensor_tensor(
                        gsil[:, cs], pb[nb][:, 0:512], rsbg[:, cs], ALU.mult), 1)
                    bank_free[nb] = (sV, sV.v)
                    nc.vector.wait_ge(sT.h, bkv[4 + nb])
                    VI(nc.vector.tensor_tensor(
                        upt[:, cs], pb[4 + nb][:, 0:512], rsbu[:, cs],
                        ALU.mult), 1)
                    bank_free[4 + nb] = (sV, sV.v)
                nc.scalar.wait_ge(sV.h, sV.v)
                SI(nc.scalar.activation(gsil[:], gsil[:], AF.Silu), 1)
                nc.vector.wait_ge(sS.h, sS.v)
                if sp_tok[0] is not None:
                    qX.wait(nc.vector, sp_tok[0])
                VI(nc.vector.tensor_tensor(intb[:], gsil[:], upt[:], ALU.mult), 1)
                nc.sync.wait_ge(sV.h, sV.v)
                sp_tok[0] = qX.start((inter_sp[:, m].transpose([1, 0, 2]),
                                      intb[:]))
            drain_dmas()
            c3_es.close()
            x1_es.close()

            nc.leave_named_scope("c3", _sc3, False)
            _sc4a, _ = nc.enter_named_scope("c4a", False)
            c4_es = contextlib.ExitStack()
            def c4a(n, s, d):
                return c4_es.enter_context(nc.sbuf_tensor(n, s, d))
            iz = c4a("iz", [128, 64 * 256], BF16)
            z1 = c4a("z1", [128, 64 * 256], BF16)
            fw = iz
            s2a = c4a("s2a", [128, 2 * 512], BF16)
            s2b = c4a("s2b", [128, 2 * 512], BF16)
            mxj = c4a("mxj", [128, 256], F32)
            mxjb = c4a("mxjb", [128, 256], BF16)
            mcol = c4a("mcol", [128, 2], F32)
            mcolb = c4a("mcolb", [128, 2], BF16)
            mrow = c4a("mrow", [16, 128], BF16)
            s2r = c4a("s2r", [128, 256], F32)
            xq2b = z1
            i32f = c4a("i32f", [128, 16 * 256], I32)
            fw32 = c4a("fw32", [128, 16 * 256], F32)
            VI(nc.vector.memset(mrow[:], 0.0), 1)
            for blk in range(8):
                qX.wait_all(nc.sync)
                ti = qX.start((iz[:], inter_sp[blk].transpose([1, 0, 2])))
                qX.wait(nc.tensor, ti)
                iz3 = iz[:].rearrange("p (i t) -> p i t", i=64)
                z13 = z1[:].rearrange("p (i t) -> p i t", i=64)
                fw3 = fw[:].rearrange("p (i t) -> p i t", i=64)
                z1g = z1[:].rearrange("p (g i t) -> p g i t", g=128, i=64)
                for ig in range(32):
                    bk = next_bank(0)
                    bank_wait(bk)
                    for j in range(2):
                        mm = nc.tensor.matmul(
                            pb[bk][:, j * 256:(j + 1) * 256],
                            H128[:], iz3[:, ig * 2 + j, :],
                            start=True, stop=True)
                    sT.inc(mm, 1)
                    nc.scalar.wait_ge(sT.h, sT.v)
                    SI(nc.scalar.copy(
                        z1g[:, :, ig * 2:ig * 2 + 2, :].transpose([0, 2, 1, 3]),
                        pb[bk][:, :].rearrange("p (i g t) -> p i g t",
                                               i=2, g=128)), 1)
                    bank_free[bk] = (sS, sS.v)
                nc.tensor.wait_ge(sS.h, sS.v)
                st1_ev = {}
                st2_ev = {}
                for step in range(34):
                    g1, g2, g3 = step, step - 1, step - 2
                    if g1 < 32:
                        sl = g1 % 2
                        bk = next_bank(0)
                        bank_wait(bk)
                        for j in range(4):
                            g = g1 * 4 + j
                            mm = nc.tensor.matmul(
                                pb[bk][:, j * 128:(j + 1) * 128],
                                z1[:, g * 128:(g + 1) * 128], I_bf[:],
                                start=True, stop=True)
                        sT.inc(mm, 1)
                        nc.scalar.wait_ge(sT.h, sT.v)
                        SI(nc.scalar.copy(s2a[:, sl * 512:(sl + 1) * 512],
                                          pb[bk][:, :]), 1)
                        st1_ev[g1] = sS.v
                        bank_free[bk] = (sS, sS.v)
                    if 0 <= g2 < 32:
                        sl = g2 % 2
                        bk2 = next_bank(1)
                        bank_wait(bk2)
                        nc.tensor.wait_ge(sS.h, st1_ev[g2])
                        for j in range(4):
                            mm = nc.tensor.matmul(
                                pb[bk2][:, j * 128:(j + 1) * 128], HXs[:],
                                s2a[:, sl * 512 + j * 128:
                                    sl * 512 + (j + 1) * 128],
                                start=True, stop=True)
                        sT.inc(mm, 1)
                        nc.scalar.wait_ge(sT.h, sT.v)
                        SI(nc.scalar.copy(s2b[:, sl * 512:(sl + 1) * 512],
                                          pb[bk2][:, :]), 1)
                        st2_ev[g2] = sS.v
                        bank_free[bk2] = (sS, sS.v)
                    if 0 <= g3:
                        sl = g3 % 2
                        bk3 = next_bank(0)
                        bank_wait(bk3)
                        nc.tensor.wait_ge(sS.h, st2_ev[g3])
                        for j in range(4):
                            mm = nc.tensor.matmul(
                                pb[bk3][:, j * 128:(j + 1) * 128],
                                s2b[:, sl * 512 + j * 128:
                                    sl * 512 + (j + 1) * 128], I_bf[:],
                                start=True, stop=True)
                        sT.inc(mm, 1)
                        nc.vector.wait_ge(sT.h, sT.v)
                        for j in range(4):
                            g = g3 * 4 + j
                            VI(nc.vector.tensor_copy(
                                fw3[:, :, 2 * g:2 * g + 2],
                                pb[bk3][:, j * 128:(j + 1) * 128]
                                .rearrange("p (i t) -> p i t", i=64)), 1)
                        bank_free[bk3] = (sV, sV.v)
                VI(nc.vector.tensor_reduce(
                    mxj[:], fw3.transpose([0, 2, 1]), AX.X, ALU.max,
                    apply_absolute_value=True), 1)
                VI(nc.vector.tensor_copy(mxjb[:], mxj[:]), 1)
                nc.tensor.wait_ge(sV.h, sV.v)
                for j in range(2):
                    bk = next_bank(0)
                    bank_wait(bk)
                    mm = nc.tensor.matmul(
                        pb[bk][:, 0:128], mxjb[:, j * 128:(j + 1) * 128],
                        I_bf[:], start=True, stop=True)
                    sT.inc(mm, 1)
                    nc.vector.wait_ge(sT.h, sT.v)
                    VI(nc.vector.tensor_reduce(
                        mcol[:, j:j + 1], pb[bk][:, 0:128], AX.X, ALU.max), 1)
                    bank_free[bk] = (sV, sV.v)
                VI(nc.vector.tensor_scalar(mcol[:], mcol[:], N1, 1e-5,
                                           ALU.mult, ALU.max), 1)
                VI(nc.vector.reciprocal(mcol[:], mcol[:]), 1)
                VI(nc.vector.tensor_scalar_mul(mcol[:], mcol[:], 128.0 * N1), 1)
                VI(nc.vector.tensor_copy(mcolb[:], mcol[:]), 1)
                VI(nc.vector.tensor_copy(mcol[:], mcolb[:]), 1)
                VI(nc.vector.reciprocal(s2r[:, 0:2], mcol[:]), 1)
                VI(nc.vector.tensor_scalar(s2r[:, 0:2], s2r[:, 0:2],
                                           sbars[:, 2:3], N1,
                                           ALU.mult, ALU.mult), 1)
                VI(nc.vector.tensor_copy(rs2_all[:, blk * 2:blk * 2 + 2],
                                         s2r[:, 0:2]), 1)
                nc.tensor.wait_ge(sV.h, sV.v)
                bk = next_bank(0)
                bank_wait(bk)
                mm = nc.tensor.matmul(pb[bk][0:2, 0:128], mcolb[:], I_bf[:],
                                      start=True, stop=True)
                sT.inc(mm, 1)
                nc.vector.wait_ge(sT.h, sT.v)
                VI(nc.vector.tensor_copy(mrow[0:2, :], pb[bk][0:2, 0:128]), 1)
                bank_free[bk] = (sV, sV.v)
                nc.tensor.wait_ge(sV.h, sV.v)
                bk = next_bank(0)
                bank_wait(bk)
                nc.tensor.matmul(pb[bk][:, 0:128],
                                 sel16b[:, 0:128], mrow[:, 0:128],
                                 start=True, stop=True)
                mm = nc.tensor.matmul(pb[bk][:, 128:256],
                                      sel16b[:, 128:256], mrow[:, 0:128],
                                      start=True, stop=True)
                sT.inc(mm, 1)
                nc.vector.wait_ge(sT.h, sT.v)
                VI(nc.vector.tensor_copy(s2r[:], pb[bk][:, 0:256]), 1)
                bank_free[bk] = (sV, sV.v)
                for qc_ in range(4):
                    i0 = qc_ * 16
                    VI(nc.vector.tensor_tensor(
                        fw32[:].rearrange("p (i t) -> p i t", i=16),
                        fw3[:, i0:i0 + 16, :],
                        s2r[:].unsqueeze(1).broadcast_to((128, 16, 256)),
                        ALU.mult), 1)
                    VI(nc.vector.tensor_scalar(fw32[:], fw32[:], CLIP_LO,
                                               CLIP_HI, ALU.max, ALU.min), 1)
                    VI(nc.vector.tensor_scalar(i32f[:], fw32[:], MAGIC, MAGIC,
                                               ALU.add, ALU.subtract), 1)
                    VI(nc.vector.tensor_copy(
                        xq2b[:, i0 * 256:(i0 + 16) * 256], i32f[:]), 1)
                nc.sync.wait_ge(sV.h, sV.v)
                for sb_ in range(2):
                    for iq in range(4):
                        qX.start((xq2_sp[2 * blk + sb_, iq * 16:(iq + 1) * 16]
                                  .transpose([1, 0, 2]),
                                  xq2b[:].rearrange("p (i t) -> p i t", i=64)
                                  [:, iq * 16:(iq + 1) * 16,
                                   sb_ * 128:(sb_ + 1) * 128]))
            drain_dmas()
            c4_es.close()

            nc.leave_named_scope("c4a", _sc4a, False)
            _sc4b, _ = nc.enter_named_scope("c4b", False)
            c5_es = contextlib.ExitStack()
            def c5a(n, s, d):
                return c5_es.enter_context(nc.sbuf_tensor(n, s, d))
            wdn = c5a("wdn", [128, 64 * 512], BF16)
            xqm = c5a("xqm", [128, 2 * 64 * 128], BF16)
            x2s = c5a("x2s", [128, 2 * 512], F32)
            yt = c5a("yt", [128, 2 * 512], F32)
            xq_tok = [None, None]
            x2_tok = [None, None]
            st_tok = [None, None]
            mt_mm = {}
            mt_vv = {}

            def c4b_loads(mt, nb):
                sl = mt % 2
                if mt >= 2:
                    nc.sync.wait_ge(sT.h, mt_mm[mt - 2])
                    nc.sync.wait_ge(sV.h, mt_vv[mt - 2])
                xqv3 = xqm[:, sl * 8192:sl * 8192 + 8192] \
                    .rearrange("p (i t) -> p i t", i=64)
                xq_tok[sl] = [qX.start(
                    (xqv3[:, iq * 16:(iq + 1) * 16, :],
                     xq2_sp[mt, iq * 16:(iq + 1) * 16].transpose([1, 0, 2])))
                    for iq in range(4)]
                x2_tok[sl] = qX.start(
                    (x2s[:, sl * 512:(sl + 1) * 512],
                     x2_sp[mt][:, nb * 512:(nb + 1) * 512]))

            for nb in range(4):
                nc.sync.wait_ge(sT.h, sT.v)
                wdt = []
                wdn4 = wdn[:].rearrange("p (k m c) -> p k m c", k=64, m=4)
                for r in range(8):
                    for mloc in range(4):
                        off = 2 * SLAB + (4 * nb + mloc) * 8 * 16384
                        srcw = ag_w_out[r, off:off + 8 * 16384]
                        wdt.append(qX.start(
                            (wdn4[:, r * 8:(r + 1) * 8, mloc, :],
                             srcw.rearrange("(kl pp c) -> pp kl c",
                                            kl=8, pp=128))))
                mt_mm.clear()
                mt_vv.clear()
                c4b_loads(0, nb)
                for t_ in wdt:
                    qX.wait(nc.tensor, t_)
                for mt in range(16):
                    sl = mt % 2
                    if mt + 1 < 16:
                        c4b_loads(mt + 1, nb)
                    for t_ in xq_tok[sl]:
                        qX.wait(nc.tensor, t_)
                    bk = next_bank(0)
                    bank_wait(bk)
                    for kk in range(64):
                        mm = nc.tensor.matmul(
                            pb[bk][:, 0:512],
                            xqm[:, sl * 8192 + kk * 128:
                                sl * 8192 + (kk + 1) * 128],
                            wdn[:, kk * 512:(kk + 1) * 512],
                            start=(kk == 0), stop=(kk == 63))
                    sT.inc(mm, 1)
                    mt_mm[mt] = sT.v
                    qX.wait(nc.vector, x2_tok[sl])
                    if st_tok[sl] is not None:
                        qX.wait(nc.vector, st_tok[sl])
                    nc.vector.wait_ge(sT.h, sT.v)
                    ys = yt[:, sl * 512:(sl + 1) * 512]
                    VI(nc.vector.tensor_scalar(
                        ys, pb[bk][:, 0:512], rs2_all[:, mt:mt + 1],
                        None, ALU.mult), 1)
                    bank_free[bk] = (sV, sV.v)
                    VI(nc.vector.tensor_tensor(
                        ys, ys, gbb[:, nb * 512:(nb + 1) * 512],
                        ALU.mult), 1)
                    VI(nc.vector.tensor_tensor(
                        ys, ys, x2s[:, sl * 512:(sl + 1) * 512], ALU.add), 1)
                    mt_vv[mt] = sV.v
                    nc.sync.wait_ge(sV.h, sV.v)
                    st_tok[sl] = qX.start(
                        (y_d[mt * 128:(mt + 1) * 128,
                             nb * 512:(nb + 1) * 512], ys))
            nc.leave_named_scope("c4b", _sc4b, False)
            drain_dmas()
            c5_es.close()

        def _scoped(name, fn):
            sid, _ = nc.enter_named_scope(name, False)
            fn()
            nc.leave_named_scope(name, sid, False)
        if 0 in phases:
            _scoped("p0", phase0)
        if 1 in phases:
            _scoped("p1", phase1)
        if 2 in phases:
            _scoped("p2", phase2)
        if 3 in phases:
            _scoped("p3", phase3)
        if 4 in phases:
            _scoped("p4", phase4)
        nc.all_engine_barrier()
    return nc


def kernel(**inputs):
    nc = build()
    maps = _host_inputs(inputs)
    res = run_bass_kernel_spmd(nc, maps, core_ids=list(range(NCORE)))
    out = np.empty((4, 4096, 2048), dtype=np.float32)
    for c in range(NCORE):
        b, s = c // 2, c % 2
        out[b, s * TOK:(s + 1) * TOK] = res.results[c]["y"]
    return out


# version 04
# speedup vs baseline: 1.1589x; 1.0689x over previous
import contextlib
import numpy as np
import concourse.bass as bass
import concourse.mybir as mybir
from concourse.bass_utils import run_bass_kernel_spmd

F32 = mybir.dt.float32
BF16 = mybir.dt.bfloat16
FP16 = mybir.dt.float16
I32 = mybir.dt.int32
AF = mybir.ActivationFunctionType
ALU = mybir.AluOpType
AX = mybir.AxisListType

MAGIC = 12582912.0
NCORE = 8
TOK = 2048
D = 2048
NKD = 16
H = 16
DK = DV = 128
CH = 16
TOKH = TOK + 3
HT = 2052
FFI = 8192
CLIP_LO = -128.49609375
CLIP_HI = 127.49609375
SLAB = 64 * 2 * 128 * 128


class SemC:
    def __init__(self, h):
        self.h = h
        self.v = 0

    def inc(self, instr, n=1):
        instr.then_inc(self.h, n)
        self.v += n
        return self.v


def _hadamard(n):
    h = np.array([[1.0]], dtype=np.float32)
    while h.shape[0] < n:
        h = np.block([[h, h], [h, -h]])
    return h


def _consts():
    i128 = np.eye(128, dtype=np.float32)
    h128 = _hadamard(128)
    h64 = _hadamard(64)
    hx = np.kron(h64, np.eye(2)).astype(np.float32)
    sel16 = np.zeros((16, 2048), dtype=np.float32)
    for hh in range(16):
        sel16[hh, hh * 128:(hh + 1) * 128] = 1.0
    tril_s = np.tril(np.ones((128, 128), dtype=np.float32), -1)
    triu_s = np.ascontiguousarray(tril_s.T)
    triu_i = np.triu(np.ones((128, 128), dtype=np.float32), 0)
    return {
        "c_I": i128, "c_H128": h128, "c_HX": hx,
        "c_mL": tril_s, "c_oL": (1.0 - tril_s) * 88.0,
        "c_mU": triu_s, "c_oU": (1.0 - triu_s) * (-88.0),
        "c_mUi": triu_i, "c_oUi": (1.0 - triu_i) * (-88.0),
        "c_sel16": sel16,
    }


def _host_inputs(inputs):
    x = np.ascontiguousarray(np.asarray(inputs["L0_attn_hidden_states"], dtype=np.float32))
    Wqkv = np.asarray(inputs["L0_attn_W_qkv"], dtype=np.float32)
    convw = np.asarray(inputs["L0_attn_conv_w"], dtype=np.float32)
    Wz = np.asarray(inputs["L0_attn_W_z"], dtype=np.float32)
    Wb = np.asarray(inputs["L0_attn_W_b"], dtype=np.float32)
    Wa = np.asarray(inputs["L0_attn_W_a"], dtype=np.float32)
    dtb = np.asarray(inputs["L0_attn_dt_bias"], dtype=np.float32)
    Alog = np.asarray(inputs["L0_attn_A_log"], dtype=np.float32)
    nw = np.asarray(inputs["L0_attn_norm_weight"], dtype=np.float32)
    Wout = np.asarray(inputs["L0_attn_W_out"], dtype=np.float32)
    wg = np.asarray(inputs["L0_ffn_w_gate"], dtype=np.float32)
    wu = np.asarray(inputs["L0_ffn_w_up"], dtype=np.float32)
    wd = np.asarray(inputs["L0_ffn_w_down"], dtype=np.float32)
    cc = np.asarray(inputs["L0_c"], dtype=np.float32)
    wmod = np.asarray(inputs["L0_wmod"], dtype=np.float32)

    def img(W, nm):
        nk = W.shape[0] // 128
        t = W.reshape(nk, 128, nm, 128)
        return np.ascontiguousarray(t.transpose(2, 1, 0, 3).reshape(nm, 128, nk * 128))

    wqkv_img = img(Wqkv, 48)
    wz_img = img(Wz, 16)
    Wba = np.concatenate([Wb, Wa], axis=1)
    wba_img = np.ascontiguousarray(
        Wba.reshape(16, 128, 32).transpose(1, 0, 2).reshape(128, 512))
    wout_img = np.ascontiguousarray(Wout.reshape(16, 128, 2048))
    convw_t = np.ascontiguousarray(convw.reshape(48, 128, 4))
    cT = np.ascontiguousarray(cc.T.reshape(16, 128, 4))
    wgT = np.ascontiguousarray(wg.T)
    wuT = np.ascontiguousarray(wu.T)
    wdT = np.ascontiguousarray(wd.T)

    def slab(WT, rows0, nkl, nm):
        s = WT[rows0:rows0 + nkl * 128]
        s = s.reshape(nkl, 128, nm, 128)
        return np.ascontiguousarray(s.transpose(2, 0, 1, 3))

    consts = _consts()
    maps = []
    for c in range(NCORE):
        b, s = c // 2, c % 2
        xs = np.zeros((TOKH, D), dtype=np.float32)
        t0 = s * TOK
        if t0 >= 3:
            xs[0:3] = x[b, t0 - 3:t0]
        xs[3:] = x[b, t0:t0 + TOK]
        halo = np.full((128, 1), 1.0 if s == 1 else 0.0, dtype=np.float32)
        sel = np.zeros((128, 4), dtype=np.float32)
        if s == 1:
            sel[:, (c - 1) // 2] = 1.0
        bsel = np.zeros((128, 4), dtype=np.float32)
        bsel[:, b] = 1.0
        m = {
            "x": xs, "halo_mask": halo, "sel4": sel, "bsel": bsel,
            "wqkv_img": wqkv_img, "wz_img": wz_img, "wba_img": wba_img,
            "wout_img": wout_img, "convw_t": convw_t, "cT": cT,
            "wmod_t": np.ascontiguousarray(
                wmod[:, c * 1536:(c + 1) * 1536].reshape(16, 128, 1536)),
            "dtb_col": dtb.reshape(16, 1).copy(),
            "alog_col": Alog.reshape(16, 1).copy(),
            "nw_row": nw.reshape(1, 128).copy(),
            "wg_slab": slab(wgT, c * 256, 2, 64),
            "wu_slab": slab(wuT, c * 256, 2, 64),
            "wd_slab": slab(wdT, c * 1024, 8, 16),
        }
        m.update(consts)
        maps.append(m)
    return maps


def build(phases=(0, 1, 2, 3, 4), taps=()):
    nc = bass.Bass("TRN2", target_bir_lowering=False)
    tap = set(taps)

    def dram_in(name, shape, dt=F32):
        return nc.dram_tensor(name, shape, dt, kind="ExternalInput").ap()

    def dram_scr(name, shape, dt):
        kind = "ExternalOutput" if name in tap else "Internal"
        return nc.dram_tensor(name, shape, dt, kind=kind).ap()

    x_d = dram_in("x", [TOKH, D])
    halo_d = dram_in("halo_mask", [128, 1])
    sel4_d = dram_in("sel4", [128, 4])
    bsel_d = dram_in("bsel", [128, 4])
    wqkv_d = dram_in("wqkv_img", [48, 128, 2048])
    wz_d = dram_in("wz_img", [16, 128, 2048])
    wba_d = dram_in("wba_img", [128, 512])
    wout_d = dram_in("wout_img", [16, 128, 2048])
    convw_d = dram_in("convw_t", [48, 128, 4])
    cT_d = dram_in("cT", [16, 128, 4])
    wmod_d = dram_in("wmod_t", [16, 128, 1536])
    dtb_d = dram_in("dtb_col", [16, 1])
    alog_d = dram_in("alog_col", [16, 1])
    nw_d = dram_in("nw_row", [1, 128])
    wgs_d = dram_in("wg_slab", [64, 2, 128, 128])
    wus_d = dram_in("wu_slab", [64, 2, 128, 128])
    wds_d = dram_in("wd_slab", [16, 8, 128, 128])
    cI_d = dram_in("c_I", [128, 128])
    cH_d = dram_in("c_H128", [128, 128])
    cHX_d = dram_in("c_HX", [128, 128])
    cmL_d = dram_in("c_mL", [128, 128])
    coL_d = dram_in("c_oL", [128, 128])
    cmU_d = dram_in("c_mU", [128, 128])
    coU_d = dram_in("c_oU", [128, 128])
    cmUi_d = dram_in("c_mUi", [128, 128])
    coUi_d = dram_in("c_oUi", [128, 128])
    csel_d = dram_in("c_sel16", [16, 2048])

    y_d = nc.dram_tensor("y", [TOK, D], F32, kind="ExternalOutput").ap()

    ag_w_in = dram_scr("ag_w_in", [3 * SLAB], BF16)
    ag_w_out = dram_scr("ag_w_out", [NCORE, 3 * SLAB], BF16)
    ar_s_in = dram_scr("ar_s_in", [1, 4], F32)
    ar_s_out = dram_scr("ar_s_out", [1, 4], F32)
    ag_m_in = dram_scr("ag_m_in", [4, 1536], F32)
    ag_m_out = dram_scr("ag_m_out", [NCORE, 4, 1536], F32)
    mod2_d = dram_scr("mod2", [4, 12288], F32)
    q_sp = dram_scr("q_sp", [CH, H, 128, 128], BF16)
    k_sp = dram_scr("k_sp", [CH, H, 128, 128], BF16)
    v_sp = dram_scr("v_sp", [CH, H, 128, 128], BF16)
    z_sp = dram_scr("z_sp", [CH, 128, 2048], BF16)
    gcum_d = dram_scr("gcum_sp", [16, 2048], F32)
    beta_d = dram_scr("beta_sp", [16, 2048], F32)
    op_sp = dram_scr("op_sp", [CH, 6, H, 128, 128], BF16)
    ag_s_in = dram_scr("ag_s_in", [128, H * 128], F32)
    ag_s_out = dram_scr("ag_s_out", [NCORE, 128, H * 128], F32)
    ocm_sp = dram_scr("ocm_sp", [H, 128, TOK], BF16)
    x2_sp = dram_scr("x2_sp", [16, 128, D], F32)
    inter_sp = dram_scr("inter_sp", [8, 64, 128, 256], BF16)
    xq2_sp = dram_scr("xq2_sp", [16, 64, 128, 128], BF16)
    rs1_d = dram_scr("rs1_sp", [16, 128], F32)
    s2tmp_d = dram_scr("s2tmp", [8, 2, 128], F32)
    rs2_d = dram_scr("rs2_sp", [16, 128], F32)

    es = contextlib.ExitStack()
    with es:
        sT = SemC(es.enter_context(nc.semaphore("sT")))
        sV = SemC(es.enter_context(nc.semaphore("sV")))
        sS = SemC(es.enter_context(nc.semaphore("sS")))
        sG = SemC(es.enter_context(nc.semaphore("sG")))
        sC = SemC(es.enter_context(nc.semaphore("sC")))

        class DmaPool:
            def __init__(self, name, eng, n):
                self.sems = [SemC(es.enter_context(nc.semaphore(f"{name}{i}")))
                             for i in range(n)]
                self.eng = eng
                self.i = 0
                self.used = [False] * n

            def start(self, args, _n=16, **kw):
                if not isinstance(args, tuple):
                    args = (args,)
                s = self.sems[self.i]
                if self.used[self.i]:
                    self.eng.wait_ge(s.h, s.v)
                self.used[self.i] = True
                self.i = (self.i + 1) % len(self.sems)
                s.inc(self.eng.dma_start(*args, **kw), 16)
                return (s, s.v)

            def wait(self, e, tok):
                e.wait_ge(tok[0].h, tok[1])

            def wait_all(self, e):
                for s in self.sems:
                    e.wait_ge(s.h, s.v)

        qX = DmaPool("qX", nc.sync, 8)
        qW = DmaPool("qW", nc.gpsimd, 6)
        qP = DmaPool("qP", nc.scalar, 6)

        pb = [es.enter_context(nc.psum_tensor(f"pb{i}", [128, 512], F32))
              for i in range(8)]
        bank_free = [(None, 0)] * 8
        bank_rr = [0, 4]

        def bank_wait(i):
            semc, val = bank_free[i]
            if semc is not None:
                nc.tensor.wait_ge(semc.h, val)

        def next_bank(grp=0):
            lo, hi = (0, 4) if grp == 0 else (4, 8)
            bank_rr[grp] = lo + (bank_rr[grp] + 1 - lo) % (hi - lo)
            return bank_rr[grp]

        def drain_dmas():
            qX.wait_all(nc.sync)
            qP.wait_all(nc.sync)
            qW.wait_all(nc.sync)
            nc.all_engine_barrier()

        def alloc(name, shape, dt):
            return es.enter_context(nc.sbuf_tensor(name, shape, dt))

        def VI(instr, n=1):
            sV.inc(instr, n)
            nc.vector.drain()
            return sV.v

        def SI(instr, n=1):
            sS.inc(instr, n)
            nc.scalar.drain()
            return sS.v

        def GI(instr, n=1):
            sG.inc(instr, n)
            nc.gpsimd.drain()
            return sG.v

        I_bf = alloc("I_bf", [128, 128], BF16)
        H128 = alloc("H128", [128, 128], BF16)
        HXs = alloc("HXs", [128, 128], BF16)
        mL = alloc("mL", [128, 128], F32)
        oL = alloc("oL", [128, 128], F32)
        mU = alloc("mU", [128, 128], F32)
        oU = alloc("oU", [128, 128], F32)
        mUi = alloc("mUi", [128, 128], F32)
        oUi = alloc("oUi", [128, 128], F32)
        ones_b = alloc("ones_b", [128, 32], BF16)
        ones16 = alloc("ones16", [16, 128], BF16)
        sel16b = alloc("sel16b", [16, 2048], BF16)
        halo_m = alloc("halo_s", [128, 1], F32)
        sel4 = alloc("sel4_s", [128, 4], F32)
        bsel = alloc("bsel_s", [128, 4], F32)
        dtb = alloc("dtb", [16, 1], F32)
        al_c = alloc("al_c", [16, 1], F32)
        negea = alloc("negea", [16, 1], F32)
        nwb = alloc("nwb", [128, 128], BF16)
        kc1 = alloc("kc1", [128, 1], F32)

        ca1b = alloc("ca1b", [128, 2048], BF16)
        sab = alloc("sab", [128, 2048], BF16)
        gab = alloc("gab", [128, 2048], BF16)
        cb1b = alloc("cb1b", [128, 2048], BF16)
        sbb = alloc("sbb", [128, 2048], BF16)
        gbb = alloc("gbb", [128, 2048], BF16)

        beta_r = alloc("beta_r", [16, 2048], F32)
        g_r = alloc("g_r", [16, 2048], F32)
        gc_r = alloc("gc_r", [16, 2048], F32)
        gcT_r = alloc("gcT_r", [128, 16 * CH], F32)
        btT_r = alloc("btT_r", [128, 16 * CH], F32)
        sbars = alloc("sbars", [128, 4], F32)
        sc1_r = alloc("sc1_r", [128, 16], F32)

        nc.gpsimd.memset(kc1[:], 1.0)
        nc.gpsimd.memset(ones_b[:], 1.0)
        GI(nc.gpsimd.memset(ones16[:], 1.0), 1)
        for dst, src in [(I_bf, cI_d), (H128, cH_d), (HXs, cHX_d),
                         (sel16b, csel_d),
                         (nwb, nw_d[0:1, :].broadcast_to((128, 128)))]:
            qW.start((dst[:], src), 16)
        for dst, src in [(mL, cmL_d), (oL, coL_d), (mU, cmU_d),
                         (oU, coU_d), (mUi, cmUi_d), (oUi, coUi_d),
                         (halo_m, halo_d), (sel4, sel4_d), (bsel, bsel_d),
                         (dtb, dtb_d), (al_c, alog_d)]:
            qX.start((dst[:], src), 16)
        for e in (nc.tensor, nc.vector, nc.scalar, nc.gpsimd, nc.sync):
            qW.wait_all(e)
            qX.wait_all(e)
            e.wait_ge(sG.h, sG.v)
        SI(nc.scalar.activation(negea[:], al_c[:], AF.Exp), 1)
        SI(nc.scalar.mul(negea[:], negea[:], -1.0), 1)
        nc.all_engine_barrier()

        def phase0():
          with contextlib.ExitStack() as ps:
            def pa(n, s, d):
                return ps.enter_context(nc.sbuf_tensor(n, s, d))
            csT = pa("csT", [128, 64], BF16)
            cTl = pa("cTl", [128, 64], F32)
            wmod_s = pa("wmod_s", [128, 8 * 1536], BF16)
            mod_sl = pa("mod_sl", [4, 1536], F32)
            qX.start((cTl[:], cT_d.transpose([1, 0, 2])), 16)
            qX.wait_all(nc.scalar)
            SI(nc.scalar.activation(csT[:], cTl[:], AF.Silu), 1)
            nc.tensor.wait_ge(sS.h, sS.v)
            mbks = [next_bank(), next_bank(), next_bank()]
            for half in range(2):
                qW.start((
                    wmod_s[:], wmod_d[half * 8:(half + 1) * 8]
                    .transpose([1, 0, 2])), 16)
                qW.wait_all(nc.tensor)
                for nb in range(3):
                    bk = mbks[nb]
                    if half == 0:
                        bank_wait(bk)
                    for k2 in range(8):
                        mm = nc.tensor.matmul(
                            pb[bk][0:4, :],
                            csT[:, (half * 8 + k2) * 4:(half * 8 + k2) * 4 + 4],
                            wmod_s[:, k2 * 1536 + nb * 512: k2 * 1536 + nb * 512 + 512],
                            start=(half == 0 and k2 == 0),
                            stop=(half == 1 and k2 == 7))
                    sT.inc(mm, 1)
                if half == 0:
                    sG.inc(nc.gpsimd.nop(), 1)
                    nc.gpsimd.wait_ge(sT.h, sT.v)
            for nb in range(3):
                bk = mbks[nb]
                nc.scalar.wait_ge(sT.h, sT.v)
                SI(nc.scalar.copy(
                    mod_sl[:, nb * 512:(nb + 1) * 512], pb[bk][0:4, :]), 1)
                bank_free[bk] = (sS, sS.v)
            nc.scalar.wait_ge(sS.h, sS.v)
            qP.start((ag_m_in, mod_sl[:]), 16)
            qP.wait_all(nc.gpsimd)
            sC.inc(nc.gpsimd.collective_compute(
                "AllGather", ALU.bypass, replica_groups=[list(range(NCORE))],
                ins=[ag_m_in.opt()], outs=[ag_m_out.opt()]), 1)

            asum = pa("asum", [128, 4], F32)
            red = pa("red", [128, 1], F32)
            wbuf = pa("wbuf", [128, 2 * 4096], F32)
            VI(nc.vector.memset(asum[:], 0.0), 1)
            flats = [wgs_d.rearrange("a b p f -> (a b) p f"),
                     wus_d.rearrange("a b p f -> (a b) p f"),
                     wds_d.rearrange("a b p f -> (a b) p f")]
            vred = []
            vg = 0
            for si in range(3):
                for gg in range(4):
                    buf = wbuf[:, (vg % 2) * 4096:(vg % 2) * 4096 + 4096]
                    if vg >= 2:
                        nc.sync.wait_ge(sV.h, vred[vg - 2])
                    qX.start((
                        buf, flats[si][gg * 32:(gg + 1) * 32].transpose([1, 0, 2])), 16)
                    qX.wait_all(nc.vector)
                    VI(nc.vector.tensor_reduce(
                        red[:], buf, AX.X, ALU.add, apply_absolute_value=True), 1)
                    VI(nc.vector.tensor_tensor(
                        asum[:, si:si + 1], asum[:, si:si + 1], red[:], ALU.add), 1)
                    vred.append(sV.v)
                    vg += 1
            p4h = pa("p4h", [128, 4], BF16)
            p4f = pa("p4f", [128, 4], F32)
            p4l = pa("p4l", [128, 4], BF16)
            p4r = pa("p4r", [128, 4], F32)
            p4l2 = pa("p4l2", [128, 4], BF16)
            arow = pa("arow", [1, 4], F32)
            VI(nc.vector.tensor_copy(p4h[:], asum[:]), 1)
            VI(nc.vector.tensor_copy(p4f[:], p4h[:]), 1)
            VI(nc.vector.tensor_tensor(p4r[:], asum[:], p4f[:], ALU.subtract), 1)
            VI(nc.vector.tensor_copy(p4l[:], p4r[:]), 1)
            VI(nc.vector.tensor_copy(p4f[:], p4l[:]), 1)
            VI(nc.vector.tensor_tensor(p4r[:], p4r[:], p4f[:], ALU.subtract), 1)
            VI(nc.vector.tensor_copy(p4l2[:], p4r[:]), 1)
            bk = next_bank()
            bank_wait(bk)
            nc.tensor.wait_ge(sV.h, sV.v)
            nc.tensor.matmul(pb[bk][0:32, 0:4], ones_b[:], p4h[:], start=True, stop=False)
            nc.tensor.matmul(pb[bk][0:32, 0:4], ones_b[:], p4l[:], start=False, stop=False)
            mm = nc.tensor.matmul(pb[bk][0:32, 0:4], ones_b[:], p4l2[:], start=False, stop=True)
            sT.inc(mm, 1)
            nc.scalar.wait_ge(sT.h, sT.v)
            SI(nc.scalar.copy(arow[:], pb[bk][0:1, 0:4]), 1)
            bank_free[bk] = (sS, sS.v)
            nc.scalar.wait_ge(sS.h, sS.v)
            qP.start((ar_s_in, arow[:]), 16)
            qP.wait_all(nc.gpsimd)
            sC.inc(nc.gpsimd.collective_compute(
                "AllReduce", ALU.add, replica_groups=[list(range(NCORE))],
                ins=[ar_s_in.opt()], outs=[ar_s_out.opt()]), 1)
            nc.sync.wait_ge(sC.h, sC.v)
            qX.start((
                sbars[:], ar_s_out[0:1, :].broadcast_to((128, 4))), 16)
            qX.wait_all(nc.vector)
            nv = 1.0 / float(2048 * 8192)
            VI(nc.vector.tensor_scalar_mul(sbars[:], sbars[:], nv), 1)
            thrb = pa("thrb", [128, 4], F32)
            nthrb = pa("nthrb", [128, 4], F32)
            VI(nc.vector.tensor_scalar_mul(thrb[:], sbars[:], 0.5), 1)
            VI(nc.vector.tensor_scalar_mul(nthrb[:], sbars[:], -0.5), 1)
            b1 = pa("b1", [128, 4096], F32)
            b2t = pa("b2t", [128, 2 * 4096], BF16)
            vq, pq = [], []
            vg = 0
            for si in range(3):
                for gg in range(4):
                    buf = wbuf[:, (vg % 2) * 4096:(vg % 2) * 4096 + 4096]
                    obuf = b2t[:, (vg % 2) * 4096:(vg % 2) * 4096 + 4096]
                    if vg >= 2:
                        nc.sync.wait_ge(sV.h, vq[vg - 2])
                    qX.start((
                        buf, flats[si][gg * 32:(gg + 1) * 32].transpose([1, 0, 2])), 16)
                    qX.wait_all(nc.vector)
                    if vg >= 2:
                        qP.wait(nc.vector, pq[vg - 2])
                    VI(nc.vector.tensor_scalar(
                        obuf, buf, nthrb[:, si:si + 1], None, ALU.is_lt), 1)
                    VI(nc.vector.tensor_scalar(
                        b1[:], buf, thrb[:, si:si + 1], None, ALU.is_gt), 1)
                    VI(nc.vector.tensor_tensor(obuf, b1[:], obuf, ALU.subtract), 1)
                    vq.append(sV.v)
                    nc.scalar.wait_ge(sV.h, sV.v)
                    dst = ag_w_in[si * SLAB + gg * 32 * 16384:
                                  si * SLAB + (gg + 1) * 32 * 16384]
                    pq.append(qP.start((
                        dst.rearrange("(t p f) -> t p f", t=32, p=128)
                        .transpose([1, 0, 2]), obuf)))

                    vg += 1
            qP.wait_all(nc.gpsimd)
            sC.inc(nc.gpsimd.collective_compute(
                "AllGather", ALU.bypass, replica_groups=[list(range(NCORE))],
                ins=[ag_w_in.opt()], outs=[ag_w_out.opt()]), 1)

            nc.sync.wait_ge(sC.h, sC.v)
            qX.start((
                mod2_d.rearrange("b (r j) -> b r j", r=8),
                ag_m_out.transpose([1, 0, 2])), 16)
            qX.wait_all(nc.sync)
            mtmp = pa("mtmp", [128, 2048], F32)
            macc = pa("macc", [128, 2048], F32)
            for ii, dst in enumerate((sab, ca1b, gab, sbb, cb1b, gbb)):
                for bb in range(4):
                    qX.start((
                        mtmp[:], mod2_d[bb, ii * 2048:(ii + 1) * 2048]
                        .unsqueeze(0).broadcast_to((128, 2048))), 16)
                    qX.wait_all(nc.vector)
                    if bb == 0:
                        VI(nc.vector.tensor_scalar(
                            macc[:], mtmp[:], bsel[:, 0:1], None, ALU.mult), 1)
                    else:
                        VI(nc.vector.scalar_tensor_tensor(
                            macc[:], mtmp[:], bsel[:, bb:bb + 1], macc[:],
                            ALU.mult, ALU.add), 1)
                    nc.sync.wait_ge(sV.h, sV.v)
                if ii in (1, 4):
                    VI(nc.vector.tensor_scalar(
                        macc[:], macc[:], 1.0, None, ALU.add), 1)
                if ii in (2, 5):
                    VI(nc.vector.tensor_scalar(
                        mtmp[:], macc[:], 0.0, 2.0, ALU.is_ge, ALU.mult), 1)
                    VI(nc.vector.tensor_scalar(
                        mtmp[:], mtmp[:], 1.0, None, ALU.subtract), 1)
                    VI(nc.vector.tensor_tensor(macc[:], macc[:], mtmp[:],
                                               ALU.mult), 1)
                    VI(nc.vector.tensor_scalar(
                        macc[:], macc[:], 0.25, None, ALU.max), 1)
                    VI(nc.vector.tensor_tensor(macc[:], macc[:], mtmp[:],
                                               ALU.mult), 1)
                VI(nc.vector.tensor_copy(dst[:], macc[:]), 1)
                nc.sync.wait_ge(sV.h, sV.v)
            drain_dmas()

        def phase1():
          with contextlib.ExitStack() as ps:
            def pa(n, s, d):
                return ps.enter_context(nc.sbuf_tensor(n, s, d))
            h1T = pa("h1T", [128, NKD * HT], BF16)
            kceps5 = pa("kceps5", [128, 1], F32)
            kceps6 = pa("kceps6", [128, 1], F32)
            nc.gpsimd.memset(kceps5[:], 1e-5)
            GI(nc.gpsimd.memset(kceps6[:], 1e-6), 1)
            nc.vector.wait_ge(sG.h, sG.v)
            nc.scalar.wait_ge(sG.h, sG.v)

            h1T3 = h1T[:].rearrange("p (k c) -> p k c", k=NKD)
            tr_done = {}
            ln_es = contextlib.ExitStack()
            xt = ln_es.enter_context(nc.sbuf_tensor("xt", [128, 2 * 2048], F32))
            sqt = ln_es.enter_context(nc.sbuf_tensor("sqt", [128, 2048], F32))
            stat = ln_es.enter_context(nc.sbuf_tensor("stat", [128, 8], F32))
            h1a = ln_es.enter_context(nc.sbuf_tensor("h1a", [128, 2048], F32))
            h1b = ln_es.enter_context(nc.sbuf_tensor("h1b", [128, 2 * 2048], BF16))
            xh = ln_es.enter_context(nc.sbuf_tensor("xh", [16, 2048], F32))

            def ln_tile(x_ap, np_, col0, slot, idx):
                hb = h1b[:, slot * 2048:(slot + 1) * 2048]
                VI(nc.vector.tensor_reduce(
                    stat[0:np_, 0:1], x_ap, AX.X, ALU.add), 1)
                SI(nc.scalar.activation(
                    sqt[0:np_, :], x_ap, AF.Square,
                    accum_out=stat[0:np_, 2:3]), 1)
                nc.vector.wait_ge(sS.h, sS.v)
                VI(nc.vector.tensor_scalar_mul(
                    stat[0:np_, 1:2], stat[0:np_, 0:1], 1.0 / 2048.0), 1)
                VI(nc.vector.tensor_scalar_mul(
                    stat[0:np_, 3:4], stat[0:np_, 2:3], 1.0 / 2048.0), 1)
                VI(nc.vector.tensor_tensor(
                    stat[0:np_, 4:5], stat[0:np_, 1:2], stat[0:np_, 1:2],
                    ALU.mult), 1)
                VI(nc.vector.tensor_tensor(
                    stat[0:np_, 3:4], stat[0:np_, 3:4], stat[0:np_, 4:5],
                    ALU.subtract), 1)
                nc.scalar.wait_ge(sV.h, sV.v)
                SI(nc.scalar.activation(
                    stat[0:np_, 5:6], stat[0:np_, 3:4], AF.Sqrt,
                    bias=kceps5[0:np_, :]), 1)
                nc.vector.wait_ge(sS.h, sS.v)
                VI(nc.vector.reciprocal(
                    stat[0:np_, 6:7], stat[0:np_, 5:6]), 1)
                VI(nc.vector.tensor_scalar(
                    h1a[0:np_, :], x_ap, stat[0:np_, 1:2], stat[0:np_, 6:7],
                    ALU.subtract, ALU.mult), 1)
                VI(nc.vector.tensor_tensor(
                    h1a[0:np_, :], h1a[0:np_, :], ca1b[0:np_, :], ALU.mult), 1)
                if idx - 2 in tr_done:
                    nc.vector.wait_ge(sT.h, tr_done[idx - 2])
                VI(nc.vector.tensor_tensor(
                    hb[0:np_, :], h1a[0:np_, :], sab[0:np_, :], ALU.add), 1)
                nc.tensor.wait_ge(sV.h, sV.v)
                kpart = max(np_, 16)
                for g4 in range(4):
                    bk = next_bank(0)
                    bank_wait(bk)
                    for j in range(4):
                        kk = g4 * 4 + j
                        mm = nc.tensor.matmul(
                            pb[bk][:, j * 128:j * 128 + np_],
                            hb[0:kpart, kk * 128:(kk + 1) * 128],
                            I_bf[0:kpart, 0:np_],
                            start=True, stop=True)
                    sT.inc(mm, 1)
                    nc.scalar.wait_ge(sT.h, sT.v)
                    SI(nc.scalar.copy(
                        h1T3[:, g4 * 4:(g4 + 1) * 4, col0:col0 + np_],
                        pb[bk][:, :].rearrange("p (j c) -> p j c", j=4)
                        [:, :, 0:np_]), 1)
                    bank_free[bk] = (sS, sS.v)
                tr_done[idx] = sT.v

            qX.start((xh[0:3, 0:2048], x_d[0:3, :]), 16)
            qX.wait_all(nc.vector)
            qX.wait_all(nc.scalar)
            VI(nc.vector.memset(h1b[0:16, 0:2048], 0.0), 1)
            ln_tile(xh[0:3, 0:2048], 3, 0, 0, 0)
            ln_v = {}
            for tt in range(16):
                xbuf = xt[:, (tt % 2) * 2048:(tt % 2) * 2048 + 2048]
                if tt >= 2:
                    nc.sync.wait_ge(sV.h, ln_v[tt - 2])
                qX.start((
                    xbuf, x_d[3 + tt * 128: 3 + (tt + 1) * 128, :]), 16)
                qX.wait_all(nc.vector)
                qX.wait_all(nc.scalar)
                ln_tile(xbuf, 128, 3 + tt * 128, (tt + 1) % 2, tt + 1)
                ln_v[tt] = sV.v
            drain_dmas()
            ln_es.close()

            qk_es = contextlib.ExitStack()
            def qa(n, s, d):
                return qk_es.enter_context(nc.sbuf_tensor(n, s, d))
            wimg = qa("wimg", [128, 2 * 2048], BF16)
            cwt = qa("cwt", [128, 2 * 4], F32)
            mp = qa("mp", [128, 2 * 2051], F32)
            acc = qa("acc", [128, 2 * 2048], BF16)
            silt = qa("silt", [128, 2 * 2048], BF16)
            sqb = qa("sqb", [128, 2048], BF16)
            rrow = qa("rrow", [16, 2048], F32)
            rrb = qa("rrb", [16, 2048], BF16)
            rb = qa("rb", [128, 2048], F32)
            stg = qa("stg", [128, 512], BF16)
            VI(nc.vector.memset(rrb[:], 0.0), 1)

            NBQ = [(0, 512), (512, 512), (1024, 512), (1536, 512), (2048, 3)]
            NBZ = [(3, 512), (515, 512), (1027, 512), (1539, 512)]
            wfree = {}
            mpfree = {}
            silu_s = {}
            spill_x = {}

            def stream_w(m, src_ap, conv_m):
                if m - 2 in wfree:
                    nc.gpsimd.wait_ge(sT.h, wfree[m - 2])
                if conv_m is not None and m - 2 in mpfree:
                    nc.gpsimd.wait_ge(sV.h, mpfree[m - 2])
                qW.start((
                    wimg[:, (m % 2) * 2048:(m % 2) * 2048 + 2048], src_ap), 16)
                if conv_m is not None:
                    qW.start((
                        cwt[:, (m % 2) * 4:(m % 2) * 4 + 4], convw_d[conv_m]), 16)
                qW.wait_all(nc.tensor)

            def mm_tile(m, nblocks, evac):
                for bi, (c0, cn) in enumerate(nblocks):
                    bk = next_bank(0)
                    bank_wait(bk)
                    for kk in range(NKD):
                        mm = nc.tensor.matmul(
                            pb[bk][:, 0:cn],
                            wimg[:, (m % 2) * 2048 + kk * 128:
                                 (m % 2) * 2048 + (kk + 1) * 128],
                            h1T3[:, kk, c0:c0 + cn],
                            start=(kk == 0), stop=(kk == NKD - 1))
                    sT.inc(mm, 1)
                    evac(bi, c0, cn, bk)
                wfree[m] = sT.v

            for m in range(48):
                stream_w(m, wqkv_d[m], m)
                if m - 2 in mpfree:
                    nc.tensor.wait_ge(sV.h, mpfree[m - 2])
                ms = mp[:, (m % 2) * 2051:(m % 2) * 2051 + 2051]

                def evac_q(bi, c0, cn, bk, ms=ms):
                    nc.scalar.wait_ge(sT.h, sT.v)
                    SI(nc.scalar.copy(ms[:, c0:c0 + cn], pb[bk][:, 0:cn]), 1)
                    bank_free[bk] = (sS, sS.v)
                mm_tile(m, NBQ, evac_q)
                nc.vector.wait_ge(sS.h, sS.v)
                VI(nc.vector.tensor_scalar(
                    ms[:, 0:3], ms[:, 0:3], halo_m[:, 0:1], None, ALU.mult), 1)
                a0 = acc[:, (m % 2) * 2048:(m % 2) * 2048 + 2048]
                a1 = acc[:, ((m + 1) % 2) * 2048:((m + 1) % 2) * 2048 + 2048]
                cw = cwt[:, (m % 2) * 4:(m % 2) * 4 + 4]
                if m - 1 in silu_s:
                    nc.vector.wait_ge(sS.h, silu_s[m - 1])
                VI(nc.vector.tensor_scalar(
                    a0, ms[:, 0:2048], cw[:, 0:1], None, ALU.mult), 1)
                VI(nc.vector.scalar_tensor_tensor(
                    a1, ms[:, 1:2049], cw[:, 1:2], a0, ALU.mult, ALU.add), 1)
                VI(nc.vector.scalar_tensor_tensor(
                    a0, ms[:, 2:2050], cw[:, 2:3], a1, ALU.mult, ALU.add), 1)
                VI(nc.vector.scalar_tensor_tensor(
                    a1, ms[:, 3:2051], cw[:, 3:4], a0, ALU.mult, ALU.add), 1)
                mpfree[m] = sV.v
                sl = silt[:, (m % 2) * 2048:(m % 2) * 2048 + 2048]
                nc.scalar.wait_ge(sV.h, sV.v)
                if m - 2 in spill_x:
                    qX.wait(nc.scalar, spill_x[m - 2])
                silu_s[m] = SI(nc.scalar.activation(sl, a1, AF.Silu), 1)
                if m < 32:
                    SI(nc.scalar.activation(sqb[:], sl, AF.Square), 1)
                    nc.tensor.wait_ge(sS.h, sS.v)
                    for nb in range(4):
                        bk = next_bank(1)
                        bank_wait(bk)
                        mm = nc.tensor.matmul(
                            pb[bk][0:32, 0:512], ones_b[:],
                            sqb[:, nb * 512:(nb + 1) * 512],
                            start=True, stop=True)
                        sT.inc(mm, 1)
                        nc.vector.wait_ge(sT.h, sT.v)
                        VI(nc.vector.tensor_copy(
                            rrow[0:1, nb * 512:(nb + 1) * 512],
                            pb[bk][0:1, 0:512]), 1)
                        bank_free[bk] = (sV, sV.v)
                    nc.scalar.wait_ge(sV.h, sV.v)
                    SI(nc.scalar.activation(
                        rrow[0:1, :], rrow[0:1, :], AF.Sqrt,
                        bias=kceps6[0:1, :]), 1)
                    nc.vector.wait_ge(sS.h, sS.v)
                    VI(nc.vector.reciprocal(rrow[0:1, :], rrow[0:1, :]), 1)
                    if m < 16:
                        VI(nc.vector.tensor_scalar_mul(
                            rrow[0:1, :], rrow[0:1, :], float(128 ** -0.5)), 1)
                    VI(nc.vector.tensor_copy(rrb[0:1, :], rrow[0:1, :]), 1)
                    nc.tensor.wait_ge(sV.h, sV.v)
                    for nb in range(4):
                        bk = next_bank(1)
                        bank_wait(bk)
                        mm = nc.tensor.matmul(
                            pb[bk][:, 0:512], ones16[:],
                            rrb[:, nb * 512:(nb + 1) * 512], start=True, stop=True)
                        sT.inc(mm, 1)
                        nc.scalar.wait_ge(sT.h, sT.v)
                        SI(nc.scalar.copy(
                            rb[:, nb * 512:(nb + 1) * 512], pb[bk][:, 0:512]), 1)
                        bank_free[bk] = (sS, sS.v)
                    nc.vector.wait_ge(sS.h, sS.v)
                    VI(nc.vector.tensor_tensor(sl, sl, rb[:], ALU.mult), 1)
                    last = (sV, sV.v)
                else:
                    last = (sS, silu_s[m])
                dst = (q_sp if m < 16 else k_sp if m < 32 else v_sp)
                hh = m % 16
                nc.sync.wait_ge(last[0].h, last[1])
                spill_x[m] = qX.start((
                    dst[:, hh].transpose([1, 0, 2]), sl))
            drain_dmas()

            zspill = {}
            for m in range(16):
                stream_w(m, wz_d[m], None)

                def evac_z(bi, c0, cn, bk):
                    nc.scalar.wait_ge(sT.h, sT.v)
                    SI(nc.scalar.activation(
                        silt[:, bi * 512:(bi + 1) * 512], pb[bk][:, 0:cn],
                        AF.Silu), 1)
                    bank_free[bk] = (sS, sS.v)
                mm_tile(m, NBZ, evac_z)
                nc.tensor.wait_ge(sS.h, sS.v)
                for c4 in range(4):
                    bk = next_bank(1)
                    bank_wait(bk)
                    for j in range(4):
                        cci = c4 * 4 + j
                        mm = nc.tensor.matmul(
                            pb[bk][:, j * 128:(j + 1) * 128],
                            silt[:, cci * 128:(cci + 1) * 128],
                            I_bf[:], start=True, stop=True)
                    sT.inc(mm, 1)
                    nc.vector.wait_ge(sT.h, sT.v)
                    if (m, c4) != (0, 0):
                        qX.wait_all(nc.vector)
                    VI(nc.vector.tensor_copy(stg[:], pb[bk][:, :]), 1)
                    bank_free[bk] = (sV, sV.v)
                    nc.sync.wait_ge(sV.h, sV.v)
                    qX.start((
                        z_sp[c4 * 4:(c4 + 1) * 4, :, m * 128:(m + 1) * 128]
                        .transpose([1, 0, 2]), stg[:]), 16)
            drain_dmas()
            qk_es.close()

            wbal = pa("wbal", [128, 512], BF16)
            bag = pa("bag", [16, 2048], F32)
            aag = pa("aag", [16, 2048], F32)
            qW.start((wbal[:], wba_d), 16)
            qW.wait_all(nc.tensor)
            for part, out in ((0, bag), (16, aag)):
                for bi, (c0, cn) in enumerate(NBZ):
                    bk = next_bank(0)
                    bank_wait(bk)
                    for kk in range(NKD):
                        mm = nc.tensor.matmul(
                            pb[bk][0:16, 0:cn],
                            wbal[:, kk * 32 + part:kk * 32 + part + 16],
                            h1T3[:, kk, c0:c0 + cn],
                            start=(kk == 0), stop=(kk == NKD - 1))
                    sT.inc(mm, 1)
                    nc.scalar.wait_ge(sT.h, sT.v)
                    SI(nc.scalar.copy(out[:, bi * 512:(bi + 1) * 512],
                                          pb[bk][0:16, 0:cn]), 1)
                    bank_free[bk] = (sS, sS.v)
            SI(nc.scalar.activation(beta_r[:], bag[:], AF.Sigmoid), 1)
            SI(nc.scalar.activation(g_r[:], aag[:], AF.Exp, bias=dtb[:]), 1)
            SI(nc.scalar.activation(g_r[:], g_r[:], AF.Ln,
                                        bias=kc1[0:16, :]), 1)
            nc.vector.wait_ge(sS.h, sS.v)
            VI(nc.vector.tensor_scalar(
                g_r[:], g_r[:], negea[:], None, ALU.mult), 1)
            ga_, gb_ = g_r, gc_r
            for st_ in range(7):
                hh = 1 << st_
                va = ga_[:].rearrange("p (c f) -> p c f", c=CH)
                vb = gb_[:].rearrange("p (c f) -> p c f", c=CH)
                VI(nc.vector.tensor_copy(vb[:, :, 0:hh], va[:, :, 0:hh]), 1)
                VI(nc.vector.tensor_tensor(
                    vb[:, :, hh:128], va[:, :, hh:128], va[:, :, 0:128 - hh],
                    ALU.add), 1)
                ga_, gb_ = gb_, ga_
            nc.sync.wait_ge(sV.h, sV.v)
            qX.start((gcum_d, gc_r[:]), 16)
            qX.start((beta_d, beta_r[:]), 16)
            p_hi = pa("p_hi", [16, 2048], BF16)
            p_f = pa("p_f", [16, 2048], F32)
            p_lo = pa("p_lo", [16, 2048], BF16)
            p_r = pa("p_r", [16, 2048], F32)
            p_l2 = pa("p_l2", [16, 2048], BF16)
            bt_b = pa("bt_b", [16, 2048], BF16)
            VI(nc.vector.tensor_copy(p_hi[:], gc_r[:]), 1)
            VI(nc.vector.tensor_copy(p_f[:], p_hi[:]), 1)
            VI(nc.vector.tensor_tensor(p_r[:], gc_r[:], p_f[:], ALU.subtract), 1)
            VI(nc.vector.tensor_copy(p_lo[:], p_r[:]), 1)
            VI(nc.vector.tensor_copy(p_f[:], p_lo[:]), 1)
            VI(nc.vector.tensor_tensor(p_r[:], p_r[:], p_f[:], ALU.subtract), 1)
            VI(nc.vector.tensor_copy(p_l2[:], p_r[:]), 1)
            VI(nc.vector.tensor_copy(bt_b[:], beta_r[:]), 1)
            nc.tensor.wait_ge(sV.h, sV.v)
            for c in range(CH):
                bk = next_bank(0)
                bank_wait(bk)
                cs = slice(c * 128, (c + 1) * 128)
                nc.tensor.matmul(pb[bk][:, 0:16], p_hi[:, cs], I_bf[0:16, 0:16],
                                 start=True, stop=False)
                nc.tensor.matmul(pb[bk][:, 0:16], p_lo[:, cs], I_bf[0:16, 0:16],
                                 start=False, stop=False)
                nc.tensor.matmul(pb[bk][:, 0:16], p_l2[:, cs], I_bf[0:16, 0:16],
                                 start=False, stop=True)
                mm = nc.tensor.matmul(pb[bk][:, 16:32], bt_b[:, cs],
                                      I_bf[0:16, 0:16], start=True, stop=True)
                sT.inc(mm, 1)
                nc.scalar.wait_ge(sT.h, sT.v)
                SI(nc.scalar.copy(gcT_r[:, c * 16:(c + 1) * 16],
                                      pb[bk][:, 0:16]), 1)
                SI(nc.scalar.copy(btT_r[:, c * 16:(c + 1) * 16],
                                      pb[bk][:, 16:32]), 1)
                bank_free[bk] = (sS, sS.v)
            drain_dmas()


        def phase2():
          with contextlib.ExitStack() as ps:
            def pa(n, s, d):
                return ps.enter_context(nc.sbuf_tensor(n, s, d))
            qc = pa("qc", [128, 2048], BF16)
            kc = pa("kc", [128, 2048], BF16)
            vc = pa("vc", [128, 2048], BF16)
            ktil = pa("ktil", [128, 2048], BF16)
            qbar = pa("qbar", [128, 2048], BF16)
            aprow = pa("aprow", [128, 2048], BF16)
            e1 = pa("e1", [128, 2048], BF16)
            e2 = pa("e2", [128, 2048], BF16)
            e3 = pa("e3", [128, 2048], BF16)
            pre_all = pa("pre_all", [128, 2048], BF16)
            pq = pa("pq", [128, 4096], BF16)
            Ptmat = pa("Ptmat", [128, 2048], BF16)
            vT = pa("vT", [128, 2048], BF16)
            kT = pa("kT", [128, 2048], BF16)
            khat = pa("khat", [128, 2048], BF16)
            w2tn = pa("w2tn", [128, 2048], BF16)
            U_all = pa("U_all", [128, 2048], BF16)
            S_f = pa("S_f", [128, 2048], F32)
            S_b = pa("S_b", [128, 2048], BF16)
            Yb = pa("Yb", [128, 4096], BF16)
            rowb = pa("rowb", [128, 2048], F32)
            browb = pa("browb", [128, 2048], F32)
            art = pa("art", [128, 2048], F32)
            art2 = pa("art2", [128, 2048], F32)
            sq16 = pa("sq16", [128, 16], F32)
            akm = pa("akm", [128, 16], F32)
            Gbe = pa("Gbe", [128, 16], F32)
            apT = pa("apT", [128, 16], F32)
            bapT = pa("bapT", [128, 16], F32)
            obuf = pa("obuf", [128, 2048], BF16)
            onb = pa("onb", [128, 2048], BF16)
            zc = pa("zc", [128, 2048], BF16)
            stg2 = pa("stg2", [128, 512], BF16)
            sqs = pa("sqs", [128, 512], BF16)
            ep6 = pa("ep6", [128, 1], F32)
            GI(nc.gpsimd.memset(ep6[:], 1e-6), 1)
            VI(nc.vector.memset(S_f[:], 0.0), 1)
            VI(nc.vector.memset(S_b[:], 0.0), 1)
            nc.gpsimd.wait_ge(sG.h, sG.v)

            Y3 = Yb[:].rearrange("p (h u) -> p h u", h=H)
            qc3 = qc[:].rearrange("p (h t) -> p h t", h=H)
            kc3 = kc[:].rearrange("p (h t) -> p h t", h=H)
            art3 = art[:].rearrange("p (h t) -> p h t", h=H)
            rowb3 = rowb[:].rearrange("p (h t) -> p h t", h=H)

            def bc16(x):
                return x.unsqueeze(2).broadcast_to((128, 16, 128))

            def m128(x):
                return x.unsqueeze(1).broadcast_to((128, 16, 128))

            def soft_sync():
                nc.sync.wait_ge(sT.h, sT.v)
                nc.sync.wait_ge(sV.h, sV.v)
                nc.sync.wait_ge(sS.h, sS.v)
                nc.vector.wait_ge(sT.h, sT.v)
                nc.scalar.wait_ge(sT.h, sT.v)
                qP.wait_all(nc.vector)
                qP.wait_all(nc.scalar)
                qX.wait_all(nc.vector)
                qX.wait_all(nc.scalar)

            def chunk_pre(c):
                qX.start((qc[:], q_sp[c].transpose([1, 0, 2])))
                qX.start((kc[:], k_sp[c].transpose([1, 0, 2])))
                qX.start((vc[:], v_sp[c].transpose([1, 0, 2])))
                qX.start((rowb[:],
                          gcum_d[0:16, c * 128:(c + 1) * 128]
                          .unsqueeze(0).broadcast_to((128, 16, 128))))
                qX.start((browb[:],
                          beta_d[0:16, c * 128:(c + 1) * 128]
                          .unsqueeze(0).broadcast_to((128, 16, 128))))
                qX.wait_all(nc.vector)
                qX.wait_all(nc.scalar)
                gcs = gcT_r[:, c * 16:(c + 1) * 16]
                bts = btT_r[:, c * 16:(c + 1) * 16]
                VI(nc.vector.tensor_tensor(art3, rowb3, bc16(gcs), ALU.subtract), 1)
                SI(nc.scalar.activation(aprow[:], rowb[:], AF.Exp), 1)
                nc.scalar.wait_ge(sV.h, sV.v)
                SI(nc.scalar.activation(
                    akm[:].unsqueeze(2), art3[:, :, 127:128], AF.Exp), 1)
                SI(nc.scalar.activation(
                    Gbe[:].unsqueeze(2), rowb3[:, :, 127:128], AF.Exp), 1)
                SI(nc.scalar.activation(apT[:], gcs, AF.Exp), 1)
                a2v = art2[:].rearrange("p (h t) -> p h t", h=H)
                for msk, off, scl, dst in ((mL, oL, -1.0, e1),
                                           (mU, oU, 1.0, e2),
                                           (mUi, oUi, 1.0, e3)):
                    nc.vector.wait_ge(sS.h, sS.v)
                    VI(nc.vector.tensor_tensor(a2v, art3, m128(msk[:]), ALU.mult), 1)
                    VI(nc.vector.tensor_tensor(a2v, a2v, m128(off[:]), ALU.add), 1)
                    nc.scalar.wait_ge(sV.h, sV.v)
                    SI(nc.scalar.activation(dst[:], art2[:], AF.Exp, scale=scl), 1)
                nc.vector.wait_ge(sS.h, sS.v)
                VI(nc.vector.tensor_tensor(ktil[:], kc[:], browb[:], ALU.mult), 1)
                VI(nc.vector.tensor_tensor(qbar[:], qc[:], aprow[:], ALU.mult), 1)
                VI(nc.vector.tensor_tensor(bapT[:], bts, apT[:], ALU.mult), 1)
                nc.tensor.wait_ge(sV.h, sV.v)
                pq3 = pq[:].rearrange("p (h u) -> p h u", h=H)
                pre3v = pre_all[:].rearrange("p (h t) -> p h t", h=H)
                for lhs, rhs, emat, dst in (
                        (ktil, kc, e1, pq3[:, :, 0:128]),
                        (kc, ktil, e2, pq3[:, :, 128:256]),
                        (kc, qc, e3,
                         Ptmat[:].rearrange("p (h t) -> p h t", h=H))):
                    nc.scalar.wait_ge(sV.h, sV.v)
                    for hg in range(4):
                        bk = next_bank(0)
                        bank_wait(bk)
                        for j in range(4):
                            hh = hg * 4 + j
                            mm = nc.tensor.matmul(
                                pb[bk][:, j * 128:(j + 1) * 128],
                                lhs[:, hh * 128:(hh + 1) * 128],
                                rhs[:, hh * 128:(hh + 1) * 128],
                                start=True, stop=True)
                        sT.inc(mm, 1)
                        nc.scalar.wait_ge(sT.h, sT.v)
                        SI(nc.scalar.copy(
                            pre_all[:, hg * 512:(hg + 1) * 512], pb[bk][:, :]), 1)
                        bank_free[bk] = (sS, sS.v)
                    nc.vector.wait_ge(sS.h, sS.v)
                    VI(nc.vector.tensor_tensor(
                        dst, pre3v,
                        emat[:].rearrange("p (h t) -> p h t", h=H),
                        ALU.mult), 1)
                for srcb, dstb in ((vc, vT), (kc, kT)):
                    nc.tensor.wait_ge(sV.h, sV.v)
                    for hg in range(4):
                        bk = next_bank(0)
                        bank_wait(bk)
                        for j in range(4):
                            hh = hg * 4 + j
                            mm = nc.tensor.matmul(
                                pb[bk][:, j * 128:(j + 1) * 128],
                                srcb[:, hh * 128:(hh + 1) * 128],
                                I_bf[:], start=True, stop=True)
                        sT.inc(mm, 1)
                        nc.scalar.wait_ge(sT.h, sT.v)
                        SI(nc.scalar.copy(
                            dstb[:, hg * 512:(hg + 1) * 512], pb[bk][:, :]), 1)
                        bank_free[bk] = (sS, sS.v)
                vT3 = vT[:].rearrange("p (h t) -> p h t", h=H)
                kT3 = kT[:].rearrange("p (h t) -> p h t", h=H)
                nc.vector.wait_ge(sS.h, sS.v)
                VI(nc.vector.tensor_tensor(Y3[:, :, 0:128], vT3, bc16(bts),
                                           ALU.mult), 1)
                VI(nc.vector.tensor_tensor(Y3[:, :, 128:256], kT3, bc16(bapT[:]),
                                           ALU.mult), 1)
                VI(nc.vector.tensor_tensor(
                    khat[:].rearrange("p (h t) -> p h t", h=H), kT3,
                    bc16(akm[:]), ALU.mult), 1)
                nc.tensor.wait_ge(sV.h, sV.v)
                yev = [None] * 8
                aev = [0] * 8
                for pr in range(8):
                    bk = next_bank(pr % 2)
                    bank_wait(bk)
                    for u in range(2):
                        hh = pr * 2 + u
                        mm = nc.tensor.matmul(
                            pb[bk][:, u * 256:(u + 1) * 256],
                            pq[:, hh * 256 + 128:hh * 256 + 256],
                            Yb[:, hh * 256:(hh + 1) * 256],
                            start=True, stop=True)
                    sT.inc(mm, 1)
                    nc.vector.wait_ge(sT.h, sT.v)
                    VI(nc.vector.tensor_tensor(
                        Yb[:, pr * 512:(pr + 1) * 512],
                        Yb[:, pr * 512:(pr + 1) * 512],
                        pb[bk][:, 0:512], ALU.subtract), 1)
                    bank_free[bk] = (sV, sV.v)
                    yev[pr] = (sV.h, sV.v)
                for j in range(1, 7):
                    for pr in range(8):
                        bk = next_bank(pr % 2)
                        bank_wait(bk)
                        if j > 1:
                            nc.tensor.wait_ge(sS.h, aev[pr])
                        for u in range(2):
                            hh = pr * 2 + u
                            pcur = pq[:, hh * 256:hh * 256 + 128]
                            ptcur = pq[:, hh * 256 + 128:hh * 256 + 256]
                            nc.tensor.matmul(
                                pb[bk][:, u * 256:u * 256 + 128],
                                ptcur, pcur, start=True, stop=True)
                            mm = nc.tensor.matmul(
                                pb[bk][:, u * 256 + 128:u * 256 + 256],
                                pcur, ptcur, start=True, stop=True)
                        sT.inc(mm, 1)
                        nc.scalar.wait_ge(sT.h, sT.v)
                        SI(nc.scalar.copy(pq[:, pr * 512:(pr + 1) * 512],
                                          pb[bk][:, 0:512]), 1)
                        bank_free[bk] = (sS, sS.v)
                        aev[pr] = sS.v
                    for pr in range(8):
                        bk = next_bank(pr % 2)
                        bank_wait(bk)
                        nc.tensor.wait_ge(sS.h, aev[pr])
                        nc.tensor.wait_ge(*yev[pr])
                        for u in range(2):
                            hh = pr * 2 + u
                            mm = nc.tensor.matmul(
                                pb[bk][:, u * 256:(u + 1) * 256],
                                pq[:, hh * 256 + 128:hh * 256 + 256],
                                Yb[:, hh * 256:(hh + 1) * 256],
                                start=True, stop=True)
                        sT.inc(mm, 1)
                        nc.vector.wait_ge(sT.h, sT.v)
                        VI(nc.vector.tensor_tensor(
                            Yb[:, pr * 512:(pr + 1) * 512],
                            Yb[:, pr * 512:(pr + 1) * 512],
                            pb[bk][:, 0:512], ALU.add), 1)
                        bank_free[bk] = (sV, sV.v)
                        yev[pr] = (sV.h, sV.v)
                nc.tensor.wait_ge(sS.h, sS.v)
                nc.tensor.wait_ge(sV.h, sV.v)
                for hg in range(4):
                    bk = next_bank(0)
                    bank_wait(bk)
                    for j in range(4):
                        hh = hg * 4 + j
                        mm = nc.tensor.matmul(
                            pb[bk][:, j * 128:(j + 1) * 128],
                            Yb[:, hh * 256 + 128:(hh + 1) * 256],
                            I_bf[:], start=True, stop=True)
                    sT.inc(mm, 1)
                    nc.scalar.wait_ge(sT.h, sT.v)
                    SI(nc.scalar.activation(
                        w2tn[:, hg * 512:(hg + 1) * 512], pb[bk][:, :],
                        AF.Copy, scale=-1.0), 1)
                    bank_free[bk] = (sS, sS.v)

            def seq_step(c, with_o):
                nc.tensor.wait_ge(sS.h, sS.v)
                nc.tensor.wait_ge(sV.h, sV.v)
                uev = [0] * 4
                for qq in range(4):
                    bk = next_bank(1)
                    bank_wait(bk)
                    for u in range(4):
                        hh = qq * 4 + u
                        hs = slice(hh * 128, (hh + 1) * 128)
                        mm = nc.tensor.matmul(
                            pb[bk][:, u * 128:(u + 1) * 128], w2tn[:, hs],
                            S_b[:, hs], start=True, stop=True)
                    sT.inc(mm, 1)
                    nc.vector.wait_ge(sT.h, sT.v)
                    VI(nc.vector.tensor_tensor(
                        U_all[:, qq * 512:(qq + 1) * 512]
                        .rearrange("p (q t) -> p q t", q=4),
                        Y3[:, qq * 4:(qq + 1) * 4, 0:128],
                        pb[bk][:, 0:512].rearrange("p (q t) -> p q t", q=4),
                        ALU.add), 1)
                    bank_free[bk] = (sV, sV.v)
                    uev[qq] = sV.v
                if with_o:
                    for qq in range(4):
                        bk = next_bank(0)
                        bank_wait(bk)
                        nc.tensor.wait_ge(sV.h, uev[qq])
                        for u in range(4):
                            hh = qq * 4 + u
                            hs = slice(hh * 128, (hh + 1) * 128)
                            nc.tensor.matmul(
                                pb[bk][:, u * 128:(u + 1) * 128], qbar[:, hs],
                                S_b[:, hs], start=True, stop=False)
                            mm = nc.tensor.matmul(
                                pb[bk][:, u * 128:(u + 1) * 128], Ptmat[:, hs],
                                U_all[:, hs], start=False, stop=True)
                        sT.inc(mm, 1)
                        nc.scalar.wait_ge(sT.h, sT.v)
                        SI(nc.scalar.copy(obuf[:, qq * 512:(qq + 1) * 512],
                                          pb[bk][:, 0:512]), 1)
                        bank_free[bk] = (sS, sS.v)
                for qq in range(4):
                    bk = next_bank(1)
                    bank_wait(bk)
                    nc.tensor.wait_ge(sV.h, uev[qq])
                    for u in range(4):
                        hh = qq * 4 + u
                        hs = slice(hh * 128, (hh + 1) * 128)
                        mm = nc.tensor.matmul(
                            pb[bk][:, u * 128:(u + 1) * 128], khat[:, hs],
                            U_all[:, hs], start=True, stop=True)
                    sT.inc(mm, 1)
                    nc.vector.wait_ge(sT.h, sT.v)
                    S_f3 = S_f[:, qq * 512:(qq + 1) * 512] \
                        .rearrange("p (q t) -> p q t", q=4)
                    VI(nc.vector.tensor_tensor(
                        S_f3, S_f3,
                        Gbe[:, qq * 4:(qq + 1) * 4].unsqueeze(2)
                        .broadcast_to((128, 4, 128)), ALU.mult), 1)
                    VI(nc.vector.tensor_tensor(
                        S_f[:, qq * 512:(qq + 1) * 512],
                        S_f[:, qq * 512:(qq + 1) * 512],
                        pb[bk][:, 0:512], ALU.add), 1)
                    VI(nc.vector.tensor_copy(
                        S_b[:, qq * 512:(qq + 1) * 512],
                        S_f[:, qq * 512:(qq + 1) * 512]), 1)
                    bank_free[bk] = (sV, sV.v)

            _sid2a, _ = nc.enter_named_scope("p2a", False)
            for c in range(CH):
                soft_sync()
                chunk_pre(c)
                nc.scalar.wait_ge(sV.h, sV.v)
                qP.start((op_sp[c, 0].transpose([1, 0, 2]), Y3[:, :, 0:128]))
                qP.start((op_sp[c, 1].transpose([1, 0, 2]), Y3[:, :, 128:256]))
                qP.start((op_sp[c, 2].transpose([1, 0, 2]),
                          w2tn[:].rearrange("p (h t) -> p h t", h=H)))
                qP.start((op_sp[c, 3].transpose([1, 0, 2]),
                          khat[:].rearrange("p (h t) -> p h t", h=H)))
                qP.start((op_sp[c, 4].transpose([1, 0, 2]),
                          Ptmat[:].rearrange("p (h t) -> p h t", h=H)))
                qP.start((op_sp[c, 5].transpose([1, 0, 2]),
                          qbar[:].rearrange("p (h t) -> p h t", h=H)))
                seq_step(c, with_o=False)
            nc.leave_named_scope("p2a", _sid2a, False)
            _sid2g, _ = nc.enter_named_scope("p2ag", False)
            soft_sync()
            qX.start((ag_s_in, S_f[:]))
            qX.wait_all(nc.gpsimd)
            sC.inc(nc.gpsimd.collective_compute(
                "AllGather", ALU.bypass, replica_groups=[list(range(NCORE))],
                ins=[ag_s_in.opt()], outs=[ag_s_out.opt()]), 1)
            nc.sync.wait_ge(sC.h, sC.v)
            nc.vector.wait_ge(sC.h, sC.v)
            qX.wait_all(nc.vector)
            VI(nc.vector.memset(S_f[:], 0.0), 1)
            for j in range(4):
                qX.start((rowb[:], ag_s_out[2 * j]))
                qX.wait_all(nc.vector)
                VI(nc.vector.scalar_tensor_tensor(
                    S_f[:], rowb[:], sel4[:, j:j + 1], S_f[:],
                    ALU.mult, ALU.add), 1)
                nc.sync.wait_ge(sV.h, sV.v)
            VI(nc.vector.tensor_copy(S_b[:], S_f[:]), 1)

            nc.leave_named_scope("p2ag", _sid2g, False)
            _sid2b, _ = nc.enter_named_scope("p2b", False)
            for c in range(CH):
                soft_sync()
                qX.start((Y3[:, :, 0:128], op_sp[c, 0].transpose([1, 0, 2])))
                qX.start((Y3[:, :, 128:256], op_sp[c, 1].transpose([1, 0, 2])))
                qX.start((w2tn[:], op_sp[c, 2].transpose([1, 0, 2])))
                qX.start((khat[:], op_sp[c, 3].transpose([1, 0, 2])))
                qX.start((Ptmat[:], op_sp[c, 4].transpose([1, 0, 2])))
                qX.start((qbar[:], op_sp[c, 5].transpose([1, 0, 2])))
                qX.start((zc[:], z_sp[c]))
                qX.start((rowb[:],
                          gcum_d[0:16, c * 128:(c + 1) * 128]
                          .unsqueeze(0).broadcast_to((128, 16, 128))))
                qX.wait_all(nc.scalar)
                SI(nc.scalar.activation(
                    Gbe[:].unsqueeze(2), rowb3[:, :, 127:128], AF.Exp), 1)
                qX.wait_all(nc.tensor)
                qX.wait_all(nc.vector)
                seq_step(c, with_o=True)
                ob3 = obuf[:].rearrange("p (h t) -> p h t", h=H)
                on3 = onb[:].rearrange("p (h t) -> p h t", h=H)
                nc.vector.wait_ge(sS.h, sS.v)
                VI(nc.vector.tensor_tensor(obuf[:], obuf[:], zc[:], ALU.mult), 1)
                VI(nc.vector.tensor_tensor(art[:], obuf[:], obuf[:], ALU.mult), 1)
                VI(nc.vector.tensor_reduce(
                    sq16[:], art[:].rearrange("p (h t) -> p h t", h=H),
                    AX.X, ALU.add), 1)
                VI(nc.vector.tensor_scalar_mul(sq16[:], sq16[:], 1.0 / 128.0), 1)
                nc.scalar.wait_ge(sV.h, sV.v)
                SI(nc.scalar.activation(sq16[:], sq16[:], AF.Sqrt,
                                        bias=ep6[:]), 1)
                nc.vector.wait_ge(sS.h, sS.v)
                VI(nc.vector.reciprocal(sq16[:], sq16[:]), 1)
                VI(nc.vector.tensor_tensor(on3, ob3, bc16(sq16[:]), ALU.mult), 1)
                VI(nc.vector.tensor_tensor(
                    on3, on3, nwb[:].unsqueeze(1).broadcast_to((128, 16, 128)),
                    ALU.mult), 1)
                nc.tensor.wait_ge(sV.h, sV.v)
                for hg in range(4):
                    bk = next_bank(0)
                    bank_wait(bk)
                    for j in range(4):
                        hh = hg * 4 + j
                        mm = nc.tensor.matmul(
                            pb[bk][:, j * 128:(j + 1) * 128],
                            onb[:, hh * 128:(hh + 1) * 128],
                            I_bf[:], start=True, stop=True)
                    sT.inc(mm, 1)
                    nc.vector.wait_ge(sT.h, sT.v)
                    qX.wait_all(nc.vector)
                    VI(nc.vector.tensor_copy(stg2[:], pb[bk][:, :]), 1)
                    bank_free[bk] = (sV, sV.v)
                    nc.sync.wait_ge(sV.h, sV.v)
                    qX.start((ocm_sp[hg * 4:(hg + 1) * 4, :, c * 128:(c + 1) * 128]
                              .transpose([1, 0, 2]), stg2[:]))
            nc.leave_named_scope("p2b", _sid2b, False)
            drain_dmas()


        def phase3():
          with contextlib.ExitStack() as ps:
            def pa(n, s, d):
                return ps.enter_context(nc.sbuf_tensor(n, s, d))
            wo = pa("wo", [128, 16 * 2048], BF16)
            oc_l = pa("oc_l", [128, 2048], BF16)
            xres = pa("xres", [128, 2048], F32)
            x2t = pa("x2t", [128, 2048], F32)
            for kk in range(NKD):
                qW.start((wo[:, kk * 2048:(kk + 1) * 2048], wout_d[kk]))
            qW.wait_all(nc.tensor)
            for tt in range(16):
                qX.wait_all(nc.sync)
                tko = qX.start((oc_l[:], ocm_sp[:, :, tt * 128:(tt + 1) * 128]
                                .transpose([1, 0, 2])))
                txr = qX.start((xres[:], x_d[3 + tt * 128:3 + (tt + 1) * 128, :]))
                qX.wait(nc.tensor, tko)
                qX.wait(nc.vector, txr)
                for nb in range(4):
                    bk = next_bank(0)
                    bank_wait(bk)
                    for kk in range(NKD):
                        mm = nc.tensor.matmul(
                            pb[bk][:, 0:512],
                            oc_l[:, kk * 128:(kk + 1) * 128],
                            wo[:, kk * 2048 + nb * 512:kk * 2048 + nb * 512 + 512],
                            start=(kk == 0), stop=(kk == NKD - 1))
                    sT.inc(mm, 1)
                    nc.vector.wait_ge(sT.h, sT.v)
                    cs = slice(nb * 512, (nb + 1) * 512)
                    VI(nc.vector.tensor_tensor(
                        x2t[:, cs], pb[bk][:, 0:512], gab[:, cs], ALU.mult), 1)
                    bank_free[bk] = (sV, sV.v)
                    VI(nc.vector.tensor_tensor(
                        x2t[:, cs], x2t[:, cs], xres[:, cs], ALU.add), 1)
                nc.sync.wait_ge(sV.h, sV.v)
                qX.start((x2_sp[tt], x2t[:]))
            drain_dmas()

        def phase4():
          with contextlib.ExitStack() as ps:
            def pa(n, s, d):
                return ps.enter_context(nc.sbuf_tensor(n, s, d))
            ep5 = pa("ep5", [128, 1], F32)
            GI(nc.gpsimd.memset(ep5[:], 1e-5), 1)
            nc.vector.wait_ge(sG.h, sG.v)
            nc.scalar.wait_ge(sG.h, sG.v)
            N1 = float(1.0 / np.sqrt(8192.0))
            rs2_all = pa("rs2_all", [128, 16], F32)
            x1_es = contextlib.ExitStack()
            xq1T = x1_es.enter_context(
                nc.sbuf_tensor("xq1T", [128, NKD * 2048], BF16))
            xq1T3 = xq1T[:].rearrange("p (k c) -> p k c", k=NKD)

            _sc2, _ = nc.enter_named_scope("c2", False)
            c2_es = contextlib.ExitStack()
            def ca(n, s, d):
                return c2_es.enter_context(nc.sbuf_tensor(n, s, d))
            x2t = ca("x2l", [128, 2 * 2048], F32)
            sqt = ca("sqt4", [128, 2048], F32)
            stat = ca("stat4", [128, 8], F32)
            h2a = ca("h2a", [128, 2048], F32)
            h2b = ca("h2b", [128, 2 * 2048], BF16)
            i32t = ca("i32t", [128, 2048], I32)
            tr_done4 = {}
            ln_v4 = {}
            rs1tok = {}
            for tt in range(16):
                xbuf = x2t[:, (tt % 2) * 2048:(tt % 2) * 2048 + 2048]
                if tt >= 2:
                    nc.sync.wait_ge(sV.h, ln_v4[tt - 2])
                tx = qX.start((xbuf, x2_sp[tt]))
                qX.wait(nc.vector, tx)
                qX.wait(nc.scalar, tx)
                VI(nc.vector.tensor_reduce(stat[:, 0:1], xbuf, AX.X, ALU.add), 1)
                SI(nc.scalar.activation(sqt[:], xbuf, AF.Square,
                                        accum_out=stat[:, 2:3]), 1)
                nc.vector.wait_ge(sS.h, sS.v)
                VI(nc.vector.tensor_scalar_mul(stat[:, 1:2], stat[:, 0:1],
                                               1.0 / 2048.0), 1)
                VI(nc.vector.tensor_scalar_mul(stat[:, 3:4], stat[:, 2:3],
                                               1.0 / 2048.0), 1)
                VI(nc.vector.tensor_tensor(stat[:, 4:5], stat[:, 1:2],
                                           stat[:, 1:2], ALU.mult), 1)
                VI(nc.vector.tensor_tensor(stat[:, 3:4], stat[:, 3:4],
                                           stat[:, 4:5], ALU.subtract), 1)
                nc.scalar.wait_ge(sV.h, sV.v)
                SI(nc.scalar.activation(stat[:, 5:6], stat[:, 3:4], AF.Sqrt,
                                        bias=ep5[:]), 1)
                nc.vector.wait_ge(sS.h, sS.v)
                VI(nc.vector.reciprocal(stat[:, 6:7], stat[:, 5:6]), 1)
                VI(nc.vector.tensor_scalar(h2a[:], xbuf, stat[:, 1:2],
                                           stat[:, 6:7], ALU.subtract, ALU.mult), 1)
                VI(nc.vector.tensor_tensor(h2a[:], h2a[:], cb1b[:], ALU.mult), 1)
                VI(nc.vector.tensor_tensor(h2a[:], h2a[:], sbb[:], ALU.add), 1)
                VI(nc.vector.tensor_reduce(stat[:, 0:1], h2a[:], AX.X, ALU.max,
                                           apply_absolute_value=True), 1)
                VI(nc.vector.tensor_scalar(stat[:, 0:1], stat[:, 0:1], 1e-5,
                                           None, ALU.max), 1)
                VI(nc.vector.reciprocal(stat[:, 1:2], stat[:, 0:1]), 1)
                VI(nc.vector.tensor_scalar_mul(stat[:, 2:3], stat[:, 1:2],
                                               128.0), 1)
                VI(nc.vector.tensor_scalar_mul(stat[:, 3:4], stat[:, 0:1],
                                               1.0 / 128.0), 1)
                VI(nc.vector.tensor_copy(sc1_r[:, tt:tt + 1], stat[:, 3:4]), 1)
                hb = h2b[:, (tt % 2) * 2048:(tt % 2) * 2048 + 2048]
                if tt - 2 in tr_done4:
                    nc.vector.wait_ge(sT.h, tr_done4[tt - 2])
                VI(nc.vector.tensor_scalar(h2a[:], h2a[:], stat[:, 2:3],
                                           CLIP_HI, ALU.mult, ALU.min), 1)
                VI(nc.vector.tensor_scalar(h2a[:], h2a[:], CLIP_LO, None,
                                           ALU.max), 1)
                VI(nc.vector.tensor_scalar(i32t[:], h2a[:], MAGIC, MAGIC,
                                           ALU.add, ALU.subtract), 1)
                VI(nc.vector.tensor_copy(hb, i32t[:]), 1)
                nc.tensor.wait_ge(sV.h, sV.v)
                for g4 in range(4):
                    bk = next_bank(0)
                    bank_wait(bk)
                    for j in range(4):
                        kk = g4 * 4 + j
                        mm = nc.tensor.matmul(
                            pb[bk][:, j * 128:(j + 1) * 128],
                            hb[:, kk * 128:(kk + 1) * 128],
                            I_bf[:], start=True, stop=True)
                    sT.inc(mm, 1)
                    nc.scalar.wait_ge(sT.h, sT.v)
                    SI(nc.scalar.copy(
                        xq1T3[:, g4 * 4:(g4 + 1) * 4, tt * 128:(tt + 1) * 128],
                        pb[bk][:, :].rearrange("p (j c) -> p j c", j=4)), 1)
                    bank_free[bk] = (sS, sS.v)
                tr_done4[tt] = sT.v
                ln_v4[tt] = sV.v
            drain_dmas()
            c2_es.close()

            nc.leave_named_scope("c2", _sc2, False)
            _sc3, _ = nc.enter_named_scope("c3", False)
            c3_es = contextlib.ExitStack()
            def c3a(n, s, d):
                return c3_es.enter_context(nc.sbuf_tensor(n, s, d))
            wgt = c3a("wgt", [128, 2 * 2048], BF16)
            wut = c3a("wut", [128, 2 * 2048], BF16)
            rsbg = c3a("rsbg", [128, 2048], F32)
            rsbu = c3a("rsbu", [128, 2048], F32)
            gsil = c3a("gsil", [128, 2048], BF16)
            upt = c3a("upt", [128, 2048], BF16)
            intb = c3a("intb", [128, 2048], BF16)
            rs1b = c3a("rs1b", [128, 16], BF16)
            rrow16 = c3a("rrow16", [16, 128], BF16)
            VI(nc.vector.tensor_copy(rs1b[:], sc1_r[:]), 1)
            nc.tensor.wait_ge(sV.h, sV.v)
            bk = next_bank(0)
            bank_wait(bk)
            mm = nc.tensor.matmul(pb[bk][0:16, 0:128], rs1b[:], I_bf[:],
                                  start=True, stop=True)
            sT.inc(mm, 1)
            nc.vector.wait_ge(sT.h, sT.v)
            VI(nc.vector.tensor_copy(rrow16[:], pb[bk][0:16, 0:128]), 1)
            bank_free[bk] = (sV, sV.v)
            nc.tensor.wait_ge(sV.h, sV.v)
            for tt in range(16):
                bk = next_bank(0)
                bank_wait(bk)
                mm = nc.tensor.matmul(
                    pb[bk][:, 0:128], sel16b[:, tt * 128:(tt + 1) * 128],
                    rrow16[:], start=True, stop=True)
                sT.inc(mm, 1)
                nc.vector.wait_ge(sT.h, sT.v)
                VI(nc.vector.tensor_copy(rsbg[:, tt * 128:(tt + 1) * 128],
                                         pb[bk][:, 0:128]), 1)
                bank_free[bk] = (sV, sV.v)
            VI(nc.vector.tensor_scalar(rsbu[:], rsbg[:], sbars[:, 1:2],
                                       None, ALU.mult), 1)
            VI(nc.vector.tensor_scalar(rsbg[:], rsbg[:], sbars[:, 0:1],
                                       None, ALU.mult), 1)
            c3wt = [None, None]
            c3mm = {}
            sp_tok = [None]

            def c3_loads(m):
                toks = []
                for wi, soff in ((0, 0), (1, SLAB)):
                    buf = wgt if wi == 0 else wut
                    dst = buf[:, (m % 2) * 2048:(m % 2) * 2048 + 2048]
                    dst4 = dst.rearrange("p (r kl c) -> p r kl c", r=8, kl=2)
                    for kl in range(2):
                        src = ag_w_out[:, soff + m * 32768 + kl * 16384:
                                       soff + m * 32768 + (kl + 1) * 16384]
                        toks.append(qX.start(
                            (dst4[:, :, kl, :],
                             src.rearrange("r (pp c) -> pp r c", pp=128))))
                c3wt[m % 2] = toks

            c3_loads(0)
            for m in range(64):
                if m + 1 < 64:
                    if m >= 1:
                        nc.sync.wait_ge(sT.h, c3mm[m - 1])
                    c3_loads(m + 1)
                for tok in c3wt[m % 2]:
                    qX.wait(nc.tensor, tok)
                bkv = []
                for wi in range(2):
                    buf = wgt if wi == 0 else wut
                    for nb in range(4):
                        bk = wi * 4 + nb
                        bank_wait(bk)
                        for kk in range(NKD):
                            mm = nc.tensor.matmul(
                                pb[bk][:, 0:512],
                                buf[:, (m % 2) * 2048 + kk * 128:
                                    (m % 2) * 2048 + (kk + 1) * 128],
                                xq1T3[:, kk, nb * 512:(nb + 1) * 512],
                                start=(kk == 0), stop=(kk == NKD - 1))
                        sT.inc(mm, 1)
                        bkv.append(sT.v)
                c3mm[m] = sT.v
                for nb in range(4):
                    cs = slice(nb * 512, (nb + 1) * 512)
                    nc.vector.wait_ge(sT.h, bkv[nb])
                    VI(nc.vector.tensor_tensor(
                        gsil[:, cs], pb[nb][:, 0:512], rsbg[:, cs], ALU.mult), 1)
                    bank_free[nb] = (sV, sV.v)
                    nc.vector.wait_ge(sT.h, bkv[4 + nb])
                    VI(nc.vector.tensor_tensor(
                        upt[:, cs], pb[4 + nb][:, 0:512], rsbu[:, cs],
                        ALU.mult), 1)
                    bank_free[4 + nb] = (sV, sV.v)
                nc.scalar.wait_ge(sV.h, sV.v)
                SI(nc.scalar.activation(gsil[:], gsil[:], AF.Silu), 1)
                nc.vector.wait_ge(sS.h, sS.v)
                if sp_tok[0] is not None:
                    qX.wait(nc.vector, sp_tok[0])
                VI(nc.vector.tensor_tensor(intb[:], gsil[:], upt[:], ALU.mult), 1)
                nc.sync.wait_ge(sV.h, sV.v)
                sp_tok[0] = qX.start((inter_sp[:, m].transpose([1, 0, 2]),
                                      intb[:]))
            drain_dmas()
            c3_es.close()
            x1_es.close()

            nc.leave_named_scope("c3", _sc3, False)
            _sc4a, _ = nc.enter_named_scope("c4a", False)
            c4_es = contextlib.ExitStack()
            def c4a(n, s, d):
                return c4_es.enter_context(nc.sbuf_tensor(n, s, d))
            iz = c4a("iz", [128, 64 * 256], BF16)
            z1 = c4a("z1", [128, 64 * 256], BF16)
            fw = iz
            s2a = c4a("s2a", [128, 2 * 512], BF16)
            s2b = c4a("s2b", [128, 2 * 512], BF16)
            mxj = c4a("mxj", [128, 256], F32)
            mxjb = c4a("mxjb", [128, 256], BF16)
            mcol = c4a("mcol", [128, 2], F32)
            mcolb = c4a("mcolb", [128, 2], BF16)
            mrow = c4a("mrow", [16, 128], BF16)
            s2r = c4a("s2r", [128, 256], F32)
            xq2b = z1
            i32f = c4a("i32f", [128, 16 * 256], I32)
            fw32 = c4a("fw32", [128, 16 * 256], F32)
            VI(nc.vector.memset(mrow[:], 0.0), 1)
            for blk in range(8):
                qX.wait_all(nc.sync)
                ti = qX.start((iz[:], inter_sp[blk].transpose([1, 0, 2])))
                qX.wait(nc.tensor, ti)
                iz3 = iz[:].rearrange("p (i t) -> p i t", i=64)
                z13 = z1[:].rearrange("p (i t) -> p i t", i=64)
                fw3 = fw[:].rearrange("p (i t) -> p i t", i=64)
                z1g = z1[:].rearrange("p (g i t) -> p g i t", g=128, i=64)
                for ig in range(32):
                    bk = next_bank(0)
                    bank_wait(bk)
                    for j in range(2):
                        mm = nc.tensor.matmul(
                            pb[bk][:, j * 256:(j + 1) * 256],
                            H128[:], iz3[:, ig * 2 + j, :],
                            start=True, stop=True)
                    sT.inc(mm, 1)
                    nc.scalar.wait_ge(sT.h, sT.v)
                    SI(nc.scalar.copy(
                        z1g[:, :, ig * 2:ig * 2 + 2, :].transpose([0, 2, 1, 3]),
                        pb[bk][:, :].rearrange("p (i g t) -> p i g t",
                                               i=2, g=128)), 1)
                    bank_free[bk] = (sS, sS.v)
                nc.tensor.wait_ge(sS.h, sS.v)
                st1_ev = {}
                st2_ev = {}
                for step in range(34):
                    g1, g2, g3 = step, step - 1, step - 2
                    if g1 < 32:
                        sl = g1 % 2
                        bk = next_bank(0)
                        bank_wait(bk)
                        for j in range(4):
                            g = g1 * 4 + j
                            mm = nc.tensor.matmul(
                                pb[bk][:, j * 128:(j + 1) * 128],
                                z1[:, g * 128:(g + 1) * 128], I_bf[:],
                                start=True, stop=True)
                        sT.inc(mm, 1)
                        nc.scalar.wait_ge(sT.h, sT.v)
                        SI(nc.scalar.copy(s2a[:, sl * 512:(sl + 1) * 512],
                                          pb[bk][:, :]), 1)
                        st1_ev[g1] = sS.v
                        bank_free[bk] = (sS, sS.v)
                    if 0 <= g2 < 32:
                        sl = g2 % 2
                        bk2 = next_bank(1)
                        bank_wait(bk2)
                        nc.tensor.wait_ge(sS.h, st1_ev[g2])
                        for j in range(4):
                            mm = nc.tensor.matmul(
                                pb[bk2][:, j * 128:(j + 1) * 128], HXs[:],
                                s2a[:, sl * 512 + j * 128:
                                    sl * 512 + (j + 1) * 128],
                                start=True, stop=True)
                        sT.inc(mm, 1)
                        nc.scalar.wait_ge(sT.h, sT.v)
                        SI(nc.scalar.copy(s2b[:, sl * 512:(sl + 1) * 512],
                                          pb[bk2][:, :]), 1)
                        st2_ev[g2] = sS.v
                        bank_free[bk2] = (sS, sS.v)
                    if 0 <= g3:
                        sl = g3 % 2
                        bk3 = next_bank(0)
                        bank_wait(bk3)
                        nc.tensor.wait_ge(sS.h, st2_ev[g3])
                        for j in range(4):
                            mm = nc.tensor.matmul(
                                pb[bk3][:, j * 128:(j + 1) * 128],
                                s2b[:, sl * 512 + j * 128:
                                    sl * 512 + (j + 1) * 128], I_bf[:],
                                start=True, stop=True)
                        sT.inc(mm, 1)
                        nc.vector.wait_ge(sT.h, sT.v)
                        for j in range(4):
                            g = g3 * 4 + j
                            VI(nc.vector.tensor_copy(
                                fw3[:, :, 2 * g:2 * g + 2],
                                pb[bk3][:, j * 128:(j + 1) * 128]
                                .rearrange("p (i t) -> p i t", i=64)), 1)
                        bank_free[bk3] = (sV, sV.v)
                VI(nc.vector.tensor_reduce(
                    mxj[:], fw3.transpose([0, 2, 1]), AX.X, ALU.max,
                    apply_absolute_value=True), 1)
                VI(nc.vector.tensor_copy(mxjb[:], mxj[:]), 1)
                nc.tensor.wait_ge(sV.h, sV.v)
                for j in range(2):
                    bk = next_bank(0)
                    bank_wait(bk)
                    mm = nc.tensor.matmul(
                        pb[bk][:, 0:128], mxjb[:, j * 128:(j + 1) * 128],
                        I_bf[:], start=True, stop=True)
                    sT.inc(mm, 1)
                    nc.vector.wait_ge(sT.h, sT.v)
                    VI(nc.vector.tensor_reduce(
                        mcol[:, j:j + 1], pb[bk][:, 0:128], AX.X, ALU.max), 1)
                    bank_free[bk] = (sV, sV.v)
                VI(nc.vector.tensor_scalar(mcol[:], mcol[:], N1, 1e-5,
                                           ALU.mult, ALU.max), 1)
                VI(nc.vector.reciprocal(mcol[:], mcol[:]), 1)
                VI(nc.vector.tensor_scalar_mul(mcol[:], mcol[:], 128.0 * N1), 1)
                VI(nc.vector.tensor_copy(mcolb[:], mcol[:]), 1)
                VI(nc.vector.tensor_copy(mcol[:], mcolb[:]), 1)
                VI(nc.vector.reciprocal(s2r[:, 0:2], mcol[:]), 1)
                VI(nc.vector.tensor_scalar(s2r[:, 0:2], s2r[:, 0:2],
                                           sbars[:, 2:3], N1,
                                           ALU.mult, ALU.mult), 1)
                VI(nc.vector.tensor_copy(rs2_all[:, blk * 2:blk * 2 + 2],
                                         s2r[:, 0:2]), 1)
                nc.tensor.wait_ge(sV.h, sV.v)
                bk = next_bank(0)
                bank_wait(bk)
                mm = nc.tensor.matmul(pb[bk][0:2, 0:128], mcolb[:], I_bf[:],
                                      start=True, stop=True)
                sT.inc(mm, 1)
                nc.vector.wait_ge(sT.h, sT.v)
                VI(nc.vector.tensor_copy(mrow[0:2, :], pb[bk][0:2, 0:128]), 1)
                bank_free[bk] = (sV, sV.v)
                nc.tensor.wait_ge(sV.h, sV.v)
                bk = next_bank(0)
                bank_wait(bk)
                nc.tensor.matmul(pb[bk][:, 0:128],
                                 sel16b[:, 0:128], mrow[:, 0:128],
                                 start=True, stop=True)
                mm = nc.tensor.matmul(pb[bk][:, 128:256],
                                      sel16b[:, 128:256], mrow[:, 0:128],
                                      start=True, stop=True)
                sT.inc(mm, 1)
                nc.vector.wait_ge(sT.h, sT.v)
                VI(nc.vector.tensor_copy(s2r[:], pb[bk][:, 0:256]), 1)
                bank_free[bk] = (sV, sV.v)
                for qc_ in range(4):
                    i0 = qc_ * 16
                    VI(nc.vector.tensor_tensor(
                        fw32[:].rearrange("p (i t) -> p i t", i=16),
                        fw3[:, i0:i0 + 16, :],
                        s2r[:].unsqueeze(1).broadcast_to((128, 16, 256)),
                        ALU.mult), 1)
                    VI(nc.vector.tensor_scalar(fw32[:], fw32[:], CLIP_LO,
                                               CLIP_HI, ALU.max, ALU.min), 1)
                    VI(nc.vector.tensor_scalar(i32f[:], fw32[:], MAGIC, MAGIC,
                                               ALU.add, ALU.subtract), 1)
                    VI(nc.vector.tensor_copy(
                        xq2b[:, i0 * 256:(i0 + 16) * 256], i32f[:]), 1)
                nc.sync.wait_ge(sV.h, sV.v)
                for sb_ in range(2):
                    for iq in range(4):
                        qX.start((xq2_sp[2 * blk + sb_, iq * 16:(iq + 1) * 16]
                                  .transpose([1, 0, 2]),
                                  xq2b[:].rearrange("p (i t) -> p i t", i=64)
                                  [:, iq * 16:(iq + 1) * 16,
                                   sb_ * 128:(sb_ + 1) * 128]))
            drain_dmas()
            c4_es.close()

            nc.leave_named_scope("c4a", _sc4a, False)
            _sc4b, _ = nc.enter_named_scope("c4b", False)
            c5_es = contextlib.ExitStack()
            def c5a(n, s, d):
                return c5_es.enter_context(nc.sbuf_tensor(n, s, d))
            wdn = c5a("wdn", [128, 64 * 512], BF16)
            xqm = c5a("xqm", [128, 2 * 64 * 128], BF16)
            x2s = c5a("x2s", [128, 2 * 512], F32)
            yt = c5a("yt", [128, 2 * 512], F32)
            xq_tok = [None, None]
            x2_tok = [None, None]
            st_tok = [None, None]
            mt_mm = {}
            mt_vv = {}

            def c4b_loads(mt, nb):
                sl = mt % 2
                if mt >= 2:
                    nc.sync.wait_ge(sT.h, mt_mm[mt - 2])
                    nc.sync.wait_ge(sV.h, mt_vv[mt - 2])
                xqv3 = xqm[:, sl * 8192:sl * 8192 + 8192] \
                    .rearrange("p (i t) -> p i t", i=64)
                xq_tok[sl] = [qX.start(
                    (xqv3[:, iq * 16:(iq + 1) * 16, :],
                     xq2_sp[mt, iq * 16:(iq + 1) * 16].transpose([1, 0, 2])))
                    for iq in range(4)]
                x2_tok[sl] = qX.start(
                    (x2s[:, sl * 512:(sl + 1) * 512],
                     x2_sp[mt][:, nb * 512:(nb + 1) * 512]))

            for nb in range(4):
                nc.sync.wait_ge(sT.h, sT.v)
                wdt = []
                wdn4 = wdn[:].rearrange("p (k m c) -> p k m c", k=64, m=4)
                for r in range(8):
                    for mloc in range(4):
                        off = 2 * SLAB + (4 * nb + mloc) * 8 * 16384
                        srcw = ag_w_out[r, off:off + 8 * 16384]
                        wdt.append(qX.start(
                            (wdn4[:, r * 8:(r + 1) * 8, mloc, :],
                             srcw.rearrange("(kl pp c) -> pp kl c",
                                            kl=8, pp=128))))
                mt_mm.clear()
                mt_vv.clear()
                c4b_loads(0, nb)
                for t_ in wdt:
                    qX.wait(nc.tensor, t_)
                for mt in range(16):
                    sl = mt % 2
                    if mt + 1 < 16:
                        c4b_loads(mt + 1, nb)
                    for t_ in xq_tok[sl]:
                        qX.wait(nc.tensor, t_)
                    bk = next_bank(0)
                    bank_wait(bk)
                    for kk in range(64):
                        mm = nc.tensor.matmul(
                            pb[bk][:, 0:512],
                            xqm[:, sl * 8192 + kk * 128:
                                sl * 8192 + (kk + 1) * 128],
                            wdn[:, kk * 512:(kk + 1) * 512],
                            start=(kk == 0), stop=(kk == 63))
                    sT.inc(mm, 1)
                    mt_mm[mt] = sT.v
                    qX.wait(nc.vector, x2_tok[sl])
                    if st_tok[sl] is not None:
                        qX.wait(nc.vector, st_tok[sl])
                    nc.vector.wait_ge(sT.h, sT.v)
                    ys = yt[:, sl * 512:(sl + 1) * 512]
                    VI(nc.vector.tensor_scalar(
                        ys, pb[bk][:, 0:512], rs2_all[:, mt:mt + 1],
                        None, ALU.mult), 1)
                    bank_free[bk] = (sV, sV.v)
                    VI(nc.vector.tensor_tensor(
                        ys, ys, gbb[:, nb * 512:(nb + 1) * 512],
                        ALU.mult), 1)
                    VI(nc.vector.tensor_tensor(
                        ys, ys, x2s[:, sl * 512:(sl + 1) * 512], ALU.add), 1)
                    mt_vv[mt] = sV.v
                    nc.sync.wait_ge(sV.h, sV.v)
                    st_tok[sl] = qX.start(
                        (y_d[mt * 128:(mt + 1) * 128,
                             nb * 512:(nb + 1) * 512], ys))
            nc.leave_named_scope("c4b", _sc4b, False)
            drain_dmas()
            c5_es.close()

        def _scoped(name, fn):
            sid, _ = nc.enter_named_scope(name, False)
            fn()
            nc.leave_named_scope(name, sid, False)
        if 0 in phases:
            _scoped("p0", phase0)
        if 1 in phases:
            _scoped("p1", phase1)
        if 2 in phases:
            _scoped("p2", phase2)
        if 3 in phases:
            _scoped("p3", phase3)
        if 4 in phases:
            _scoped("p4", phase4)
        nc.all_engine_barrier()
    return nc


def kernel(**inputs):
    nc = build()
    maps = _host_inputs(inputs)
    res = run_bass_kernel_spmd(nc, maps, core_ids=list(range(NCORE)))
    out = np.empty((4, 4096, 2048), dtype=np.float32)
    for c in range(NCORE):
        b, s = c // 2, c % 2
        out[b, s * TOK:(s + 1) * TOK] = res.results[c]["y"]
    return out


# version 05
# speedup vs baseline: 1.1600x; 1.0010x over previous
import contextlib
import numpy as np
import concourse.bass as bass
import concourse.mybir as mybir
from concourse.bass_utils import run_bass_kernel_spmd

F32 = mybir.dt.float32
BF16 = mybir.dt.bfloat16
FP16 = mybir.dt.float16
I32 = mybir.dt.int32
AF = mybir.ActivationFunctionType
ALU = mybir.AluOpType
AX = mybir.AxisListType

MAGIC = 12582912.0
NCORE = 8
TOK = 2048
D = 2048
NKD = 16
H = 16
DK = DV = 128
CH = 16
TOKH = TOK + 3
HT = 2052
FFI = 8192
CLIP_LO = -128.49609375
CLIP_HI = 127.49609375
SLAB = 64 * 2 * 128 * 128


class SemC:
    def __init__(self, h):
        self.h = h
        self.v = 0

    def inc(self, instr, n=1):
        instr.then_inc(self.h, n)
        self.v += n
        return self.v


def _hadamard(n):
    h = np.array([[1.0]], dtype=np.float32)
    while h.shape[0] < n:
        h = np.block([[h, h], [h, -h]])
    return h


def _consts():
    i128 = np.eye(128, dtype=np.float32)
    h128 = _hadamard(128)
    h64 = _hadamard(64)
    hx = np.kron(h64, np.eye(2)).astype(np.float32)
    sel16 = np.zeros((16, 2048), dtype=np.float32)
    for hh in range(16):
        sel16[hh, hh * 128:(hh + 1) * 128] = 1.0
    tril_s = np.tril(np.ones((128, 128), dtype=np.float32), -1)
    triu_s = np.ascontiguousarray(tril_s.T)
    triu_i = np.triu(np.ones((128, 128), dtype=np.float32), 0)
    return {
        "c_I": i128, "c_H128": h128, "c_HX": hx,
        "c_mL": tril_s, "c_oL": (1.0 - tril_s) * 88.0,
        "c_mU": triu_s, "c_oU": (1.0 - triu_s) * (-88.0),
        "c_mUi": triu_i, "c_oUi": (1.0 - triu_i) * (-88.0),
        "c_sel16": sel16,
    }


def _host_inputs(inputs):
    x = np.ascontiguousarray(np.asarray(inputs["L0_attn_hidden_states"], dtype=np.float32))
    Wqkv = np.asarray(inputs["L0_attn_W_qkv"], dtype=np.float32)
    convw = np.asarray(inputs["L0_attn_conv_w"], dtype=np.float32)
    Wz = np.asarray(inputs["L0_attn_W_z"], dtype=np.float32)
    Wb = np.asarray(inputs["L0_attn_W_b"], dtype=np.float32)
    Wa = np.asarray(inputs["L0_attn_W_a"], dtype=np.float32)
    dtb = np.asarray(inputs["L0_attn_dt_bias"], dtype=np.float32)
    Alog = np.asarray(inputs["L0_attn_A_log"], dtype=np.float32)
    nw = np.asarray(inputs["L0_attn_norm_weight"], dtype=np.float32)
    Wout = np.asarray(inputs["L0_attn_W_out"], dtype=np.float32)
    wg = np.asarray(inputs["L0_ffn_w_gate"], dtype=np.float32)
    wu = np.asarray(inputs["L0_ffn_w_up"], dtype=np.float32)
    wd = np.asarray(inputs["L0_ffn_w_down"], dtype=np.float32)
    cc = np.asarray(inputs["L0_c"], dtype=np.float32)
    wmod = np.asarray(inputs["L0_wmod"], dtype=np.float32)

    def img(W, nm):
        nk = W.shape[0] // 128
        t = W.reshape(nk, 128, nm, 128)
        return np.ascontiguousarray(t.transpose(2, 1, 0, 3).reshape(nm, 128, nk * 128))

    wqkv_img = img(Wqkv, 48)
    wz_img = img(Wz, 16)
    Wba = np.concatenate([Wb, Wa], axis=1)
    wba_img = np.ascontiguousarray(
        Wba.reshape(16, 128, 32).transpose(1, 0, 2).reshape(128, 512))
    wout_img = np.ascontiguousarray(Wout.reshape(16, 128, 2048))
    convw_t = np.ascontiguousarray(convw.reshape(48, 128, 4))
    cT = np.ascontiguousarray(cc.T.reshape(16, 128, 4))
    wgT = np.ascontiguousarray(wg.T)
    wuT = np.ascontiguousarray(wu.T)
    wdT = np.ascontiguousarray(wd.T)

    def slab(WT, rows0, nkl, nm):
        s = WT[rows0:rows0 + nkl * 128]
        s = s.reshape(nkl, 128, nm, 128)
        return np.ascontiguousarray(s.transpose(2, 0, 1, 3))

    consts = _consts()
    maps = []
    for c in range(NCORE):
        b, s = c // 2, c % 2
        xs = np.zeros((TOKH, D), dtype=np.float32)
        t0 = s * TOK
        if t0 >= 3:
            xs[0:3] = x[b, t0 - 3:t0]
        xs[3:] = x[b, t0:t0 + TOK]
        halo = np.full((128, 1), 1.0 if s == 1 else 0.0, dtype=np.float32)
        sel = np.zeros((128, 4), dtype=np.float32)
        if s == 1:
            sel[:, (c - 1) // 2] = 1.0
        bsel = np.zeros((128, 4), dtype=np.float32)
        bsel[:, b] = 1.0
        m = {
            "x": xs, "halo_mask": halo, "sel4": sel, "bsel": bsel,
            "wqkv_img": wqkv_img, "wz_img": wz_img, "wba_img": wba_img,
            "wout_img": wout_img, "convw_t": convw_t, "cT": cT,
            "wmod_t": np.ascontiguousarray(
                wmod[:, c * 1536:(c + 1) * 1536].reshape(16, 128, 1536)),
            "dtb_col": dtb.reshape(16, 1).copy(),
            "alog_col": Alog.reshape(16, 1).copy(),
            "nw_row": nw.reshape(1, 128).copy(),
            "wg_slab": slab(wgT, c * 256, 2, 64),
            "wu_slab": slab(wuT, c * 256, 2, 64),
            "wd_slab": slab(wdT, c * 1024, 8, 16),
        }
        m.update(consts)
        maps.append(m)
    return maps


def build(phases=(0, 1, 2, 3, 4), taps=()):
    nc = bass.Bass("TRN2", target_bir_lowering=False)
    tap = set(taps)

    def dram_in(name, shape, dt=F32):
        return nc.dram_tensor(name, shape, dt, kind="ExternalInput").ap()

    def dram_scr(name, shape, dt):
        kind = "ExternalOutput" if name in tap else "Internal"
        return nc.dram_tensor(name, shape, dt, kind=kind).ap()

    x_d = dram_in("x", [TOKH, D])
    halo_d = dram_in("halo_mask", [128, 1])
    sel4_d = dram_in("sel4", [128, 4])
    bsel_d = dram_in("bsel", [128, 4])
    wqkv_d = dram_in("wqkv_img", [48, 128, 2048])
    wz_d = dram_in("wz_img", [16, 128, 2048])
    wba_d = dram_in("wba_img", [128, 512])
    wout_d = dram_in("wout_img", [16, 128, 2048])
    convw_d = dram_in("convw_t", [48, 128, 4])
    cT_d = dram_in("cT", [16, 128, 4])
    wmod_d = dram_in("wmod_t", [16, 128, 1536])
    dtb_d = dram_in("dtb_col", [16, 1])
    alog_d = dram_in("alog_col", [16, 1])
    nw_d = dram_in("nw_row", [1, 128])
    wgs_d = dram_in("wg_slab", [64, 2, 128, 128])
    wus_d = dram_in("wu_slab", [64, 2, 128, 128])
    wds_d = dram_in("wd_slab", [16, 8, 128, 128])
    cI_d = dram_in("c_I", [128, 128])
    cH_d = dram_in("c_H128", [128, 128])
    cHX_d = dram_in("c_HX", [128, 128])
    cmL_d = dram_in("c_mL", [128, 128])
    coL_d = dram_in("c_oL", [128, 128])
    cmU_d = dram_in("c_mU", [128, 128])
    coU_d = dram_in("c_oU", [128, 128])
    cmUi_d = dram_in("c_mUi", [128, 128])
    coUi_d = dram_in("c_oUi", [128, 128])
    csel_d = dram_in("c_sel16", [16, 2048])

    y_d = nc.dram_tensor("y", [TOK, D], F32, kind="ExternalOutput").ap()

    ag_w_in = dram_scr("ag_w_in", [3 * SLAB], BF16)
    ag_w_out = dram_scr("ag_w_out", [NCORE, 3 * SLAB], BF16)
    ar_s_in = dram_scr("ar_s_in", [1, 4], F32)
    ar_s_out = dram_scr("ar_s_out", [1, 4], F32)
    ag_m_in = dram_scr("ag_m_in", [4, 1536], F32)
    ag_m_out = dram_scr("ag_m_out", [NCORE, 4, 1536], F32)
    mod2_d = dram_scr("mod2", [4, 12288], F32)
    q_sp = dram_scr("q_sp", [CH, H, 128, 128], BF16)
    k_sp = dram_scr("k_sp", [CH, H, 128, 128], BF16)
    v_sp = dram_scr("v_sp", [CH, H, 128, 128], BF16)
    z_sp = dram_scr("z_sp", [CH, 128, 2048], BF16)
    gcum_d = dram_scr("gcum_sp", [16, 2048], F32)
    beta_d = dram_scr("beta_sp", [16, 2048], F32)
    op_sp = dram_scr("op_sp", [CH, 6, H, 128, 128], BF16)
    ag_s_in = dram_scr("ag_s_in", [128, H * 128], F32)
    ag_s_out = dram_scr("ag_s_out", [NCORE, 128, H * 128], F32)
    ocm_sp = dram_scr("ocm_sp", [H, 128, TOK], BF16)
    x2_sp = dram_scr("x2_sp", [16, 128, D], F32)
    inter_sp = dram_scr("inter_sp", [8, 64, 128, 256], BF16)
    xq2_sp = dram_scr("xq2_sp", [16, 64, 128, 128], BF16)
    rs1_d = dram_scr("rs1_sp", [16, 128], F32)
    s2tmp_d = dram_scr("s2tmp", [8, 2, 128], F32)
    rs2_d = dram_scr("rs2_sp", [16, 128], F32)

    es = contextlib.ExitStack()
    with es:
        sT = SemC(es.enter_context(nc.semaphore("sT")))
        sV = SemC(es.enter_context(nc.semaphore("sV")))
        sS = SemC(es.enter_context(nc.semaphore("sS")))
        sG = SemC(es.enter_context(nc.semaphore("sG")))
        sC = SemC(es.enter_context(nc.semaphore("sC")))

        class DmaPool:
            def __init__(self, name, eng, n):
                self.sems = [SemC(es.enter_context(nc.semaphore(f"{name}{i}")))
                             for i in range(n)]
                self.eng = eng
                self.i = 0
                self.used = [False] * n

            def start(self, args, _n=16, **kw):
                if not isinstance(args, tuple):
                    args = (args,)
                s = self.sems[self.i]
                if self.used[self.i]:
                    self.eng.wait_ge(s.h, s.v)
                self.used[self.i] = True
                self.i = (self.i + 1) % len(self.sems)
                s.inc(self.eng.dma_start(*args, **kw), 16)
                return (s, s.v)

            def wait(self, e, tok):
                e.wait_ge(tok[0].h, tok[1])

            def wait_all(self, e):
                for s in self.sems:
                    e.wait_ge(s.h, s.v)

        qX = DmaPool("qX", nc.sync, 8)
        qW = DmaPool("qW", nc.gpsimd, 6)
        qP = DmaPool("qP", nc.scalar, 6)

        pb = [es.enter_context(nc.psum_tensor(f"pb{i}", [128, 512], F32))
              for i in range(8)]
        bank_free = [(None, 0)] * 8
        bank_rr = [0, 4]

        def bank_wait(i):
            semc, val = bank_free[i]
            if semc is not None:
                nc.tensor.wait_ge(semc.h, val)

        def next_bank(grp=0):
            lo, hi = (0, 4) if grp == 0 else (4, 8)
            bank_rr[grp] = lo + (bank_rr[grp] + 1 - lo) % (hi - lo)
            return bank_rr[grp]

        def drain_dmas():
            qX.wait_all(nc.sync)
            qP.wait_all(nc.sync)
            qW.wait_all(nc.sync)
            nc.all_engine_barrier()

        def alloc(name, shape, dt):
            return es.enter_context(nc.sbuf_tensor(name, shape, dt))

        def VI(instr, n=1):
            sV.inc(instr, n)
            nc.vector.drain()
            return sV.v

        def SI(instr, n=1):
            sS.inc(instr, n)
            nc.scalar.drain()
            return sS.v

        def GI(instr, n=1):
            sG.inc(instr, n)
            nc.gpsimd.drain()
            return sG.v

        I_bf = alloc("I_bf", [128, 128], BF16)
        H128 = alloc("H128", [128, 128], BF16)
        HXs = alloc("HXs", [128, 128], BF16)
        mL = alloc("mL", [128, 128], F32)
        oL = alloc("oL", [128, 128], F32)
        mU = alloc("mU", [128, 128], F32)
        oU = alloc("oU", [128, 128], F32)
        mUi = alloc("mUi", [128, 128], F32)
        oUi = alloc("oUi", [128, 128], F32)
        ones_b = alloc("ones_b", [128, 32], BF16)
        ones16 = alloc("ones16", [16, 128], BF16)
        sel16b = alloc("sel16b", [16, 2048], BF16)
        halo_m = alloc("halo_s", [128, 1], F32)
        sel4 = alloc("sel4_s", [128, 4], F32)
        bsel = alloc("bsel_s", [128, 4], F32)
        dtb = alloc("dtb", [16, 1], F32)
        al_c = alloc("al_c", [16, 1], F32)
        negea = alloc("negea", [16, 1], F32)
        nwb = alloc("nwb", [128, 128], BF16)
        kc1 = alloc("kc1", [128, 1], F32)

        ca1b = alloc("ca1b", [128, 2048], BF16)
        sab = alloc("sab", [128, 2048], BF16)
        gab = alloc("gab", [128, 2048], BF16)
        cb1b = alloc("cb1b", [128, 2048], BF16)
        sbb = alloc("sbb", [128, 2048], BF16)
        gbb = alloc("gbb", [128, 2048], BF16)

        beta_r = alloc("beta_r", [16, 2048], F32)
        g_r = alloc("g_r", [16, 2048], F32)
        gc_r = alloc("gc_r", [16, 2048], F32)
        gcT_r = alloc("gcT_r", [128, 16 * CH], F32)
        btT_r = alloc("btT_r", [128, 16 * CH], F32)
        sbars = alloc("sbars", [128, 4], F32)
        sc1_r = alloc("sc1_r", [128, 16], F32)

        nc.gpsimd.memset(kc1[:], 1.0)
        nc.gpsimd.memset(ones_b[:], 1.0)
        GI(nc.gpsimd.memset(ones16[:], 1.0), 1)
        for dst, src in [(I_bf, cI_d), (H128, cH_d), (HXs, cHX_d),
                         (sel16b, csel_d),
                         (nwb, nw_d[0:1, :].broadcast_to((128, 128)))]:
            qW.start((dst[:], src), 16)
        for dst, src in [(mL, cmL_d), (oL, coL_d), (mU, cmU_d),
                         (oU, coU_d), (mUi, cmUi_d), (oUi, coUi_d),
                         (halo_m, halo_d), (sel4, sel4_d), (bsel, bsel_d),
                         (dtb, dtb_d), (al_c, alog_d)]:
            qX.start((dst[:], src), 16)
        for e in (nc.tensor, nc.vector, nc.scalar, nc.gpsimd, nc.sync):
            qW.wait_all(e)
            qX.wait_all(e)
            e.wait_ge(sG.h, sG.v)
        SI(nc.scalar.activation(negea[:], al_c[:], AF.Exp), 1)
        SI(nc.scalar.mul(negea[:], negea[:], -1.0), 1)
        nc.all_engine_barrier()

        def phase0():
          with contextlib.ExitStack() as ps:
            def pa(n, s, d):
                return ps.enter_context(nc.sbuf_tensor(n, s, d))
            csT = pa("csT", [128, 64], BF16)
            cTl = pa("cTl", [128, 64], F32)
            wmod_s = pa("wmod_s", [128, 8 * 1536], BF16)
            mod_sl = pa("mod_sl", [4, 1536], F32)
            qX.start((cTl[:], cT_d.transpose([1, 0, 2])), 16)
            qX.wait_all(nc.scalar)
            SI(nc.scalar.activation(csT[:], cTl[:], AF.Silu), 1)
            nc.tensor.wait_ge(sS.h, sS.v)
            mbks = [next_bank(), next_bank(), next_bank()]
            for half in range(2):
                qW.start((
                    wmod_s[:], wmod_d[half * 8:(half + 1) * 8]
                    .transpose([1, 0, 2])), 16)
                qW.wait_all(nc.tensor)
                for nb in range(3):
                    bk = mbks[nb]
                    if half == 0:
                        bank_wait(bk)
                    for k2 in range(8):
                        mm = nc.tensor.matmul(
                            pb[bk][0:4, :],
                            csT[:, (half * 8 + k2) * 4:(half * 8 + k2) * 4 + 4],
                            wmod_s[:, k2 * 1536 + nb * 512: k2 * 1536 + nb * 512 + 512],
                            start=(half == 0 and k2 == 0),
                            stop=(half == 1 and k2 == 7))
                    sT.inc(mm, 1)
                if half == 0:
                    sG.inc(nc.gpsimd.nop(), 1)
                    nc.gpsimd.wait_ge(sT.h, sT.v)
            for nb in range(3):
                bk = mbks[nb]
                nc.scalar.wait_ge(sT.h, sT.v)
                SI(nc.scalar.copy(
                    mod_sl[:, nb * 512:(nb + 1) * 512], pb[bk][0:4, :]), 1)
                bank_free[bk] = (sS, sS.v)
            nc.scalar.wait_ge(sS.h, sS.v)
            qP.start((ag_m_in, mod_sl[:]), 16)
            qP.wait_all(nc.gpsimd)
            sC.inc(nc.gpsimd.collective_compute(
                "AllGather", ALU.bypass, replica_groups=[list(range(NCORE))],
                ins=[ag_m_in.opt()], outs=[ag_m_out.opt()]), 1)

            asum = pa("asum", [128, 4], F32)
            red = pa("red", [128, 1], F32)
            wbuf = pa("wbuf", [128, 2 * 4096], F32)
            VI(nc.vector.memset(asum[:], 0.0), 1)
            flats = [wgs_d.rearrange("a b p f -> (a b) p f"),
                     wus_d.rearrange("a b p f -> (a b) p f"),
                     wds_d.rearrange("a b p f -> (a b) p f")]
            vred = []
            vg = 0
            for si in range(3):
                for gg in range(4):
                    buf = wbuf[:, (vg % 2) * 4096:(vg % 2) * 4096 + 4096]
                    if vg >= 2:
                        nc.sync.wait_ge(sV.h, vred[vg - 2])
                    qX.start((
                        buf, flats[si][gg * 32:(gg + 1) * 32].transpose([1, 0, 2])), 16)
                    qX.wait_all(nc.vector)
                    VI(nc.vector.tensor_reduce(
                        red[:], buf, AX.X, ALU.add, apply_absolute_value=True), 1)
                    VI(nc.vector.tensor_tensor(
                        asum[:, si:si + 1], asum[:, si:si + 1], red[:], ALU.add), 1)
                    vred.append(sV.v)
                    vg += 1
            p4h = pa("p4h", [128, 4], BF16)
            p4f = pa("p4f", [128, 4], F32)
            p4l = pa("p4l", [128, 4], BF16)
            p4r = pa("p4r", [128, 4], F32)
            p4l2 = pa("p4l2", [128, 4], BF16)
            arow = pa("arow", [1, 4], F32)
            VI(nc.vector.tensor_copy(p4h[:], asum[:]), 1)
            VI(nc.vector.tensor_copy(p4f[:], p4h[:]), 1)
            VI(nc.vector.tensor_tensor(p4r[:], asum[:], p4f[:], ALU.subtract), 1)
            VI(nc.vector.tensor_copy(p4l[:], p4r[:]), 1)
            VI(nc.vector.tensor_copy(p4f[:], p4l[:]), 1)
            VI(nc.vector.tensor_tensor(p4r[:], p4r[:], p4f[:], ALU.subtract), 1)
            VI(nc.vector.tensor_copy(p4l2[:], p4r[:]), 1)
            bk = next_bank()
            bank_wait(bk)
            nc.tensor.wait_ge(sV.h, sV.v)
            nc.tensor.matmul(pb[bk][0:32, 0:4], ones_b[:], p4h[:], start=True, stop=False)
            nc.tensor.matmul(pb[bk][0:32, 0:4], ones_b[:], p4l[:], start=False, stop=False)
            mm = nc.tensor.matmul(pb[bk][0:32, 0:4], ones_b[:], p4l2[:], start=False, stop=True)
            sT.inc(mm, 1)
            nc.scalar.wait_ge(sT.h, sT.v)
            SI(nc.scalar.copy(arow[:], pb[bk][0:1, 0:4]), 1)
            bank_free[bk] = (sS, sS.v)
            nc.scalar.wait_ge(sS.h, sS.v)
            qP.start((ar_s_in, arow[:]), 16)
            qP.wait_all(nc.gpsimd)
            sC.inc(nc.gpsimd.collective_compute(
                "AllReduce", ALU.add, replica_groups=[list(range(NCORE))],
                ins=[ar_s_in.opt()], outs=[ar_s_out.opt()]), 1)
            nc.sync.wait_ge(sC.h, sC.v)
            qX.start((
                sbars[:], ar_s_out[0:1, :].broadcast_to((128, 4))), 16)
            qX.wait_all(nc.vector)
            nv = 1.0 / float(2048 * 8192)
            VI(nc.vector.tensor_scalar_mul(sbars[:], sbars[:], nv), 1)
            thrb = pa("thrb", [128, 4], F32)
            nthrb = pa("nthrb", [128, 4], F32)
            VI(nc.vector.tensor_scalar_mul(thrb[:], sbars[:], 0.5), 1)
            VI(nc.vector.tensor_scalar_mul(nthrb[:], sbars[:], -0.5), 1)
            b1 = pa("b1", [128, 4096], F32)
            b2t = pa("b2t", [128, 2 * 4096], BF16)
            vq, pq = [], []
            vg = 0
            for si in range(3):
                for gg in range(4):
                    buf = wbuf[:, (vg % 2) * 4096:(vg % 2) * 4096 + 4096]
                    obuf = b2t[:, (vg % 2) * 4096:(vg % 2) * 4096 + 4096]
                    if vg >= 2:
                        nc.sync.wait_ge(sV.h, vq[vg - 2])
                    qX.start((
                        buf, flats[si][gg * 32:(gg + 1) * 32].transpose([1, 0, 2])), 16)
                    qX.wait_all(nc.vector)
                    if vg >= 2:
                        qP.wait(nc.vector, pq[vg - 2])
                    VI(nc.vector.tensor_scalar(
                        obuf, buf, nthrb[:, si:si + 1], None, ALU.is_lt), 1)
                    VI(nc.vector.tensor_scalar(
                        b1[:], buf, thrb[:, si:si + 1], None, ALU.is_gt), 1)
                    VI(nc.vector.tensor_tensor(obuf, b1[:], obuf, ALU.subtract), 1)
                    vq.append(sV.v)
                    nc.scalar.wait_ge(sV.h, sV.v)
                    dst = ag_w_in[si * SLAB + gg * 32 * 16384:
                                  si * SLAB + (gg + 1) * 32 * 16384]
                    pq.append(qP.start((
                        dst.rearrange("(t p f) -> t p f", t=32, p=128)
                        .transpose([1, 0, 2]), obuf)))

                    vg += 1
            qP.wait_all(nc.gpsimd)
            sC.inc(nc.gpsimd.collective_compute(
                "AllGather", ALU.bypass, replica_groups=[list(range(NCORE))],
                ins=[ag_w_in.opt()], outs=[ag_w_out.opt()]), 1)

            nc.sync.wait_ge(sC.h, sC.v)
            qX.start((
                mod2_d.rearrange("b (r j) -> b r j", r=8),
                ag_m_out.transpose([1, 0, 2])), 16)
            qX.wait_all(nc.sync)
            mtmp = pa("mtmp", [128, 2048], F32)
            macc = pa("macc", [128, 2048], F32)
            for ii, dst in enumerate((sab, ca1b, gab, sbb, cb1b, gbb)):
                for bb in range(4):
                    qX.start((
                        mtmp[:], mod2_d[bb, ii * 2048:(ii + 1) * 2048]
                        .unsqueeze(0).broadcast_to((128, 2048))), 16)
                    qX.wait_all(nc.vector)
                    if bb == 0:
                        VI(nc.vector.tensor_scalar(
                            macc[:], mtmp[:], bsel[:, 0:1], None, ALU.mult), 1)
                    else:
                        VI(nc.vector.scalar_tensor_tensor(
                            macc[:], mtmp[:], bsel[:, bb:bb + 1], macc[:],
                            ALU.mult, ALU.add), 1)
                    nc.sync.wait_ge(sV.h, sV.v)
                if ii in (1, 4):
                    VI(nc.vector.tensor_scalar(
                        macc[:], macc[:], 1.0, None, ALU.add), 1)
                if ii in (2, 5):
                    VI(nc.vector.tensor_scalar(
                        mtmp[:], macc[:], 0.0, 2.0, ALU.is_ge, ALU.mult), 1)
                    VI(nc.vector.tensor_scalar(
                        mtmp[:], mtmp[:], 1.0, None, ALU.subtract), 1)
                    VI(nc.vector.tensor_tensor(macc[:], macc[:], mtmp[:],
                                               ALU.mult), 1)
                    VI(nc.vector.tensor_scalar(
                        macc[:], macc[:], 0.25, None, ALU.max), 1)
                    VI(nc.vector.tensor_tensor(macc[:], macc[:], mtmp[:],
                                               ALU.mult), 1)
                VI(nc.vector.tensor_copy(dst[:], macc[:]), 1)
                nc.sync.wait_ge(sV.h, sV.v)
            drain_dmas()

        def phase1():
          with contextlib.ExitStack() as ps:
            def pa(n, s, d):
                return ps.enter_context(nc.sbuf_tensor(n, s, d))
            h1T = pa("h1T", [128, NKD * HT], BF16)
            kceps5 = pa("kceps5", [128, 1], F32)
            kceps6 = pa("kceps6", [128, 1], F32)
            nc.gpsimd.memset(kceps5[:], 1e-5)
            GI(nc.gpsimd.memset(kceps6[:], 1e-6), 1)
            nc.vector.wait_ge(sG.h, sG.v)
            nc.scalar.wait_ge(sG.h, sG.v)

            h1T3 = h1T[:].rearrange("p (k c) -> p k c", k=NKD)
            tr_done = {}
            ln_es = contextlib.ExitStack()
            xt = ln_es.enter_context(nc.sbuf_tensor("xt", [128, 2 * 2048], F32))
            sqt = ln_es.enter_context(nc.sbuf_tensor("sqt", [128, 2048], F32))
            stat = ln_es.enter_context(nc.sbuf_tensor("stat", [128, 8], F32))
            h1a = ln_es.enter_context(nc.sbuf_tensor("h1a", [128, 2048], F32))
            h1b = ln_es.enter_context(nc.sbuf_tensor("h1b", [128, 2 * 2048], BF16))
            xh = ln_es.enter_context(nc.sbuf_tensor("xh", [16, 2048], F32))

            def ln_tile(x_ap, np_, col0, slot, idx):
                hb = h1b[:, slot * 2048:(slot + 1) * 2048]
                VI(nc.vector.tensor_reduce(
                    stat[0:np_, 0:1], x_ap, AX.X, ALU.add), 1)
                SI(nc.scalar.activation(
                    sqt[0:np_, :], x_ap, AF.Square,
                    accum_out=stat[0:np_, 2:3]), 1)
                nc.vector.wait_ge(sS.h, sS.v)
                VI(nc.vector.tensor_scalar_mul(
                    stat[0:np_, 1:2], stat[0:np_, 0:1], 1.0 / 2048.0), 1)
                VI(nc.vector.tensor_scalar_mul(
                    stat[0:np_, 3:4], stat[0:np_, 2:3], 1.0 / 2048.0), 1)
                VI(nc.vector.tensor_tensor(
                    stat[0:np_, 4:5], stat[0:np_, 1:2], stat[0:np_, 1:2],
                    ALU.mult), 1)
                VI(nc.vector.tensor_tensor(
                    stat[0:np_, 3:4], stat[0:np_, 3:4], stat[0:np_, 4:5],
                    ALU.subtract), 1)
                nc.scalar.wait_ge(sV.h, sV.v)
                SI(nc.scalar.activation(
                    stat[0:np_, 5:6], stat[0:np_, 3:4], AF.Sqrt,
                    bias=kceps5[0:np_, :]), 1)
                nc.vector.wait_ge(sS.h, sS.v)
                VI(nc.vector.reciprocal(
                    stat[0:np_, 6:7], stat[0:np_, 5:6]), 1)
                VI(nc.vector.tensor_scalar(
                    h1a[0:np_, :], x_ap, stat[0:np_, 1:2], stat[0:np_, 6:7],
                    ALU.subtract, ALU.mult), 1)
                VI(nc.vector.tensor_tensor(
                    h1a[0:np_, :], h1a[0:np_, :], ca1b[0:np_, :], ALU.mult), 1)
                if idx - 2 in tr_done:
                    nc.vector.wait_ge(sT.h, tr_done[idx - 2])
                VI(nc.vector.tensor_tensor(
                    hb[0:np_, :], h1a[0:np_, :], sab[0:np_, :], ALU.add), 1)
                nc.tensor.wait_ge(sV.h, sV.v)
                kpart = max(np_, 16)
                for g4 in range(4):
                    bk = next_bank(0)
                    bank_wait(bk)
                    for j in range(4):
                        kk = g4 * 4 + j
                        mm = nc.tensor.matmul(
                            pb[bk][:, j * 128:j * 128 + np_],
                            hb[0:kpart, kk * 128:(kk + 1) * 128],
                            I_bf[0:kpart, 0:np_],
                            start=True, stop=True)
                    sT.inc(mm, 1)
                    nc.scalar.wait_ge(sT.h, sT.v)
                    SI(nc.scalar.copy(
                        h1T3[:, g4 * 4:(g4 + 1) * 4, col0:col0 + np_],
                        pb[bk][:, :].rearrange("p (j c) -> p j c", j=4)
                        [:, :, 0:np_]), 1)
                    bank_free[bk] = (sS, sS.v)
                tr_done[idx] = sT.v

            qX.start((xh[0:3, 0:2048], x_d[0:3, :]), 16)
            qX.wait_all(nc.vector)
            qX.wait_all(nc.scalar)
            VI(nc.vector.memset(h1b[0:16, 0:2048], 0.0), 1)
            ln_tile(xh[0:3, 0:2048], 3, 0, 0, 0)
            ln_v = {}
            for tt in range(16):
                xbuf = xt[:, (tt % 2) * 2048:(tt % 2) * 2048 + 2048]
                if tt >= 2:
                    nc.sync.wait_ge(sV.h, ln_v[tt - 2])
                qX.start((
                    xbuf, x_d[3 + tt * 128: 3 + (tt + 1) * 128, :]), 16)
                qX.wait_all(nc.vector)
                qX.wait_all(nc.scalar)
                ln_tile(xbuf, 128, 3 + tt * 128, (tt + 1) % 2, tt + 1)
                ln_v[tt] = sV.v
            drain_dmas()
            ln_es.close()

            qk_es = contextlib.ExitStack()
            def qa(n, s, d):
                return qk_es.enter_context(nc.sbuf_tensor(n, s, d))
            wimg = qa("wimg", [128, 2 * 2048], BF16)
            cwt = qa("cwt", [128, 2 * 4], F32)
            mp = qa("mp", [128, 2 * 2051], F32)
            acc = qa("acc", [128, 2 * 2048], BF16)
            silt = qa("silt", [128, 2 * 2048], BF16)
            sqb = qa("sqb", [128, 2048], BF16)
            rrow = qa("rrow", [16, 2048], F32)
            rrb = qa("rrb", [16, 2048], BF16)
            rb = qa("rb", [128, 2048], F32)
            stg = qa("stg", [128, 512], BF16)
            VI(nc.vector.memset(rrb[:], 0.0), 1)

            NBQ = [(0, 512), (512, 512), (1024, 512), (1536, 512), (2048, 3)]
            NBZ = [(3, 512), (515, 512), (1027, 512), (1539, 512)]
            wfree = {}
            mpfree = {}
            silu_s = {}
            spill_x = {}

            wtoks = {}

            def stream_w_issue(m, src_ap, conv_m):
                if m - 2 in wfree:
                    nc.gpsimd.wait_ge(sT.h, wfree[m - 2])
                if conv_m is not None and m - 2 in mpfree:
                    nc.gpsimd.wait_ge(sV.h, mpfree[m - 2])
                toks = [qW.start((
                    wimg[:, (m % 2) * 2048:(m % 2) * 2048 + 2048],
                    src_ap), 16)]
                if conv_m is not None:
                    toks.append(qW.start((
                        cwt[:, (m % 2) * 4:(m % 2) * 4 + 4],
                        convw_d[conv_m]), 16))
                wtoks[m] = toks

            def stream_w_wait(m):
                for t in wtoks[m]:
                    qW.wait(nc.tensor, t)

            def mm_tile(m, nblocks, evac):
                for bi, (c0, cn) in enumerate(nblocks):
                    bk = next_bank(0)
                    bank_wait(bk)
                    for kk in range(NKD):
                        mm = nc.tensor.matmul(
                            pb[bk][:, 0:cn],
                            wimg[:, (m % 2) * 2048 + kk * 128:
                                 (m % 2) * 2048 + (kk + 1) * 128],
                            h1T3[:, kk, c0:c0 + cn],
                            start=(kk == 0), stop=(kk == NKD - 1))
                    sT.inc(mm, 1)
                    evac(bi, c0, cn, bk)
                wfree[m] = sT.v

            stream_w_issue(0, wqkv_d[0], 0)
            for m in range(48):
                if m + 1 < 48:
                    stream_w_issue(m + 1, wqkv_d[m + 1], m + 1)
                stream_w_wait(m)
                if m - 2 in mpfree:
                    nc.tensor.wait_ge(sV.h, mpfree[m - 2])
                ms = mp[:, (m % 2) * 2051:(m % 2) * 2051 + 2051]

                def evac_q(bi, c0, cn, bk, ms=ms):
                    nc.scalar.wait_ge(sT.h, sT.v)
                    SI(nc.scalar.copy(ms[:, c0:c0 + cn], pb[bk][:, 0:cn]), 1)
                    bank_free[bk] = (sS, sS.v)
                mm_tile(m, NBQ, evac_q)
                nc.vector.wait_ge(sS.h, sS.v)
                VI(nc.vector.tensor_scalar(
                    ms[:, 0:3], ms[:, 0:3], halo_m[:, 0:1], None, ALU.mult), 1)
                a0 = acc[:, (m % 2) * 2048:(m % 2) * 2048 + 2048]
                a1 = acc[:, ((m + 1) % 2) * 2048:((m + 1) % 2) * 2048 + 2048]
                cw = cwt[:, (m % 2) * 4:(m % 2) * 4 + 4]
                if m - 1 in silu_s:
                    nc.vector.wait_ge(sS.h, silu_s[m - 1])
                VI(nc.vector.tensor_scalar(
                    a0, ms[:, 0:2048], cw[:, 0:1], None, ALU.mult), 1)
                VI(nc.vector.scalar_tensor_tensor(
                    a1, ms[:, 1:2049], cw[:, 1:2], a0, ALU.mult, ALU.add), 1)
                VI(nc.vector.scalar_tensor_tensor(
                    a0, ms[:, 2:2050], cw[:, 2:3], a1, ALU.mult, ALU.add), 1)
                VI(nc.vector.scalar_tensor_tensor(
                    a1, ms[:, 3:2051], cw[:, 3:4], a0, ALU.mult, ALU.add), 1)
                mpfree[m] = sV.v
                sl = silt[:, (m % 2) * 2048:(m % 2) * 2048 + 2048]
                nc.scalar.wait_ge(sV.h, sV.v)
                if m - 2 in spill_x:
                    qX.wait(nc.scalar, spill_x[m - 2])
                silu_s[m] = SI(nc.scalar.activation(sl, a1, AF.Silu), 1)
                if m < 32:
                    SI(nc.scalar.activation(sqb[:], sl, AF.Square), 1)
                    nc.tensor.wait_ge(sS.h, sS.v)
                    for nb in range(4):
                        bk = next_bank(1)
                        bank_wait(bk)
                        mm = nc.tensor.matmul(
                            pb[bk][0:32, 0:512], ones_b[:],
                            sqb[:, nb * 512:(nb + 1) * 512],
                            start=True, stop=True)
                        sT.inc(mm, 1)
                        nc.vector.wait_ge(sT.h, sT.v)
                        VI(nc.vector.tensor_copy(
                            rrow[0:1, nb * 512:(nb + 1) * 512],
                            pb[bk][0:1, 0:512]), 1)
                        bank_free[bk] = (sV, sV.v)
                    nc.scalar.wait_ge(sV.h, sV.v)
                    SI(nc.scalar.activation(
                        rrow[0:1, :], rrow[0:1, :], AF.Sqrt,
                        bias=kceps6[0:1, :]), 1)
                    nc.vector.wait_ge(sS.h, sS.v)
                    VI(nc.vector.reciprocal(rrow[0:1, :], rrow[0:1, :]), 1)
                    if m < 16:
                        VI(nc.vector.tensor_scalar_mul(
                            rrow[0:1, :], rrow[0:1, :], float(128 ** -0.5)), 1)
                    VI(nc.vector.tensor_copy(rrb[0:1, :], rrow[0:1, :]), 1)
                    nc.tensor.wait_ge(sV.h, sV.v)
                    for nb in range(4):
                        bk = next_bank(1)
                        bank_wait(bk)
                        mm = nc.tensor.matmul(
                            pb[bk][:, 0:512], ones16[:],
                            rrb[:, nb * 512:(nb + 1) * 512], start=True, stop=True)
                        sT.inc(mm, 1)
                        nc.scalar.wait_ge(sT.h, sT.v)
                        SI(nc.scalar.copy(
                            rb[:, nb * 512:(nb + 1) * 512], pb[bk][:, 0:512]), 1)
                        bank_free[bk] = (sS, sS.v)
                    nc.vector.wait_ge(sS.h, sS.v)
                    VI(nc.vector.tensor_tensor(sl, sl, rb[:], ALU.mult), 1)
                    last = (sV, sV.v)
                else:
                    last = (sS, silu_s[m])
                dst = (q_sp if m < 16 else k_sp if m < 32 else v_sp)
                hh = m % 16
                nc.sync.wait_ge(last[0].h, last[1])
                spill_x[m] = qX.start((
                    dst[:, hh].transpose([1, 0, 2]), sl))
            drain_dmas()

            zspill = {}
            stream_w_issue(0, wz_d[0], None)
            for m in range(16):
                if m + 1 < 16:
                    stream_w_issue(m + 1, wz_d[m + 1], None)
                stream_w_wait(m)

                def evac_z(bi, c0, cn, bk):
                    nc.scalar.wait_ge(sT.h, sT.v)
                    SI(nc.scalar.activation(
                        silt[:, bi * 512:(bi + 1) * 512], pb[bk][:, 0:cn],
                        AF.Silu), 1)
                    bank_free[bk] = (sS, sS.v)
                mm_tile(m, NBZ, evac_z)
                nc.tensor.wait_ge(sS.h, sS.v)
                for c4 in range(4):
                    bk = next_bank(1)
                    bank_wait(bk)
                    for j in range(4):
                        cci = c4 * 4 + j
                        mm = nc.tensor.matmul(
                            pb[bk][:, j * 128:(j + 1) * 128],
                            silt[:, cci * 128:(cci + 1) * 128],
                            I_bf[:], start=True, stop=True)
                    sT.inc(mm, 1)
                    nc.vector.wait_ge(sT.h, sT.v)
                    if (m, c4) != (0, 0):
                        qX.wait_all(nc.vector)
                    VI(nc.vector.tensor_copy(stg[:], pb[bk][:, :]), 1)
                    bank_free[bk] = (sV, sV.v)
                    nc.sync.wait_ge(sV.h, sV.v)
                    qX.start((
                        z_sp[c4 * 4:(c4 + 1) * 4, :, m * 128:(m + 1) * 128]
                        .transpose([1, 0, 2]), stg[:]), 16)
            drain_dmas()
            qk_es.close()

            wbal = pa("wbal", [128, 512], BF16)
            bag = pa("bag", [16, 2048], F32)
            aag = pa("aag", [16, 2048], F32)
            qW.start((wbal[:], wba_d), 16)
            qW.wait_all(nc.tensor)
            for part, out in ((0, bag), (16, aag)):
                for bi, (c0, cn) in enumerate(NBZ):
                    bk = next_bank(0)
                    bank_wait(bk)
                    for kk in range(NKD):
                        mm = nc.tensor.matmul(
                            pb[bk][0:16, 0:cn],
                            wbal[:, kk * 32 + part:kk * 32 + part + 16],
                            h1T3[:, kk, c0:c0 + cn],
                            start=(kk == 0), stop=(kk == NKD - 1))
                    sT.inc(mm, 1)
                    nc.scalar.wait_ge(sT.h, sT.v)
                    SI(nc.scalar.copy(out[:, bi * 512:(bi + 1) * 512],
                                          pb[bk][0:16, 0:cn]), 1)
                    bank_free[bk] = (sS, sS.v)
            SI(nc.scalar.activation(beta_r[:], bag[:], AF.Sigmoid), 1)
            SI(nc.scalar.activation(g_r[:], aag[:], AF.Exp, bias=dtb[:]), 1)
            SI(nc.scalar.activation(g_r[:], g_r[:], AF.Ln,
                                        bias=kc1[0:16, :]), 1)
            nc.vector.wait_ge(sS.h, sS.v)
            VI(nc.vector.tensor_scalar(
                g_r[:], g_r[:], negea[:], None, ALU.mult), 1)
            ga_, gb_ = g_r, gc_r
            for st_ in range(7):
                hh = 1 << st_
                va = ga_[:].rearrange("p (c f) -> p c f", c=CH)
                vb = gb_[:].rearrange("p (c f) -> p c f", c=CH)
                VI(nc.vector.tensor_copy(vb[:, :, 0:hh], va[:, :, 0:hh]), 1)
                VI(nc.vector.tensor_tensor(
                    vb[:, :, hh:128], va[:, :, hh:128], va[:, :, 0:128 - hh],
                    ALU.add), 1)
                ga_, gb_ = gb_, ga_
            nc.sync.wait_ge(sV.h, sV.v)
            qX.start((gcum_d, gc_r[:]), 16)
            qX.start((beta_d, beta_r[:]), 16)
            p_hi = pa("p_hi", [16, 2048], BF16)
            p_f = pa("p_f", [16, 2048], F32)
            p_lo = pa("p_lo", [16, 2048], BF16)
            p_r = pa("p_r", [16, 2048], F32)
            p_l2 = pa("p_l2", [16, 2048], BF16)
            bt_b = pa("bt_b", [16, 2048], BF16)
            VI(nc.vector.tensor_copy(p_hi[:], gc_r[:]), 1)
            VI(nc.vector.tensor_copy(p_f[:], p_hi[:]), 1)
            VI(nc.vector.tensor_tensor(p_r[:], gc_r[:], p_f[:], ALU.subtract), 1)
            VI(nc.vector.tensor_copy(p_lo[:], p_r[:]), 1)
            VI(nc.vector.tensor_copy(p_f[:], p_lo[:]), 1)
            VI(nc.vector.tensor_tensor(p_r[:], p_r[:], p_f[:], ALU.subtract), 1)
            VI(nc.vector.tensor_copy(p_l2[:], p_r[:]), 1)
            VI(nc.vector.tensor_copy(bt_b[:], beta_r[:]), 1)
            nc.tensor.wait_ge(sV.h, sV.v)
            for c in range(CH):
                bk = next_bank(0)
                bank_wait(bk)
                cs = slice(c * 128, (c + 1) * 128)
                nc.tensor.matmul(pb[bk][:, 0:16], p_hi[:, cs], I_bf[0:16, 0:16],
                                 start=True, stop=False)
                nc.tensor.matmul(pb[bk][:, 0:16], p_lo[:, cs], I_bf[0:16, 0:16],
                                 start=False, stop=False)
                nc.tensor.matmul(pb[bk][:, 0:16], p_l2[:, cs], I_bf[0:16, 0:16],
                                 start=False, stop=True)
                mm = nc.tensor.matmul(pb[bk][:, 16:32], bt_b[:, cs],
                                      I_bf[0:16, 0:16], start=True, stop=True)
                sT.inc(mm, 1)
                nc.scalar.wait_ge(sT.h, sT.v)
                SI(nc.scalar.copy(gcT_r[:, c * 16:(c + 1) * 16],
                                      pb[bk][:, 0:16]), 1)
                SI(nc.scalar.copy(btT_r[:, c * 16:(c + 1) * 16],
                                      pb[bk][:, 16:32]), 1)
                bank_free[bk] = (sS, sS.v)
            drain_dmas()


        def phase2():
          with contextlib.ExitStack() as ps:
            def pa(n, s, d):
                return ps.enter_context(nc.sbuf_tensor(n, s, d))
            qc = pa("qc", [128, 2048], BF16)
            kc = pa("kc", [128, 2048], BF16)
            vc = pa("vc", [128, 2048], BF16)
            ktil = pa("ktil", [128, 2048], BF16)
            qbar = pa("qbar", [128, 2048], BF16)
            aprow = pa("aprow", [128, 2048], BF16)
            e1 = pa("e1", [128, 2048], BF16)
            e2 = pa("e2", [128, 2048], BF16)
            e3 = pa("e3", [128, 2048], BF16)
            pre_all = pa("pre_all", [128, 2048], BF16)
            pq = pa("pq", [128, 4096], BF16)
            Ptmat = pa("Ptmat", [128, 2048], BF16)
            vT = pa("vT", [128, 2048], BF16)
            kT = pa("kT", [128, 2048], BF16)
            khat = pa("khat", [128, 2048], BF16)
            w2tn = pa("w2tn", [128, 2048], BF16)
            U_all = pa("U_all", [128, 2048], BF16)
            S_f = pa("S_f", [128, 2048], F32)
            S_b = pa("S_b", [128, 2048], BF16)
            Yb = pa("Yb", [128, 4096], BF16)
            rowb = pa("rowb", [128, 2048], F32)
            browb = pa("browb", [128, 2048], F32)
            art = pa("art", [128, 2048], F32)
            art2 = pa("art2", [128, 2048], F32)
            sq16 = pa("sq16", [128, 16], F32)
            akm = pa("akm", [128, 16], F32)
            Gbe = pa("Gbe", [128, 16], F32)
            apT = pa("apT", [128, 16], F32)
            bapT = pa("bapT", [128, 16], F32)
            obuf = pa("obuf", [128, 2048], BF16)
            onb = pa("onb", [128, 2048], BF16)
            zc = pa("zc", [128, 2048], BF16)
            stg2 = pa("stg2", [128, 512], BF16)
            sqs = pa("sqs", [128, 512], BF16)
            ep6 = pa("ep6", [128, 1], F32)
            GI(nc.gpsimd.memset(ep6[:], 1e-6), 1)
            VI(nc.vector.memset(S_f[:], 0.0), 1)
            VI(nc.vector.memset(S_b[:], 0.0), 1)
            nc.gpsimd.wait_ge(sG.h, sG.v)

            Y3 = Yb[:].rearrange("p (h u) -> p h u", h=H)
            qc3 = qc[:].rearrange("p (h t) -> p h t", h=H)
            kc3 = kc[:].rearrange("p (h t) -> p h t", h=H)
            art3 = art[:].rearrange("p (h t) -> p h t", h=H)
            rowb3 = rowb[:].rearrange("p (h t) -> p h t", h=H)

            def bc16(x):
                return x.unsqueeze(2).broadcast_to((128, 16, 128))

            def m128(x):
                return x.unsqueeze(1).broadcast_to((128, 16, 128))

            def soft_sync():
                nc.sync.wait_ge(sT.h, sT.v)
                nc.sync.wait_ge(sV.h, sV.v)
                nc.sync.wait_ge(sS.h, sS.v)
                nc.vector.wait_ge(sT.h, sT.v)
                nc.scalar.wait_ge(sT.h, sT.v)
                qP.wait_all(nc.vector)
                qP.wait_all(nc.scalar)
                qX.wait_all(nc.vector)
                qX.wait_all(nc.scalar)

            def chunk_pre(c):
                qX.start((qc[:], q_sp[c].transpose([1, 0, 2])))
                qX.start((kc[:], k_sp[c].transpose([1, 0, 2])))
                qX.start((vc[:], v_sp[c].transpose([1, 0, 2])))
                qX.start((rowb[:],
                          gcum_d[0:16, c * 128:(c + 1) * 128]
                          .unsqueeze(0).broadcast_to((128, 16, 128))))
                qX.start((browb[:],
                          beta_d[0:16, c * 128:(c + 1) * 128]
                          .unsqueeze(0).broadcast_to((128, 16, 128))))
                qX.wait_all(nc.vector)
                qX.wait_all(nc.scalar)
                gcs = gcT_r[:, c * 16:(c + 1) * 16]
                bts = btT_r[:, c * 16:(c + 1) * 16]
                VI(nc.vector.tensor_tensor(art3, rowb3, bc16(gcs), ALU.subtract), 1)
                SI(nc.scalar.activation(aprow[:], rowb[:], AF.Exp), 1)
                nc.scalar.wait_ge(sV.h, sV.v)
                SI(nc.scalar.activation(
                    akm[:].unsqueeze(2), art3[:, :, 127:128], AF.Exp), 1)
                SI(nc.scalar.activation(
                    Gbe[:].unsqueeze(2), rowb3[:, :, 127:128], AF.Exp), 1)
                SI(nc.scalar.activation(apT[:], gcs, AF.Exp), 1)
                a2v = art2[:].rearrange("p (h t) -> p h t", h=H)
                for msk, off, scl, dst in ((mL, oL, -1.0, e1),
                                           (mU, oU, 1.0, e2),
                                           (mUi, oUi, 1.0, e3)):
                    nc.vector.wait_ge(sS.h, sS.v)
                    VI(nc.vector.tensor_tensor(a2v, art3, m128(msk[:]), ALU.mult), 1)
                    VI(nc.vector.tensor_tensor(a2v, a2v, m128(off[:]), ALU.add), 1)
                    nc.scalar.wait_ge(sV.h, sV.v)
                    SI(nc.scalar.activation(dst[:], art2[:], AF.Exp, scale=scl), 1)
                nc.vector.wait_ge(sS.h, sS.v)
                VI(nc.vector.tensor_tensor(ktil[:], kc[:], browb[:], ALU.mult), 1)
                VI(nc.vector.tensor_tensor(qbar[:], qc[:], aprow[:], ALU.mult), 1)
                VI(nc.vector.tensor_tensor(bapT[:], bts, apT[:], ALU.mult), 1)
                nc.tensor.wait_ge(sV.h, sV.v)
                pq3 = pq[:].rearrange("p (h u) -> p h u", h=H)
                pre3v = pre_all[:].rearrange("p (h t) -> p h t", h=H)
                for lhs, rhs, emat, dst in (
                        (ktil, kc, e1, pq3[:, :, 0:128]),
                        (kc, ktil, e2, pq3[:, :, 128:256]),
                        (kc, qc, e3,
                         Ptmat[:].rearrange("p (h t) -> p h t", h=H))):
                    nc.scalar.wait_ge(sV.h, sV.v)
                    for hg in range(4):
                        bk = next_bank(0)
                        bank_wait(bk)
                        for j in range(4):
                            hh = hg * 4 + j
                            mm = nc.tensor.matmul(
                                pb[bk][:, j * 128:(j + 1) * 128],
                                lhs[:, hh * 128:(hh + 1) * 128],
                                rhs[:, hh * 128:(hh + 1) * 128],
                                start=True, stop=True)
                        sT.inc(mm, 1)
                        nc.scalar.wait_ge(sT.h, sT.v)
                        SI(nc.scalar.copy(
                            pre_all[:, hg * 512:(hg + 1) * 512], pb[bk][:, :]), 1)
                        bank_free[bk] = (sS, sS.v)
                    nc.vector.wait_ge(sS.h, sS.v)
                    VI(nc.vector.tensor_tensor(
                        dst, pre3v,
                        emat[:].rearrange("p (h t) -> p h t", h=H),
                        ALU.mult), 1)
                for srcb, dstb in ((vc, vT), (kc, kT)):
                    nc.tensor.wait_ge(sV.h, sV.v)
                    for hg in range(4):
                        bk = next_bank(0)
                        bank_wait(bk)
                        for j in range(4):
                            hh = hg * 4 + j
                            mm = nc.tensor.matmul(
                                pb[bk][:, j * 128:(j + 1) * 128],
                                srcb[:, hh * 128:(hh + 1) * 128],
                                I_bf[:], start=True, stop=True)
                        sT.inc(mm, 1)
                        nc.scalar.wait_ge(sT.h, sT.v)
                        SI(nc.scalar.copy(
                            dstb[:, hg * 512:(hg + 1) * 512], pb[bk][:, :]), 1)
                        bank_free[bk] = (sS, sS.v)
                vT3 = vT[:].rearrange("p (h t) -> p h t", h=H)
                kT3 = kT[:].rearrange("p (h t) -> p h t", h=H)
                nc.vector.wait_ge(sS.h, sS.v)
                VI(nc.vector.tensor_tensor(Y3[:, :, 0:128], vT3, bc16(bts),
                                           ALU.mult), 1)
                VI(nc.vector.tensor_tensor(Y3[:, :, 128:256], kT3, bc16(bapT[:]),
                                           ALU.mult), 1)
                VI(nc.vector.tensor_tensor(
                    khat[:].rearrange("p (h t) -> p h t", h=H), kT3,
                    bc16(akm[:]), ALU.mult), 1)
                nc.tensor.wait_ge(sV.h, sV.v)
                yev = [None] * 8
                aev = [0] * 8
                for pr in range(8):
                    bk = next_bank(pr % 2)
                    bank_wait(bk)
                    for u in range(2):
                        hh = pr * 2 + u
                        mm = nc.tensor.matmul(
                            pb[bk][:, u * 256:(u + 1) * 256],
                            pq[:, hh * 256 + 128:hh * 256 + 256],
                            Yb[:, hh * 256:(hh + 1) * 256],
                            start=True, stop=True)
                    sT.inc(mm, 1)
                    nc.vector.wait_ge(sT.h, sT.v)
                    VI(nc.vector.tensor_tensor(
                        Yb[:, pr * 512:(pr + 1) * 512],
                        Yb[:, pr * 512:(pr + 1) * 512],
                        pb[bk][:, 0:512], ALU.subtract), 1)
                    bank_free[bk] = (sV, sV.v)
                    yev[pr] = (sV.h, sV.v)
                for j in range(1, 7):
                    for pr in range(8):
                        bk = next_bank(pr % 2)
                        bank_wait(bk)
                        if j > 1:
                            nc.tensor.wait_ge(sS.h, aev[pr])
                        for u in range(2):
                            hh = pr * 2 + u
                            pcur = pq[:, hh * 256:hh * 256 + 128]
                            ptcur = pq[:, hh * 256 + 128:hh * 256 + 256]
                            nc.tensor.matmul(
                                pb[bk][:, u * 256:u * 256 + 128],
                                ptcur, pcur, start=True, stop=True)
                            mm = nc.tensor.matmul(
                                pb[bk][:, u * 256 + 128:u * 256 + 256],
                                pcur, ptcur, start=True, stop=True)
                        sT.inc(mm, 1)
                        nc.scalar.wait_ge(sT.h, sT.v)
                        SI(nc.scalar.copy(pq[:, pr * 512:(pr + 1) * 512],
                                          pb[bk][:, 0:512]), 1)
                        bank_free[bk] = (sS, sS.v)
                        aev[pr] = sS.v
                    for pr in range(8):
                        bk = next_bank(pr % 2)
                        bank_wait(bk)
                        nc.tensor.wait_ge(sS.h, aev[pr])
                        nc.tensor.wait_ge(*yev[pr])
                        for u in range(2):
                            hh = pr * 2 + u
                            mm = nc.tensor.matmul(
                                pb[bk][:, u * 256:(u + 1) * 256],
                                pq[:, hh * 256 + 128:hh * 256 + 256],
                                Yb[:, hh * 256:(hh + 1) * 256],
                                start=True, stop=True)
                        sT.inc(mm, 1)
                        nc.vector.wait_ge(sT.h, sT.v)
                        VI(nc.vector.tensor_tensor(
                            Yb[:, pr * 512:(pr + 1) * 512],
                            Yb[:, pr * 512:(pr + 1) * 512],
                            pb[bk][:, 0:512], ALU.add), 1)
                        bank_free[bk] = (sV, sV.v)
                        yev[pr] = (sV.h, sV.v)
                nc.tensor.wait_ge(sS.h, sS.v)
                nc.tensor.wait_ge(sV.h, sV.v)
                for hg in range(4):
                    bk = next_bank(0)
                    bank_wait(bk)
                    for j in range(4):
                        hh = hg * 4 + j
                        mm = nc.tensor.matmul(
                            pb[bk][:, j * 128:(j + 1) * 128],
                            Yb[:, hh * 256 + 128:(hh + 1) * 256],
                            I_bf[:], start=True, stop=True)
                    sT.inc(mm, 1)
                    nc.scalar.wait_ge(sT.h, sT.v)
                    SI(nc.scalar.activation(
                        w2tn[:, hg * 512:(hg + 1) * 512], pb[bk][:, :],
                        AF.Copy, scale=-1.0), 1)
                    bank_free[bk] = (sS, sS.v)

            def seq_step(c, with_o):
                nc.tensor.wait_ge(sS.h, sS.v)
                nc.tensor.wait_ge(sV.h, sV.v)
                uev = [0] * 4
                for qq in range(4):
                    bk = next_bank(1)
                    bank_wait(bk)
                    for u in range(4):
                        hh = qq * 4 + u
                        hs = slice(hh * 128, (hh + 1) * 128)
                        mm = nc.tensor.matmul(
                            pb[bk][:, u * 128:(u + 1) * 128], w2tn[:, hs],
                            S_b[:, hs], start=True, stop=True)
                    sT.inc(mm, 1)
                    nc.vector.wait_ge(sT.h, sT.v)
                    VI(nc.vector.tensor_tensor(
                        U_all[:, qq * 512:(qq + 1) * 512]
                        .rearrange("p (q t) -> p q t", q=4),
                        Y3[:, qq * 4:(qq + 1) * 4, 0:128],
                        pb[bk][:, 0:512].rearrange("p (q t) -> p q t", q=4),
                        ALU.add), 1)
                    bank_free[bk] = (sV, sV.v)
                    uev[qq] = sV.v
                if with_o:
                    for qq in range(4):
                        bk = next_bank(0)
                        bank_wait(bk)
                        nc.tensor.wait_ge(sV.h, uev[qq])
                        for u in range(4):
                            hh = qq * 4 + u
                            hs = slice(hh * 128, (hh + 1) * 128)
                            nc.tensor.matmul(
                                pb[bk][:, u * 128:(u + 1) * 128], qbar[:, hs],
                                S_b[:, hs], start=True, stop=False)
                            mm = nc.tensor.matmul(
                                pb[bk][:, u * 128:(u + 1) * 128], Ptmat[:, hs],
                                U_all[:, hs], start=False, stop=True)
                        sT.inc(mm, 1)
                        nc.scalar.wait_ge(sT.h, sT.v)
                        SI(nc.scalar.copy(obuf[:, qq * 512:(qq + 1) * 512],
                                          pb[bk][:, 0:512]), 1)
                        bank_free[bk] = (sS, sS.v)
                for qq in range(4):
                    bk = next_bank(1)
                    bank_wait(bk)
                    nc.tensor.wait_ge(sV.h, uev[qq])
                    for u in range(4):
                        hh = qq * 4 + u
                        hs = slice(hh * 128, (hh + 1) * 128)
                        mm = nc.tensor.matmul(
                            pb[bk][:, u * 128:(u + 1) * 128], khat[:, hs],
                            U_all[:, hs], start=True, stop=True)
                    sT.inc(mm, 1)
                    nc.vector.wait_ge(sT.h, sT.v)
                    S_f3 = S_f[:, qq * 512:(qq + 1) * 512] \
                        .rearrange("p (q t) -> p q t", q=4)
                    VI(nc.vector.tensor_tensor(
                        S_f3, S_f3,
                        Gbe[:, qq * 4:(qq + 1) * 4].unsqueeze(2)
                        .broadcast_to((128, 4, 128)), ALU.mult), 1)
                    VI(nc.vector.tensor_tensor(
                        S_f[:, qq * 512:(qq + 1) * 512],
                        S_f[:, qq * 512:(qq + 1) * 512],
                        pb[bk][:, 0:512], ALU.add), 1)
                    VI(nc.vector.tensor_copy(
                        S_b[:, qq * 512:(qq + 1) * 512],
                        S_f[:, qq * 512:(qq + 1) * 512]), 1)
                    bank_free[bk] = (sV, sV.v)

            _sid2a, _ = nc.enter_named_scope("p2a", False)
            for c in range(CH):
                soft_sync()
                chunk_pre(c)
                nc.scalar.wait_ge(sV.h, sV.v)
                qP.start((op_sp[c, 0].transpose([1, 0, 2]), Y3[:, :, 0:128]))
                qP.start((op_sp[c, 1].transpose([1, 0, 2]), Y3[:, :, 128:256]))
                qP.start((op_sp[c, 2].transpose([1, 0, 2]),
                          w2tn[:].rearrange("p (h t) -> p h t", h=H)))
                qP.start((op_sp[c, 3].transpose([1, 0, 2]),
                          khat[:].rearrange("p (h t) -> p h t", h=H)))
                qP.start((op_sp[c, 4].transpose([1, 0, 2]),
                          Ptmat[:].rearrange("p (h t) -> p h t", h=H)))
                qP.start((op_sp[c, 5].transpose([1, 0, 2]),
                          qbar[:].rearrange("p (h t) -> p h t", h=H)))
                seq_step(c, with_o=False)
            nc.leave_named_scope("p2a", _sid2a, False)
            _sid2g, _ = nc.enter_named_scope("p2ag", False)
            soft_sync()
            qX.start((ag_s_in, S_f[:]))
            qX.wait_all(nc.gpsimd)
            sC.inc(nc.gpsimd.collective_compute(
                "AllGather", ALU.bypass, replica_groups=[list(range(NCORE))],
                ins=[ag_s_in.opt()], outs=[ag_s_out.opt()]), 1)
            nc.sync.wait_ge(sC.h, sC.v)
            nc.vector.wait_ge(sC.h, sC.v)
            qX.wait_all(nc.vector)
            VI(nc.vector.memset(S_f[:], 0.0), 1)
            for j in range(4):
                qX.start((rowb[:], ag_s_out[2 * j]))
                qX.wait_all(nc.vector)
                VI(nc.vector.scalar_tensor_tensor(
                    S_f[:], rowb[:], sel4[:, j:j + 1], S_f[:],
                    ALU.mult, ALU.add), 1)
                nc.sync.wait_ge(sV.h, sV.v)
            VI(nc.vector.tensor_copy(S_b[:], S_f[:]), 1)

            nc.leave_named_scope("p2ag", _sid2g, False)
            _sid2b, _ = nc.enter_named_scope("p2b", False)
            for c in range(CH):
                soft_sync()
                qX.start((Y3[:, :, 0:128], op_sp[c, 0].transpose([1, 0, 2])))
                qX.start((Y3[:, :, 128:256], op_sp[c, 1].transpose([1, 0, 2])))
                qX.start((w2tn[:], op_sp[c, 2].transpose([1, 0, 2])))
                qX.start((khat[:], op_sp[c, 3].transpose([1, 0, 2])))
                qX.start((Ptmat[:], op_sp[c, 4].transpose([1, 0, 2])))
                qX.start((qbar[:], op_sp[c, 5].transpose([1, 0, 2])))
                qX.start((zc[:], z_sp[c]))
                qX.start((rowb[:],
                          gcum_d[0:16, c * 128:(c + 1) * 128]
                          .unsqueeze(0).broadcast_to((128, 16, 128))))
                qX.wait_all(nc.scalar)
                SI(nc.scalar.activation(
                    Gbe[:].unsqueeze(2), rowb3[:, :, 127:128], AF.Exp), 1)
                qX.wait_all(nc.tensor)
                qX.wait_all(nc.vector)
                seq_step(c, with_o=True)
                ob3 = obuf[:].rearrange("p (h t) -> p h t", h=H)
                on3 = onb[:].rearrange("p (h t) -> p h t", h=H)
                nc.vector.wait_ge(sS.h, sS.v)
                VI(nc.vector.tensor_tensor(obuf[:], obuf[:], zc[:], ALU.mult), 1)
                VI(nc.vector.tensor_tensor(art[:], obuf[:], obuf[:], ALU.mult), 1)
                VI(nc.vector.tensor_reduce(
                    sq16[:], art[:].rearrange("p (h t) -> p h t", h=H),
                    AX.X, ALU.add), 1)
                VI(nc.vector.tensor_scalar_mul(sq16[:], sq16[:], 1.0 / 128.0), 1)
                nc.scalar.wait_ge(sV.h, sV.v)
                SI(nc.scalar.activation(sq16[:], sq16[:], AF.Sqrt,
                                        bias=ep6[:]), 1)
                nc.vector.wait_ge(sS.h, sS.v)
                VI(nc.vector.reciprocal(sq16[:], sq16[:]), 1)
                VI(nc.vector.tensor_tensor(on3, ob3, bc16(sq16[:]), ALU.mult), 1)
                VI(nc.vector.tensor_tensor(
                    on3, on3, nwb[:].unsqueeze(1).broadcast_to((128, 16, 128)),
                    ALU.mult), 1)
                nc.tensor.wait_ge(sV.h, sV.v)
                for hg in range(4):
                    bk = next_bank(0)
                    bank_wait(bk)
                    for j in range(4):
                        hh = hg * 4 + j
                        mm = nc.tensor.matmul(
                            pb[bk][:, j * 128:(j + 1) * 128],
                            onb[:, hh * 128:(hh + 1) * 128],
                            I_bf[:], start=True, stop=True)
                    sT.inc(mm, 1)
                    nc.vector.wait_ge(sT.h, sT.v)
                    qX.wait_all(nc.vector)
                    VI(nc.vector.tensor_copy(stg2[:], pb[bk][:, :]), 1)
                    bank_free[bk] = (sV, sV.v)
                    nc.sync.wait_ge(sV.h, sV.v)
                    qX.start((ocm_sp[hg * 4:(hg + 1) * 4, :, c * 128:(c + 1) * 128]
                              .transpose([1, 0, 2]), stg2[:]))
            nc.leave_named_scope("p2b", _sid2b, False)
            drain_dmas()


        def phase3():
          with contextlib.ExitStack() as ps:
            def pa(n, s, d):
                return ps.enter_context(nc.sbuf_tensor(n, s, d))
            wo = pa("wo", [128, 16 * 2048], BF16)
            oc_l = pa("oc_l", [128, 2048], BF16)
            xres = pa("xres", [128, 2048], F32)
            x2t = pa("x2t", [128, 2048], F32)
            for kk in range(NKD):
                qW.start((wo[:, kk * 2048:(kk + 1) * 2048], wout_d[kk]))
            qW.wait_all(nc.tensor)
            for tt in range(16):
                qX.wait_all(nc.sync)
                tko = qX.start((oc_l[:], ocm_sp[:, :, tt * 128:(tt + 1) * 128]
                                .transpose([1, 0, 2])))
                txr = qX.start((xres[:], x_d[3 + tt * 128:3 + (tt + 1) * 128, :]))
                qX.wait(nc.tensor, tko)
                qX.wait(nc.vector, txr)
                for nb in range(4):
                    bk = next_bank(0)
                    bank_wait(bk)
                    for kk in range(NKD):
                        mm = nc.tensor.matmul(
                            pb[bk][:, 0:512],
                            oc_l[:, kk * 128:(kk + 1) * 128],
                            wo[:, kk * 2048 + nb * 512:kk * 2048 + nb * 512 + 512],
                            start=(kk == 0), stop=(kk == NKD - 1))
                    sT.inc(mm, 1)
                    nc.vector.wait_ge(sT.h, sT.v)
                    cs = slice(nb * 512, (nb + 1) * 512)
                    VI(nc.vector.tensor_tensor(
                        x2t[:, cs], pb[bk][:, 0:512], gab[:, cs], ALU.mult), 1)
                    bank_free[bk] = (sV, sV.v)
                    VI(nc.vector.tensor_tensor(
                        x2t[:, cs], x2t[:, cs], xres[:, cs], ALU.add), 1)
                nc.sync.wait_ge(sV.h, sV.v)
                qX.start((x2_sp[tt], x2t[:]))
            drain_dmas()

        def phase4():
          with contextlib.ExitStack() as ps:
            def pa(n, s, d):
                return ps.enter_context(nc.sbuf_tensor(n, s, d))
            ep5 = pa("ep5", [128, 1], F32)
            GI(nc.gpsimd.memset(ep5[:], 1e-5), 1)
            nc.vector.wait_ge(sG.h, sG.v)
            nc.scalar.wait_ge(sG.h, sG.v)
            N1 = float(1.0 / np.sqrt(8192.0))
            rs2_all = pa("rs2_all", [128, 16], F32)
            x1_es = contextlib.ExitStack()
            xq1T = x1_es.enter_context(
                nc.sbuf_tensor("xq1T", [128, NKD * 2048], BF16))
            xq1T3 = xq1T[:].rearrange("p (k c) -> p k c", k=NKD)

            _sc2, _ = nc.enter_named_scope("c2", False)
            c2_es = contextlib.ExitStack()
            def ca(n, s, d):
                return c2_es.enter_context(nc.sbuf_tensor(n, s, d))
            x2t = ca("x2l", [128, 2 * 2048], F32)
            sqt = ca("sqt4", [128, 2048], F32)
            stat = ca("stat4", [128, 8], F32)
            h2a = ca("h2a", [128, 2048], F32)
            h2b = ca("h2b", [128, 2 * 2048], BF16)
            i32t = ca("i32t", [128, 2048], I32)
            tr_done4 = {}
            ln_v4 = {}
            rs1tok = {}
            for tt in range(16):
                xbuf = x2t[:, (tt % 2) * 2048:(tt % 2) * 2048 + 2048]
                if tt >= 2:
                    nc.sync.wait_ge(sV.h, ln_v4[tt - 2])
                tx = qX.start((xbuf, x2_sp[tt]))
                qX.wait(nc.vector, tx)
                qX.wait(nc.scalar, tx)
                VI(nc.vector.tensor_reduce(stat[:, 0:1], xbuf, AX.X, ALU.add), 1)
                SI(nc.scalar.activation(sqt[:], xbuf, AF.Square,
                                        accum_out=stat[:, 2:3]), 1)
                nc.vector.wait_ge(sS.h, sS.v)
                VI(nc.vector.tensor_scalar_mul(stat[:, 1:2], stat[:, 0:1],
                                               1.0 / 2048.0), 1)
                VI(nc.vector.tensor_scalar_mul(stat[:, 3:4], stat[:, 2:3],
                                               1.0 / 2048.0), 1)
                VI(nc.vector.tensor_tensor(stat[:, 4:5], stat[:, 1:2],
                                           stat[:, 1:2], ALU.mult), 1)
                VI(nc.vector.tensor_tensor(stat[:, 3:4], stat[:, 3:4],
                                           stat[:, 4:5], ALU.subtract), 1)
                nc.scalar.wait_ge(sV.h, sV.v)
                SI(nc.scalar.activation(stat[:, 5:6], stat[:, 3:4], AF.Sqrt,
                                        bias=ep5[:]), 1)
                nc.vector.wait_ge(sS.h, sS.v)
                VI(nc.vector.reciprocal(stat[:, 6:7], stat[:, 5:6]), 1)
                VI(nc.vector.tensor_scalar(h2a[:], xbuf, stat[:, 1:2],
                                           stat[:, 6:7], ALU.subtract, ALU.mult), 1)
                VI(nc.vector.tensor_tensor(h2a[:], h2a[:], cb1b[:], ALU.mult), 1)
                VI(nc.vector.tensor_tensor(h2a[:], h2a[:], sbb[:], ALU.add), 1)
                VI(nc.vector.tensor_reduce(stat[:, 0:1], h2a[:], AX.X, ALU.max,
                                           apply_absolute_value=True), 1)
                VI(nc.vector.tensor_scalar(stat[:, 0:1], stat[:, 0:1], 1e-5,
                                           None, ALU.max), 1)
                VI(nc.vector.reciprocal(stat[:, 1:2], stat[:, 0:1]), 1)
                VI(nc.vector.tensor_scalar_mul(stat[:, 2:3], stat[:, 1:2],
                                               128.0), 1)
                VI(nc.vector.tensor_scalar_mul(stat[:, 3:4], stat[:, 0:1],
                                               1.0 / 128.0), 1)
                VI(nc.vector.tensor_copy(sc1_r[:, tt:tt + 1], stat[:, 3:4]), 1)
                hb = h2b[:, (tt % 2) * 2048:(tt % 2) * 2048 + 2048]
                if tt - 2 in tr_done4:
                    nc.vector.wait_ge(sT.h, tr_done4[tt - 2])
                VI(nc.vector.tensor_scalar(h2a[:], h2a[:], stat[:, 2:3],
                                           CLIP_HI, ALU.mult, ALU.min), 1)
                VI(nc.vector.tensor_scalar(h2a[:], h2a[:], CLIP_LO, None,
                                           ALU.max), 1)
                VI(nc.vector.tensor_scalar(i32t[:], h2a[:], MAGIC, MAGIC,
                                           ALU.add, ALU.subtract), 1)
                VI(nc.vector.tensor_copy(hb, i32t[:]), 1)
                nc.tensor.wait_ge(sV.h, sV.v)
                for g4 in range(4):
                    bk = next_bank(0)
                    bank_wait(bk)
                    for j in range(4):
                        kk = g4 * 4 + j
                        mm = nc.tensor.matmul(
                            pb[bk][:, j * 128:(j + 1) * 128],
                            hb[:, kk * 128:(kk + 1) * 128],
                            I_bf[:], start=True, stop=True)
                    sT.inc(mm, 1)
                    nc.scalar.wait_ge(sT.h, sT.v)
                    SI(nc.scalar.copy(
                        xq1T3[:, g4 * 4:(g4 + 1) * 4, tt * 128:(tt + 1) * 128],
                        pb[bk][:, :].rearrange("p (j c) -> p j c", j=4)), 1)
                    bank_free[bk] = (sS, sS.v)
                tr_done4[tt] = sT.v
                ln_v4[tt] = sV.v
            drain_dmas()
            c2_es.close()

            nc.leave_named_scope("c2", _sc2, False)
            _sc3, _ = nc.enter_named_scope("c3", False)
            c3_es = contextlib.ExitStack()
            def c3a(n, s, d):
                return c3_es.enter_context(nc.sbuf_tensor(n, s, d))
            wgt = c3a("wgt", [128, 2 * 2048], BF16)
            wut = c3a("wut", [128, 2 * 2048], BF16)
            rsbg = c3a("rsbg", [128, 2048], F32)
            rsbu = c3a("rsbu", [128, 2048], F32)
            gsil = c3a("gsil", [128, 2048], BF16)
            upt = c3a("upt", [128, 2048], BF16)
            intb = c3a("intb", [128, 2048], BF16)
            rs1b = c3a("rs1b", [128, 16], BF16)
            rrow16 = c3a("rrow16", [16, 128], BF16)
            VI(nc.vector.tensor_copy(rs1b[:], sc1_r[:]), 1)
            nc.tensor.wait_ge(sV.h, sV.v)
            bk = next_bank(0)
            bank_wait(bk)
            mm = nc.tensor.matmul(pb[bk][0:16, 0:128], rs1b[:], I_bf[:],
                                  start=True, stop=True)
            sT.inc(mm, 1)
            nc.vector.wait_ge(sT.h, sT.v)
            VI(nc.vector.tensor_copy(rrow16[:], pb[bk][0:16, 0:128]), 1)
            bank_free[bk] = (sV, sV.v)
            nc.tensor.wait_ge(sV.h, sV.v)
            for tt in range(16):
                bk = next_bank(0)
                bank_wait(bk)
                mm = nc.tensor.matmul(
                    pb[bk][:, 0:128], sel16b[:, tt * 128:(tt + 1) * 128],
                    rrow16[:], start=True, stop=True)
                sT.inc(mm, 1)
                nc.vector.wait_ge(sT.h, sT.v)
                VI(nc.vector.tensor_copy(rsbg[:, tt * 128:(tt + 1) * 128],
                                         pb[bk][:, 0:128]), 1)
                bank_free[bk] = (sV, sV.v)
            VI(nc.vector.tensor_scalar(rsbu[:], rsbg[:], sbars[:, 1:2],
                                       None, ALU.mult), 1)
            VI(nc.vector.tensor_scalar(rsbg[:], rsbg[:], sbars[:, 0:1],
                                       None, ALU.mult), 1)
            c3wt = [None, None]
            c3mm = {}
            sp_tok = [None]

            def c3_loads(m):
                toks = []
                for wi, soff in ((0, 0), (1, SLAB)):
                    buf = wgt if wi == 0 else wut
                    dst = buf[:, (m % 2) * 2048:(m % 2) * 2048 + 2048]
                    dst4 = dst.rearrange("p (r kl c) -> p r kl c", r=8, kl=2)
                    for kl in range(2):
                        src = ag_w_out[:, soff + m * 32768 + kl * 16384:
                                       soff + m * 32768 + (kl + 1) * 16384]
                        toks.append(qX.start(
                            (dst4[:, :, kl, :],
                             src.rearrange("r (pp c) -> pp r c", pp=128))))
                c3wt[m % 2] = toks

            c3_loads(0)
            for m in range(64):
                if m + 1 < 64:
                    if m >= 1:
                        nc.sync.wait_ge(sT.h, c3mm[m - 1])
                    c3_loads(m + 1)
                for tok in c3wt[m % 2]:
                    qX.wait(nc.tensor, tok)
                bkv = []
                for wi in range(2):
                    buf = wgt if wi == 0 else wut
                    for nb in range(4):
                        bk = wi * 4 + nb
                        bank_wait(bk)
                        for kk in range(NKD):
                            mm = nc.tensor.matmul(
                                pb[bk][:, 0:512],
                                buf[:, (m % 2) * 2048 + kk * 128:
                                    (m % 2) * 2048 + (kk + 1) * 128],
                                xq1T3[:, kk, nb * 512:(nb + 1) * 512],
                                start=(kk == 0), stop=(kk == NKD - 1))
                        sT.inc(mm, 1)
                        bkv.append(sT.v)
                c3mm[m] = sT.v
                for nb in range(4):
                    cs = slice(nb * 512, (nb + 1) * 512)
                    nc.vector.wait_ge(sT.h, bkv[nb])
                    VI(nc.vector.tensor_tensor(
                        gsil[:, cs], pb[nb][:, 0:512], rsbg[:, cs], ALU.mult), 1)
                    bank_free[nb] = (sV, sV.v)
                    nc.vector.wait_ge(sT.h, bkv[4 + nb])
                    VI(nc.vector.tensor_tensor(
                        upt[:, cs], pb[4 + nb][:, 0:512], rsbu[:, cs],
                        ALU.mult), 1)
                    bank_free[4 + nb] = (sV, sV.v)
                nc.scalar.wait_ge(sV.h, sV.v)
                SI(nc.scalar.activation(gsil[:], gsil[:], AF.Silu), 1)
                nc.vector.wait_ge(sS.h, sS.v)
                if sp_tok[0] is not None:
                    qX.wait(nc.vector, sp_tok[0])
                VI(nc.vector.tensor_tensor(intb[:], gsil[:], upt[:], ALU.mult), 1)
                nc.sync.wait_ge(sV.h, sV.v)
                sp_tok[0] = qX.start((inter_sp[:, m].transpose([1, 0, 2]),
                                      intb[:]))
            drain_dmas()
            c3_es.close()
            x1_es.close()

            nc.leave_named_scope("c3", _sc3, False)
            _sc4a, _ = nc.enter_named_scope("c4a", False)
            c4_es = contextlib.ExitStack()
            def c4a(n, s, d):
                return c4_es.enter_context(nc.sbuf_tensor(n, s, d))
            iz = c4a("iz", [128, 64 * 256], BF16)
            z1 = c4a("z1", [128, 64 * 256], BF16)
            fw = iz
            s2a = c4a("s2a", [128, 2 * 512], BF16)
            s2b = c4a("s2b", [128, 2 * 512], BF16)
            mxj = c4a("mxj", [128, 256], F32)
            mxjb = c4a("mxjb", [128, 256], BF16)
            mcol = c4a("mcol", [128, 2], F32)
            mcolb = c4a("mcolb", [128, 2], BF16)
            mrow = c4a("mrow", [16, 128], BF16)
            s2r = c4a("s2r", [128, 256], F32)
            xq2b = z1
            i32f = c4a("i32f", [128, 16 * 256], I32)
            fw32 = c4a("fw32", [128, 16 * 256], F32)
            VI(nc.vector.memset(mrow[:], 0.0), 1)
            for blk in range(8):
                qX.wait_all(nc.sync)
                ti = qX.start((iz[:], inter_sp[blk].transpose([1, 0, 2])))
                qX.wait(nc.tensor, ti)
                iz3 = iz[:].rearrange("p (i t) -> p i t", i=64)
                z13 = z1[:].rearrange("p (i t) -> p i t", i=64)
                fw3 = fw[:].rearrange("p (i t) -> p i t", i=64)
                z1g = z1[:].rearrange("p (g i t) -> p g i t", g=128, i=64)
                for ig in range(32):
                    bk = next_bank(0)
                    bank_wait(bk)
                    for j in range(2):
                        mm = nc.tensor.matmul(
                            pb[bk][:, j * 256:(j + 1) * 256],
                            H128[:], iz3[:, ig * 2 + j, :],
                            start=True, stop=True)
                    sT.inc(mm, 1)
                    nc.scalar.wait_ge(sT.h, sT.v)
                    SI(nc.scalar.copy(
                        z1g[:, :, ig * 2:ig * 2 + 2, :].transpose([0, 2, 1, 3]),
                        pb[bk][:, :].rearrange("p (i g t) -> p i g t",
                                               i=2, g=128)), 1)
                    bank_free[bk] = (sS, sS.v)
                nc.tensor.wait_ge(sS.h, sS.v)
                st1_ev = {}
                st2_ev = {}
                for step in range(34):
                    g1, g2, g3 = step, step - 1, step - 2
                    if g1 < 32:
                        sl = g1 % 2
                        bk = next_bank(0)
                        bank_wait(bk)
                        for j in range(4):
                            g = g1 * 4 + j
                            mm = nc.tensor.matmul(
                                pb[bk][:, j * 128:(j + 1) * 128],
                                z1[:, g * 128:(g + 1) * 128], I_bf[:],
                                start=True, stop=True)
                        sT.inc(mm, 1)
                        nc.scalar.wait_ge(sT.h, sT.v)
                        SI(nc.scalar.copy(s2a[:, sl * 512:(sl + 1) * 512],
                                          pb[bk][:, :]), 1)
                        st1_ev[g1] = sS.v
                        bank_free[bk] = (sS, sS.v)
                    if 0 <= g2 < 32:
                        sl = g2 % 2
                        bk2 = next_bank(1)
                        bank_wait(bk2)
                        nc.tensor.wait_ge(sS.h, st1_ev[g2])
                        for j in range(4):
                            mm = nc.tensor.matmul(
                                pb[bk2][:, j * 128:(j + 1) * 128], HXs[:],
                                s2a[:, sl * 512 + j * 128:
                                    sl * 512 + (j + 1) * 128],
                                start=True, stop=True)
                        sT.inc(mm, 1)
                        nc.scalar.wait_ge(sT.h, sT.v)
                        SI(nc.scalar.copy(s2b[:, sl * 512:(sl + 1) * 512],
                                          pb[bk2][:, :]), 1)
                        st2_ev[g2] = sS.v
                        bank_free[bk2] = (sS, sS.v)
                    if 0 <= g3:
                        sl = g3 % 2
                        bk3 = next_bank(0)
                        bank_wait(bk3)
                        nc.tensor.wait_ge(sS.h, st2_ev[g3])
                        for j in range(4):
                            mm = nc.tensor.matmul(
                                pb[bk3][:, j * 128:(j + 1) * 128],
                                s2b[:, sl * 512 + j * 128:
                                    sl * 512 + (j + 1) * 128], I_bf[:],
                                start=True, stop=True)
                        sT.inc(mm, 1)
                        nc.vector.wait_ge(sT.h, sT.v)
                        for j in range(4):
                            g = g3 * 4 + j
                            VI(nc.vector.tensor_copy(
                                fw3[:, :, 2 * g:2 * g + 2],
                                pb[bk3][:, j * 128:(j + 1) * 128]
                                .rearrange("p (i t) -> p i t", i=64)), 1)
                        bank_free[bk3] = (sV, sV.v)
                VI(nc.vector.tensor_reduce(
                    mxj[:], fw3.transpose([0, 2, 1]), AX.X, ALU.max,
                    apply_absolute_value=True), 1)
                VI(nc.vector.tensor_copy(mxjb[:], mxj[:]), 1)
                nc.tensor.wait_ge(sV.h, sV.v)
                for j in range(2):
                    bk = next_bank(0)
                    bank_wait(bk)
                    mm = nc.tensor.matmul(
                        pb[bk][:, 0:128], mxjb[:, j * 128:(j + 1) * 128],
                        I_bf[:], start=True, stop=True)
                    sT.inc(mm, 1)
                    nc.vector.wait_ge(sT.h, sT.v)
                    VI(nc.vector.tensor_reduce(
                        mcol[:, j:j + 1], pb[bk][:, 0:128], AX.X, ALU.max), 1)
                    bank_free[bk] = (sV, sV.v)
                VI(nc.vector.tensor_scalar(mcol[:], mcol[:], N1, 1e-5,
                                           ALU.mult, ALU.max), 1)
                VI(nc.vector.reciprocal(mcol[:], mcol[:]), 1)
                VI(nc.vector.tensor_scalar_mul(mcol[:], mcol[:], 128.0 * N1), 1)
                VI(nc.vector.tensor_copy(mcolb[:], mcol[:]), 1)
                VI(nc.vector.tensor_copy(mcol[:], mcolb[:]), 1)
                VI(nc.vector.reciprocal(s2r[:, 0:2], mcol[:]), 1)
                VI(nc.vector.tensor_scalar(s2r[:, 0:2], s2r[:, 0:2],
                                           sbars[:, 2:3], N1,
                                           ALU.mult, ALU.mult), 1)
                VI(nc.vector.tensor_copy(rs2_all[:, blk * 2:blk * 2 + 2],
                                         s2r[:, 0:2]), 1)
                nc.tensor.wait_ge(sV.h, sV.v)
                bk = next_bank(0)
                bank_wait(bk)
                mm = nc.tensor.matmul(pb[bk][0:2, 0:128], mcolb[:], I_bf[:],
                                      start=True, stop=True)
                sT.inc(mm, 1)
                nc.vector.wait_ge(sT.h, sT.v)
                VI(nc.vector.tensor_copy(mrow[0:2, :], pb[bk][0:2, 0:128]), 1)
                bank_free[bk] = (sV, sV.v)
                nc.tensor.wait_ge(sV.h, sV.v)
                bk = next_bank(0)
                bank_wait(bk)
                nc.tensor.matmul(pb[bk][:, 0:128],
                                 sel16b[:, 0:128], mrow[:, 0:128],
                                 start=True, stop=True)
                mm = nc.tensor.matmul(pb[bk][:, 128:256],
                                      sel16b[:, 128:256], mrow[:, 0:128],
                                      start=True, stop=True)
                sT.inc(mm, 1)
                nc.vector.wait_ge(sT.h, sT.v)
                VI(nc.vector.tensor_copy(s2r[:], pb[bk][:, 0:256]), 1)
                bank_free[bk] = (sV, sV.v)
                for qc_ in range(4):
                    i0 = qc_ * 16
                    VI(nc.vector.tensor_tensor(
                        fw32[:].rearrange("p (i t) -> p i t", i=16),
                        fw3[:, i0:i0 + 16, :],
                        s2r[:].unsqueeze(1).broadcast_to((128, 16, 256)),
                        ALU.mult), 1)
                    VI(nc.vector.tensor_scalar(fw32[:], fw32[:], CLIP_LO,
                                               CLIP_HI, ALU.max, ALU.min), 1)
                    VI(nc.vector.tensor_scalar(i32f[:], fw32[:], MAGIC, MAGIC,
                                               ALU.add, ALU.subtract), 1)
                    VI(nc.vector.tensor_copy(
                        xq2b[:, i0 * 256:(i0 + 16) * 256], i32f[:]), 1)
                nc.sync.wait_ge(sV.h, sV.v)
                for sb_ in range(2):
                    for iq in range(4):
                        qX.start((xq2_sp[2 * blk + sb_, iq * 16:(iq + 1) * 16]
                                  .transpose([1, 0, 2]),
                                  xq2b[:].rearrange("p (i t) -> p i t", i=64)
                                  [:, iq * 16:(iq + 1) * 16,
                                   sb_ * 128:(sb_ + 1) * 128]))
            drain_dmas()
            c4_es.close()

            nc.leave_named_scope("c4a", _sc4a, False)
            _sc4b, _ = nc.enter_named_scope("c4b", False)
            c5_es = contextlib.ExitStack()
            def c5a(n, s, d):
                return c5_es.enter_context(nc.sbuf_tensor(n, s, d))
            wdn = c5a("wdn", [128, 64 * 512], BF16)
            xqm = c5a("xqm", [128, 2 * 64 * 128], BF16)
            x2s = c5a("x2s", [128, 2 * 512], F32)
            yt = c5a("yt", [128, 2 * 512], F32)
            xq_tok = [None, None]
            x2_tok = [None, None]
            st_tok = [None, None]
            mt_mm = {}
            mt_vv = {}

            def c4b_loads(mt, nb):
                sl = mt % 2
                if mt >= 2:
                    nc.sync.wait_ge(sT.h, mt_mm[mt - 2])
                    nc.sync.wait_ge(sV.h, mt_vv[mt - 2])
                xqv3 = xqm[:, sl * 8192:sl * 8192 + 8192] \
                    .rearrange("p (i t) -> p i t", i=64)
                xq_tok[sl] = [qX.start(
                    (xqv3[:, iq * 16:(iq + 1) * 16, :],
                     xq2_sp[mt, iq * 16:(iq + 1) * 16].transpose([1, 0, 2])))
                    for iq in range(4)]
                x2_tok[sl] = qX.start(
                    (x2s[:, sl * 512:(sl + 1) * 512],
                     x2_sp[mt][:, nb * 512:(nb + 1) * 512]))

            for nb in range(4):
                nc.sync.wait_ge(sT.h, sT.v)
                wdt = []
                wdn4 = wdn[:].rearrange("p (k m c) -> p k m c", k=64, m=4)
                for r in range(8):
                    for mloc in range(4):
                        off = 2 * SLAB + (4 * nb + mloc) * 8 * 16384
                        srcw = ag_w_out[r, off:off + 8 * 16384]
                        wdt.append(qX.start(
                            (wdn4[:, r * 8:(r + 1) * 8, mloc, :],
                             srcw.rearrange("(kl pp c) -> pp kl c",
                                            kl=8, pp=128))))
                mt_mm.clear()
                mt_vv.clear()
                c4b_loads(0, nb)
                for t_ in wdt:
                    qX.wait(nc.tensor, t_)
                for mt in range(16):
                    sl = mt % 2
                    if mt + 1 < 16:
                        c4b_loads(mt + 1, nb)
                    for t_ in xq_tok[sl]:
                        qX.wait(nc.tensor, t_)
                    bk = next_bank(0)
                    bank_wait(bk)
                    for kk in range(64):
                        mm = nc.tensor.matmul(
                            pb[bk][:, 0:512],
                            xqm[:, sl * 8192 + kk * 128:
                                sl * 8192 + (kk + 1) * 128],
                            wdn[:, kk * 512:(kk + 1) * 512],
                            start=(kk == 0), stop=(kk == 63))
                    sT.inc(mm, 1)
                    mt_mm[mt] = sT.v
                    qX.wait(nc.vector, x2_tok[sl])
                    if st_tok[sl] is not None:
                        qX.wait(nc.vector, st_tok[sl])
                    nc.vector.wait_ge(sT.h, sT.v)
                    ys = yt[:, sl * 512:(sl + 1) * 512]
                    VI(nc.vector.tensor_scalar(
                        ys, pb[bk][:, 0:512], rs2_all[:, mt:mt + 1],
                        None, ALU.mult), 1)
                    bank_free[bk] = (sV, sV.v)
                    VI(nc.vector.tensor_tensor(
                        ys, ys, gbb[:, nb * 512:(nb + 1) * 512],
                        ALU.mult), 1)
                    VI(nc.vector.tensor_tensor(
                        ys, ys, x2s[:, sl * 512:(sl + 1) * 512], ALU.add), 1)
                    mt_vv[mt] = sV.v
                    nc.sync.wait_ge(sV.h, sV.v)
                    st_tok[sl] = qX.start(
                        (y_d[mt * 128:(mt + 1) * 128,
                             nb * 512:(nb + 1) * 512], ys))
            nc.leave_named_scope("c4b", _sc4b, False)
            drain_dmas()
            c5_es.close()

        def _scoped(name, fn):
            sid, _ = nc.enter_named_scope(name, False)
            fn()
            nc.leave_named_scope(name, sid, False)
        if 0 in phases:
            _scoped("p0", phase0)
        if 1 in phases:
            _scoped("p1", phase1)
        if 2 in phases:
            _scoped("p2", phase2)
        if 3 in phases:
            _scoped("p3", phase3)
        if 4 in phases:
            _scoped("p4", phase4)
        nc.all_engine_barrier()
    return nc


def kernel(**inputs):
    nc = build()
    maps = _host_inputs(inputs)
    res = run_bass_kernel_spmd(nc, maps, core_ids=list(range(NCORE)))
    out = np.empty((4, 4096, 2048), dtype=np.float32)
    for c in range(NCORE):
        b, s = c // 2, c % 2
        out[b, s * TOK:(s + 1) * TOK] = res.results[c]["y"]
    return out
